# version 04
# speedup vs baseline: 1.6484x; 1.0014x over previous
import numpy as np
import concourse.bass as bass
import concourse.mybir as mybir
from concourse.bass_utils import run_bass_kernel_spmd
from contextlib import ExitStack

F32 = mybir.dt.float32
BF16 = mybir.dt.bfloat16
AF = mybir.ActivationFunctionType
ALU = mybir.AluOpType

B, T, C = 4, 4096, 2048
NH, HS, NL = 16, 128, 512
NBLK = T // 128
MYT = 2048
EPS = 1e-5
ISQ = float(1.0 / np.sqrt(128.0))


def sslot(j):
    return (j % 2) * 16 + j // 2


class Ctx:
    def __init__(self, nc, stack, names):
        self.nc = nc
        self.cnt = {}
        for nm in names:
            s = stack.enter_context(nc.semaphore(nm))
            self.cnt[nm] = [s, 0]

    def inc(self, instr, name, n=1):
        s, v = self.cnt[name]
        instr.then_inc(s, n)
        self.cnt[name][1] = v + n
        return self.cnt[name][1]

    def val(self, name):
        return self.cnt[name][1]

    def wait(self, eng, name, val=None):
        s, v = self.cnt[name]
        eng.wait_ge(s, v if val is None else val)

    def barrier(self, engines):
        for eng in engines:
            for nm in self.cnt:
                s, v = self.cnt[nm]
                if v > 0:
                    eng.wait_ge(s, v)


def _col(ap):
    return ap


def feat_tile(dram_ap, c0, cn, t0, tn):
    return dram_ap[c0:c0 + cn, t0:t0 + tn].rearrange("(cc p) t -> p cc t", p=128)


def sb3(sb_ap, cc, t):
    return sb_ap.rearrange("p (cc t) -> p cc t", cc=cc)


def emit_ln_combine(nc, cx, qw, n_cc, get_y, get_x_dma, get_out_dma, bufs,
                    residual=True, out_lnx=None, y_is_bf16=True):
    raise NotImplementedError


def build_la0(debug=False, stop_after=None):
    nc = bass.Bass("TRN2", target_bir_lowering=False)
    xin = nc.dram_tensor("xfull", [C, T], F32, kind="ExternalInput").ap()
    xmy = nc.dram_tensor("xmy", [C, MYT], F32, kind="ExternalInput").ap()
    wq_t = nc.dram_tensor("wq_t", [C, C], F32, kind="ExternalInput").ap()
    wq_n = nc.dram_tensor("wq_n", [C, C], F32, kind="ExternalInput").ap()
    wdkv_t = nc.dram_tensor("wdkv_t", [C, NL], F32, kind="ExternalInput").ap()
    wuk = nc.dram_tensor("wuk", [C, NL], F32, kind="ExternalInput").ap()
    wuv = nc.dram_tensor("wuv", [C, NL], F32, kind="ExternalInput").ap()
    wo_t = nc.dram_tensor("wo_t", [C, C], F32, kind="ExternalInput").ap()
    masks = nc.dram_tensor("masks", [4 * 128, 512], F32, kind="ExternalInput").ap()
    ident = nc.dram_tensor("ident", [128, 128], F32, kind="ExternalInput").ap()
    xout = nc.dram_tensor("xout", [C, MYT], F32, kind="ExternalOutput").ap()
    kind = dict(kind="ExternalOutput") if debug else {}
    xdum = nc.dram_tensor("xdum", [C, MYT], F32)
    lnx_d = nc.dram_tensor("lnx_d", [C, T], BF16, **kind)
    lnmy_d = nc.dram_tensor("lnmy_d", [C, MYT], BF16, **kind)
    qlat_d = nc.dram_tensor("qlat_d", [128, 8 * 4 * NH * 256], BF16, **kind)
    y_d = nc.dram_tensor("y_d", [C, MYT], BF16, **kind)
    if debug:
        lnx_d, lnmy_d, qlat_d, y_d = lnx_d.ap(), lnmy_d.ap(), qlat_d.ap(), y_d.ap()
    else:
        lnx_d, lnmy_d, qlat_d, y_d = lnx_d.ap(), lnmy_d.ap(), qlat_d.ap(), y_d.ap()

    st = ExitStack()
    with st:
        cx = Ctx(nc, st, ("sw", "sg", "st", "sv", "sa", "ss"))
        sbb = lambda name, shape, dt=BF16: st.enter_context(nc.sbuf_tensor(name, shape, dt))

        ones_col = sbb("ones_col", [128, 1])
        ones_row = st.enter_context(nc.sbuf_tensor("ones_row", [1, 128], F32))
        eps_sb = st.enter_context(nc.sbuf_tensor("eps_sb", [1, 1], F32))
        mask_sb = sbb("mask_sb", [128, 4 * 512])
        id_sb = sbb("id_sb", [128, 128])
        veff = sbb("veff", [128, 4 * NH * 128])
        qh = sbb("qh", [128, 256])
        expb = sbb("expb", [128, 2 * 512])
        ylat_sb = sbb("ylat_sb", [128, 4 * 512])
        yqt = sbb("yqt", [128, NH * 256])
        acc = st.enter_context(nc.sbuf_tensor("acc", [128, 512], F32))
        ones_col_f = st.enter_context(nc.sbuf_tensor("ones_col_f", [128, 1], F32))
        sums = st.enter_context(nc.sbuf_tensor("sums", [1, 512], F32))
        rb = st.enter_context(nc.sbuf_tensor("rb", [128, 512], F32))
        mrb = st.enter_context(nc.sbuf_tensor("mrb", [128, 512], F32))
        rstd = st.enter_context(nc.sbuf_tensor("rstd", [1, 512], F32))
        mrs = st.enter_context(nc.sbuf_tensor("mrs", [1, 512], F32))
        scr = st.enter_context(nc.sbuf_tensor("scr", [1, 2048], F32))
        f32a = st.enter_context(nc.sbuf_tensor("f32a", [128, 512], F32))
        f32b = st.enter_context(nc.sbuf_tensor("f32b", [128, 512], F32))

        nc.gpsimd.memset(ones_col[:], 1.0)
        nc.gpsimd.memset(ones_col_f[:], 1.0)
        nc.gpsimd.memset(ones_row[:], 1.0)
        g = nc.gpsimd.memset(eps_sb[:], EPS)
        cx.inc(g, "sv")
        d = nc.gpsimd.dma_start(out=sb3(mask_sb[:], 4, 512), in_=masks.rearrange("(m p) j -> p m j", p=128))
        cx.inc(d, "sg", 16)
        d = nc.gpsimd.dma_start(out=id_sb[:], in_=ident)
        cx.inc(d, "sg", 16)

        def ln_small(jw):
            mu = scr[0:1, 0:jw]
            ex2 = scr[0:1, 512:512 + jw]
            tmp = scr[0:1, 1024:1024 + jw]
            nc.vector.tensor_tensor(out=tmp, in0=mu, in1=mu, op=ALU.mult)
            v = nc.vector.tensor_tensor(out=ex2, in0=ex2, in1=tmp, op=ALU.subtract)
            cx.inc(v, "sv")
            cx.wait(nc.scalar, "sv")
            a = nc.scalar.activation(out=ex2, in_=ex2, func=AF.Ln, bias=eps_sb[0:1, 0:1])
            a = nc.scalar.activation(out=rstd[0:1, 0:jw], in_=ex2, func=AF.Exp, scale=-0.5)
            cx.inc(a, "sa")
            cx.wait(nc.vector, "sa")
            v = nc.vector.tensor_tensor(out=mrs[0:1, 0:jw], in0=mu, in1=rstd[0:1, 0:jw], op=ALU.mult)
            return cx.inc(v, "sv")

        def bcast(vec_ap, dst_sb, jw, pb):
            cx.wait(nc.tensor, "sv")
            mm = nc.tensor.matmul(pb[:, 0:jw], ones_row[0:1, :], vec_ap, start=True, stop=True)
            tv = cx.inc(mm, "st")
            cx.wait(nc.vector, "st", tv)
            v = nc.vector.tensor_copy(out=dst_sb[:, 0:jw], in_=pb[:, 0:jw])
            return cx.inc(v, "sv")

        with ExitStack() as sA:
            keff = sA.enter_context(nc.sbuf_tensor("keff", [128, NH * NL], BF16))
            wbig = sA.enter_context(nc.sbuf_tensor("wbig", [128, 16 * 2048], BF16))
            with ExitStack() as sa:
                wsm = sa.enter_context(nc.sbuf_tensor("wsm", [128, 16 * 512], BF16))
                d = nc.gpsimd.dma_start(out=sb3(wsm[:], 16, 512), in_=wuv.rearrange("(cc p) l -> p cc l", p=128))
                cx.inc(d, "sg", 16)
                d = nc.gpsimd.dma_start(out=sb3(wbig[:], 16, 2048), in_=wo_t.rearrange("(cc p) o -> p cc o", p=128))
                cx.inc(d, "sg", 16)
                cx.wait(nc.tensor, "sg")
                with nc.psum_tensor("pv", [128, 512], F32) as pv:
                    for lc in range(4):
                        for hq in range(4):
                            for ccc in range(16):
                                mm = nc.tensor.matmul(
                                    pv[:, 0:512],
                                    wsm[:, ccc * 512 + lc * 128: ccc * 512 + lc * 128 + 128],
                                    wbig[:, ccc * 2048 + hq * 512: ccc * 2048 + hq * 512 + 512],
                                    start=(ccc == 0), stop=(ccc == 15))
                            tv = cx.inc(mm, "st")
                            cx.wait(nc.scalar, "st", tv)
                            a = nc.scalar.activation(
                                out=veff[:, lc * 2048 + hq * 512: lc * 2048 + hq * 512 + 512],
                                in_=pv[:, 0:512], func=AF.Copy)
                            av = cx.inc(a, "sa")
                            cx.wait(nc.tensor, "sa", av)
                cx.wait(nc.gpsimd, "st")
                d = nc.gpsimd.dma_start(out=sb3(wsm[:], 16, 512), in_=wuk.rearrange("(cc p) l -> p cc l", p=128))
                cx.inc(d, "sg", 16)
                d = nc.gpsimd.dma_start(out=sb3(wbig[:], 16, 2048), in_=wq_n.rearrange("(cc p) o -> p cc o", p=128))
                cx.inc(d, "sg", 16)
                cx.wait(nc.tensor, "sg")
                with nc.psum_tensor("pk", [128, 512], F32) as pk:
                    for h in range(NH):
                        for ccc in range(16):
                            mm = nc.tensor.matmul(
                                pk[:, 0:512],
                                wbig[:, ccc * 2048 + h * 128: ccc * 2048 + h * 128 + 128],
                                wsm[:, ccc * 512: ccc * 512 + 512],
                                start=(ccc == 0), stop=(ccc == 15))
                        tv = cx.inc(mm, "st")
                        cx.wait(nc.scalar, "st", tv)
                        a = nc.scalar.activation(out=keff[:, h * 512:(h + 1) * 512],
                                                 in_=pk[:, 0:512], func=AF.Copy, scale=ISQ)
                        av = cx.inc(a, "sa")
                        cx.wait(nc.tensor, "sa", av)
                cx.wait(nc.gpsimd, "st")
                d = nc.gpsimd.dma_start(out=sb3(wbig[:], 16, 2048), in_=wq_t.rearrange("(cc p) o -> p cc o", p=128))
                cx.inc(d, "sg", 16)
                cx.wait(nc.tensor, "sg")

            cx.barrier((nc.tensor, nc.vector, nc.scalar, nc.sync, nc.gpsimd))
            with ExitStack() as sb_:
                xt = sb_.enter_context(nc.sbuf_tensor("xt", [128, 16 * 512], F32))
                xb = sb_.enter_context(nc.sbuf_tensor("xb", [128, 16 * 512], BF16))
                xsq = sb_.enter_context(nc.sbuf_tensor("xsq", [128, 16 * 512], BF16))
                with (nc.psum_tensor("psm", [128, 512], F32) as psm,
                      nc.psum_tensor("psq", [128, 512], F32) as psq,
                      nc.psum_tensor("pbc", [128, 512], F32) as pbc):
                    for tt in range(12):
                        if tt < 8:
                            src_ap = feat_tile(xin, 0, C, tt * 512, 512)
                        else:
                            src_ap = feat_tile(xmy, 0, C, (tt - 8) * 512, 512)
                        d = nc.sync.dma_start(out=sb3(xt[:], 16, 512), in_=src_ap)
                        wv = cx.inc(d, "sw", 16)
                        cx.wait(nc.vector, "sw", wv)
                        cx.wait(nc.vector, "st")
                        for ccc in range(16):
                            v = nc.vector.tensor_copy(out=xb[:, ccc * 512:(ccc + 1) * 512],
                                                      in_=xt[:, ccc * 512:(ccc + 1) * 512])
                        vv = cx.inc(v, "sv")
                        cx.wait(nc.scalar, "sv", vv)
                        for ccc in range(16):
                            a = nc.scalar.activation(out=xsq[:, ccc * 512:(ccc + 1) * 512],
                                                     in_=xb[:, ccc * 512:(ccc + 1) * 512], func=AF.Square)
                        av = cx.inc(a, "sa")
                        cx.wait(nc.tensor, "sa", av)
                        cx.wait(nc.tensor, "sv", vv)
                        for ccc in range(16):
                            nc.tensor.matmul(psm[0:1, 0:512], ones_col[:, 0:1],
                                             xb[:, ccc * 512:(ccc + 1) * 512],
                                             start=(ccc == 0), stop=(ccc == 15))
                        for ccc in range(16):
                            mm = nc.tensor.matmul(psq[0:1, 0:512], ones_col[:, 0:1],
                                                  xsq[:, ccc * 512:(ccc + 1) * 512],
                                                  start=(ccc == 0), stop=(ccc == 15))
                        tv = cx.inc(mm, "st")
                        cx.wait(nc.vector, "st", tv)
                        nc.vector.tensor_scalar(out=scr[0:1, 0:512], in0=psm[0:1, 0:512],
                                                scalar1=1.0 / 2048, scalar2=None, op0=ALU.mult)
                        nc.vector.tensor_scalar(out=scr[0:1, 512:1024], in0=psq[0:1, 0:512],
                                                scalar1=1.0 / 2048, scalar2=None, op0=ALU.mult)
                        ln_small(512)
                        bcast(rstd[0:1, 0:512], rb, 512, pbc)
                        bv = bcast(mrs[0:1, 0:512], mrb, 512, pbc)
                        cx.wait(nc.vector, "sv", bv)
                        for ccc in range(16):
                            nc.vector.tensor_tensor(out=f32a[:], in0=xt[:, ccc * 512:(ccc + 1) * 512],
                                                    in1=rb[:], op=ALU.mult)
                            v = nc.vector.tensor_tensor(out=xb[:, ccc * 512:(ccc + 1) * 512],
                                                        in0=f32a[:], in1=mrb[:], op=ALU.subtract)
                        lv = cx.inc(v, "sv")
                        cx.wait(nc.sync, "sv", lv)
                        if tt < 8:
                            dst_ap = feat_tile(lnx_d, 0, C, tt * 512, 512)
                        else:
                            dst_ap = feat_tile(lnmy_d, 0, C, (tt - 8) * 512, 512)
                        d = nc.sync.dma_start(out=dst_ap, in_=sb3(xb[:], 16, 512))
                        sv_ = cx.inc(d, "ss", 16)
                        cx.wait(nc.vector, "ss", sv_)

            cx.barrier((nc.tensor, nc.vector, nc.scalar, nc.sync, nc.gpsimd))
            if stop_after == "P1":
                return nc
            with ExitStack() as sc:
                lnq = sc.enter_context(nc.sbuf_tensor("lnq", [128, 16 * 256], BF16))
                qlat = sc.enter_context(nc.sbuf_tensor("qlat", [128, 4 * NH * 256], BF16))
                with (nc.psum_tensor("pq", [128, 256], F32) as pq,
                      nc.psum_tensor("pql", [128, 256], F32) as pql):
                    for qt in range(8):
                        cx.wait(nc.sync, "st")
                        d = nc.sync.dma_start(out=sb3(lnq[:], 16, 256), in_=feat_tile(lnmy_d, 0, C, qt * 256, 256))
                        wv = cx.inc(d, "sw", 16)
                        cx.wait(nc.tensor, "sw", wv)
                        for h in range(NH):
                            for ccc in range(16):
                                mm = nc.tensor.matmul(
                                    pq[:, 0:256],
                                    wbig[:, ccc * 2048 + h * 128: ccc * 2048 + h * 128 + 128],
                                    lnq[:, ccc * 256:(ccc + 1) * 256],
                                    start=(ccc == 0), stop=(ccc == 15))
                            tv = cx.inc(mm, "st")
                            cx.wait(nc.scalar, "st", tv)
                            a = nc.scalar.activation(out=qh[:], in_=pq[:, 0:256], func=AF.Copy)
                            av = cx.inc(a, "sa")
                            cx.wait(nc.tensor, "sa", av)
                            for lc in range(4):
                                mm = nc.tensor.matmul(
                                    pql[:, 0:256],
                                    keff[:, h * 512 + lc * 128: h * 512 + lc * 128 + 128],
                                    qh[:], start=True, stop=True)
                                tv = cx.inc(mm, "st")
                                cx.wait(nc.scalar, "st", tv)
                                a = nc.scalar.activation(
                                    out=qlat[:, lc * (NH * 256) + h * 256: lc * (NH * 256) + h * 256 + 256],
                                    in_=pql[:, 0:256], func=AF.Copy)
                                av = cx.inc(a, "sa")
                                cx.wait(nc.tensor, "sa", av)
                        cx.wait(nc.sync, "sa")
                        d = nc.sync.dma_start(out=qlat_d[:, qt * (4 * NH * 256):(qt + 1) * (4 * NH * 256)],
                                              in_=qlat[:])
                        sv_ = cx.inc(d, "ss", 16)
                        cx.wait(nc.scalar, "ss", sv_)

        if stop_after == "P3":
            return nc
        with ExitStack() as sB:
            cx.barrier((nc.tensor, nc.vector, nc.scalar, nc.sync, nc.gpsimd))
            ckvl = sB.enter_context(nc.sbuf_tensor("ckvl", [128, 4 * T], BF16))
            ckvt = sB.enter_context(nc.sbuf_tensor("ckvt", [128, NBLK * NL], BF16))
            with ExitStack() as sd:
                wsm = sd.enter_context(nc.sbuf_tensor("wsm2", [128, 16 * 512], BF16))
                xb = sd.enter_context(nc.sbuf_tensor("xb2", [128, 16 * 512], BF16))
                d = nc.gpsimd.dma_start(out=sb3(wsm[:], 16, 512), in_=wdkv_t.rearrange("(cc p) l -> p cc l", p=128))
                cx.inc(d, "sg", 16)
                cx.wait(nc.tensor, "sg")
                with (nc.psum_tensor("pc", [128, 512], F32) as pc,
                      nc.psum_tensor("ptr", [128, 128], BF16) as ptr):
                    for tt in range(8):
                        cx.wait(nc.sync, "st")
                        d = nc.sync.dma_start(out=sb3(xb[:], 16, 512), in_=feat_tile(lnx_d, 0, C, tt * 512, 512))
                        wv = cx.inc(d, "sw", 16)
                        cx.wait(nc.tensor, "sw", wv)
                        for lc in range(4):
                            for ccc in range(16):
                                mm = nc.tensor.matmul(
                                    pc[:, 0:512],
                                    wsm[:, ccc * 512 + lc * 128: ccc * 512 + lc * 128 + 128],
                                    xb[:, ccc * 512:(ccc + 1) * 512],
                                    start=(ccc == 0), stop=(ccc == 15))
                            tv = cx.inc(mm, "st")
                            cx.wait(nc.scalar, "st", tv)
                            a = nc.scalar.activation(out=ckvl[:, lc * T + tt * 512: lc * T + tt * 512 + 512],
                                                     in_=pc[:, 0:512], func=AF.Copy)
                            av = cx.inc(a, "sa")
                            cx.wait(nc.tensor, "sa", av)
                        for tb in range(4):
                            blk = tt * 4 + tb
                            for lc in range(4):
                                mm = nc.tensor.transpose(
                                    ptr[:, 0:128],
                                    ckvl[:, lc * T + blk * 128: lc * T + blk * 128 + 128],
                                    id_sb[:])
                                tv = cx.inc(mm, "st")
                                cx.wait(nc.vector, "st", tv)
                                v = nc.vector.tensor_copy(
                                    out=ckvt[:, blk * 512 + lc * 128: blk * 512 + lc * 128 + 128],
                                    in_=ptr[:, 0:128])
                                vv = cx.inc(v, "sv")
                                cx.wait(nc.tensor, "sv", vv)

            cx.barrier((nc.tensor, nc.vector, nc.scalar, nc.sync, nc.gpsimd))
            if stop_after == "P2":
                return nc
            with ExitStack() as se:
                qlat = se.enter_context(nc.sbuf_tensor("qlat2", [128, 4 * NH * 256], BF16))
                with (nc.psum_tensor("pl0", [128, 512], F32) as pl0,
                      nc.psum_tensor("pl1", [128, 512], F32) as pl1,
                      nc.psum_tensor("pl2", [128, 512], F32) as pl2,
                      nc.psum_tensor("pl3", [128, 512], F32) as pl3,
                      nc.psum_tensor("psc0", [128, 512], F32) as psc0,
                      nc.psum_tensor("psc1", [128, 512], F32) as psc1,
                      nc.psum_tensor("paux", [128, 512], F32) as paux):
                    pls = [pl0, pl1, pl2, pl3]
                    pscs = [psc0, psc1]
                    for qt in range(8):
                        cx.wait(nc.sync, "st")
                        d = nc.sync.dma_start(out=qlat[:],
                                              in_=qlat_d[:, qt * (4 * NH * 256):(qt + 1) * (4 * NH * 256)])
                        wv = cx.inc(d, "sw", 16)
                        cx.wait(nc.tensor, "sw", wv)
                        kbs = [(0, u) for u in range(2 * qt + 2)] + [(1, u) for u in range(2 * qt + 2)]
                        nkb = len(kbs)
                        for hp in range(8):
                            h0 = 2 * hp
                            for kb, (half, u) in enumerate(kbs):
                                slot = 16 * half + u
                                psc = pscs[kb % 2]
                                for lc in range(4):
                                    mm = nc.tensor.matmul(
                                        psc[:, 0:512],
                                        ckvl[:, lc * T + slot * 128: lc * T + slot * 128 + 128],
                                        qlat[:, lc * (NH * 256) + h0 * 256: lc * (NH * 256) + h0 * 256 + 512],
                                        start=(lc == 0), stop=(lc == 3))
                                tv = cx.inc(mm, "st")
                                cx.wait(nc.scalar, "st", tv)
                                eslot = kb % 2
                                ex = expb[:, eslot * 512:(eslot + 1) * 512]
                                a = nc.scalar.activation(out=ex, in_=psc[:, 0:512], func=AF.Exp)
                                av = cx.inc(a, "sa")
                                cx.wait(nc.vector, "sa", av)
                                off = (2 * half + (u - 2 * qt)) if u >= 2 * qt else -1
                                if off >= 0:
                                    v = nc.vector.tensor_tensor(out=ex, in0=ex,
                                                                in1=mask_sb[:, off * 512:(off + 1) * 512],
                                                                op=ALU.mult)
                                    mv = cx.inc(v, "sv")
                                    cx.wait(nc.vector, "sv", mv)
                                if kb == 0:
                                    v = nc.vector.tensor_copy(out=acc[:], in_=ex)
                                else:
                                    v = nc.vector.tensor_tensor(out=acc[:], in0=acc[:], in1=ex, op=ALU.add)
                                ev = cx.inc(v, "sv")
                                cx.wait(nc.tensor, "sv", ev)
                                for lc in range(4):
                                    mm = nc.tensor.matmul(
                                        pls[lc][:, 0:512],
                                        ckvt[:, slot * 512 + lc * 128: slot * 512 + lc * 128 + 128],
                                        ex, start=(kb == 0), stop=(kb == nkb - 1))
                                cx.inc(mm, "st")
                            mm = nc.tensor.matmul(paux[0:1, 0:512], ones_col_f[:, 0:1], acc[:],
                                                  start=True, stop=True)
                            tv = cx.inc(mm, "st")
                            cx.wait(nc.vector, "st", tv)
                            v = nc.vector.reciprocal(out=sums[0:1, 0:512], in_=paux[0:1, 0:512])
                            rv = cx.inc(v, "sv")
                            cx.wait(nc.tensor, "sv", rv)
                            mm = nc.tensor.matmul(paux[:, 0:512], ones_row[0:1, :],
                                                  sums[0:1, 0:512], start=True, stop=True)
                            tv = cx.inc(mm, "st")
                            cx.wait(nc.vector, "st", tv)
                            v = nc.vector.tensor_copy(out=rb[:], in_=paux[:, 0:512])
                            rv = cx.inc(v, "sv")
                            cx.wait(nc.scalar, "st", tv)
                            for lc in range(4):
                                a = nc.scalar.activation(out=ylat_sb[:, lc * 512:(lc + 1) * 512],
                                                         in_=pls[lc][:, 0:512], func=AF.Copy)
                            av = cx.inc(a, "sa")
                            cx.wait(nc.tensor, "sa", av)
                            cx.wait(nc.tensor, "sv", rv)
                            for hh in range(2):
                                h = h0 + hh
                                for lc in range(4):
                                    mm = nc.tensor.matmul(
                                        paux[:, 0:256],
                                        veff[:, lc * 2048 + h * 128: lc * 2048 + h * 128 + 128],
                                        ylat_sb[:, lc * 512 + hh * 256: lc * 512 + hh * 256 + 256],
                                        start=(lc == 0), stop=(lc == 3))
                                tv = cx.inc(mm, "st")
                                cx.wait(nc.vector, "st", tv)
                                v = nc.vector.tensor_tensor(
                                    out=yqt[:, h * 256:(h + 1) * 256],
                                    in0=paux[:, 0:256], in1=rb[:, hh * 256: hh * 256 + 256], op=ALU.mult)
                                yv = cx.inc(v, "sv")
                                cx.wait(nc.tensor, "sv", yv)
                        cx.wait(nc.sync, "sv")
                        d = nc.sync.dma_start(
                            out=y_d[:, qt * 256:(qt + 1) * 256].rearrange("(h p) t -> p h t", p=128),
                            in_=sb3(yqt[:], NH, 256))
                        sv_ = cx.inc(d, "ss", 16)
                        cx.wait(nc.vector, "ss", sv_)

        if stop_after == "P4":
            return nc
        xout_tgt = xdum.ap() if stop_after == "P5NOSTORE" else xout
        with ExitStack() as sC:
            cx.barrier((nc.tensor, nc.vector, nc.scalar, nc.sync, nc.gpsimd))
            xt = sC.enter_context(nc.sbuf_tensor("xt5", [128, 16 * 512], F32))
            yb = sC.enter_context(nc.sbuf_tensor("yb5", [128, 16 * 512], BF16))
            xsq = sC.enter_context(nc.sbuf_tensor("xsq5", [128, 16 * 512], BF16))
            with (nc.psum_tensor("psm2", [128, 512], F32) as psm,
                  nc.psum_tensor("psq2", [128, 512], F32) as psq,
                  nc.psum_tensor("pbc2", [128, 512], F32) as pbc):
                for tt in range(4):
                    cx.wait(nc.sync, "st")
                    d = nc.sync.dma_start(out=sb3(yb[:], 16, 512), in_=feat_tile(y_d, 0, C, tt * 512, 512))
                    wv = cx.inc(d, "sw", 16)
                    cx.wait(nc.scalar, "sw", wv)
                    cx.wait(nc.scalar, "st")
                    for ccc in range(16):
                        a = nc.scalar.activation(out=xsq[:, ccc * 512:(ccc + 1) * 512],
                                                 in_=yb[:, ccc * 512:(ccc + 1) * 512], func=AF.Square)
                    av = cx.inc(a, "sa")
                    cx.wait(nc.tensor, "sa", av)
                    cx.wait(nc.tensor, "sw", wv)
                    for ccc in range(16):
                        nc.tensor.matmul(psm[0:1, 0:512], ones_col[:, 0:1],
                                         yb[:, ccc * 512:(ccc + 1) * 512],
                                         start=(ccc == 0), stop=(ccc == 15))
                    for ccc in range(16):
                        mm = nc.tensor.matmul(psq[0:1, 0:512], ones_col[:, 0:1],
                                              xsq[:, ccc * 512:(ccc + 1) * 512],
                                              start=(ccc == 0), stop=(ccc == 15))
                    tv = cx.inc(mm, "st")
                    cx.wait(nc.vector, "st", tv)
                    nc.vector.tensor_scalar(out=scr[0:1, 0:512], in0=psm[0:1, 0:512],
                                            scalar1=1.0 / 2048, scalar2=None, op0=ALU.mult)
                    nc.vector.tensor_scalar(out=scr[0:1, 512:1024], in0=psq[0:1, 0:512],
                                            scalar1=1.0 / 2048, scalar2=None, op0=ALU.mult)
                    ln_small(512)
                    bcast(rstd[0:1, 0:512], rb, 512, pbc)
                    bv = bcast(mrs[0:1, 0:512], mrb, 512, pbc)
                    d = nc.sync.dma_start(out=sb3(xt[:], 16, 512), in_=feat_tile(xmy, 0, C, tt * 512, 512))
                    wv = cx.inc(d, "sw", 16)
                    cx.wait(nc.vector, "sw", wv)
                    cx.wait(nc.vector, "sv", bv)
                    for ccc in range(16):
                        nc.vector.tensor_tensor(out=f32a[:], in0=yb[:, ccc * 512:(ccc + 1) * 512],
                                                in1=rb[:], op=ALU.mult)
                        nc.vector.tensor_tensor(out=f32b[:], in0=f32a[:], in1=mrb[:], op=ALU.subtract)
                        v = nc.vector.tensor_tensor(out=xt[:, ccc * 512:(ccc + 1) * 512],
                                                    in0=xt[:, ccc * 512:(ccc + 1) * 512],
                                                    in1=f32b[:], op=ALU.add)
                    lv = cx.inc(v, "sv")
                    cx.wait(nc.sync, "sv", lv)
                    d = nc.sync.dma_start(out=feat_tile(xout_tgt, 0, C, tt * 512, 512), in_=sb3(xt[:], 16, 512))
                    sv_ = cx.inc(d, "ss", 16)
                    cx.wait(nc.vector, "ss", sv_)
            cx.wait(nc.sync, "ss")
    return nc




class LNKit:

    def __init__(self, nc, cx, st):
        self.nc, self.cx = nc, cx
        sb = lambda name, shape, dt: st.enter_context(nc.sbuf_tensor(name, shape, dt))
        self.ones_col = sb("k_ones_col", [128, 1], BF16)
        self.ones_col_f = sb("k_ones_col_f", [128, 1], F32)
        self.ones_row = sb("k_ones_row", [1, 128], F32)
        self.eps = sb("k_eps", [1, 1], F32)
        self.rstd = sb("k_rstd", [1, 512], F32)
        self.mrs = sb("k_mrs", [1, 512], F32)
        self.scr = sb("k_scr", [1, 2048], F32)
        self.rb = sb("k_rb", [128, 512], F32)
        self.mrb = sb("k_mrb", [128, 512], F32)
        self.f32a = sb("k_f32a", [128, 512], F32)
        self.f32b = sb("k_f32b", [128, 512], F32)
        nc.gpsimd.memset(self.ones_col[:], 1.0)
        nc.gpsimd.memset(self.ones_col_f[:], 1.0)
        nc.gpsimd.memset(self.ones_row[:], 1.0)
        g = nc.gpsimd.memset(self.eps[:], EPS)
        cx.inc(g, "sv")

    def ln_small(self, jw, n):
        nc, cx = self.nc, self.cx
        mu = self.scr[0:1, 0:jw]
        ex2 = self.scr[0:1, 512:512 + jw]
        tmp = self.scr[0:1, 1024:1024 + jw]
        v = nc.vector.tensor_scalar(out=mu, in0=mu, scalar1=1.0 / n, scalar2=None, op0=ALU.mult)
        v2 = nc.vector.tensor_scalar(out=ex2, in0=ex2, scalar1=1.0 / n, scalar2=None, op0=ALU.mult)
        cx.wait(nc.vector, "sv", cx.inc(v2, "sv"))
        v = nc.vector.tensor_tensor(out=tmp, in0=mu, in1=mu, op=ALU.mult)
        cx.wait(nc.vector, "sv", cx.inc(v, "sv"))
        v = nc.vector.tensor_tensor(out=ex2, in0=ex2, in1=tmp, op=ALU.subtract)
        cx.inc(v, "sv")
        cx.wait(nc.scalar, "sv")
        a = nc.scalar.activation(out=ex2, in_=ex2, func=AF.Ln, bias=self.eps[0:1, 0:1])
        cx.wait(nc.scalar, "sa", cx.inc(a, "sa"))
        a = nc.scalar.activation(out=self.rstd[0:1, 0:jw], in_=ex2, func=AF.Exp, scale=-0.5)
        cx.inc(a, "sa")
        cx.wait(nc.vector, "sa")
        v = nc.vector.tensor_tensor(out=self.mrs[0:1, 0:jw], in0=mu,
                                    in1=self.rstd[0:1, 0:jw], op=ALU.mult)
        return cx.inc(v, "sv")

    def bcast(self, vec_ap, dst_sb, jw, pb):
        nc, cx = self.nc, self.cx
        cx.wait(nc.tensor, "sv")
        mm = nc.tensor.matmul(pb[:, 0:jw], self.ones_row[0:1, :], vec_ap, start=True, stop=True)
        tv = cx.inc(mm, "st")
        cx.wait(nc.vector, "st", tv)
        v = nc.vector.tensor_copy(out=dst_sb[:, 0:jw], in_=pb[:, 0:jw])
        return cx.inc(v, "sv")

    def stats_tile(self, xb_get, xsq_get, psm, psq, jw, n_cc):
        nc, cx = self.nc, self.cx
        for ccc in range(n_cc):
            nc.tensor.matmul(psm[0:1, 0:jw], self.ones_col[:, 0:1], xb_get(ccc),
                             start=(ccc == 0), stop=(ccc == n_cc - 1))
        for ccc in range(n_cc):
            mm = nc.tensor.matmul(psq[0:1, 0:jw], self.ones_col[:, 0:1], xsq_get(ccc),
                                  start=(ccc == 0), stop=(ccc == n_cc - 1))
        tv = cx.inc(mm, "st")
        cx.wait(nc.vector, "st", tv)
        nc.vector.tensor_copy(out=self.scr[0:1, 0:jw], in_=psm[0:1, 0:jw])
        v = nc.vector.tensor_copy(out=self.scr[0:1, 512:512 + jw], in_=psq[0:1, 0:jw])
        cx.wait(nc.vector, "sv", cx.inc(v, "sv"))
        return tv

    def ln_tile(self, xb_get, xsq_get, psm, psq, pbc, jw, n_cc, n_feat):
        self.stats_tile(xb_get, xsq_get, psm, psq, jw, n_cc)
        self.ln_small(jw, n_feat)
        self.bcast(self.rstd[0:1, 0:jw], self.rb, jw, pbc)
        return self.bcast(self.mrs[0:1, 0:jw], self.mrb, jw, pbc)


def build_lf0(debug=False, stop_after=None):
    nc = bass.Bass("TRN2", target_bir_lowering=False)
    xmy = nc.dram_tensor("xmy", [C, MYT], F32, kind="ExternalInput").ap()
    up_t = nc.dram_tensor("up_t", [C, 8192], F32, kind="ExternalInput").ap()
    down_b = nc.dram_tensor("down_b", [16, 64, 128, 128], F32, kind="ExternalInput").ap()
    usc = nc.dram_tensor("usc", [128, 64], F32, kind="ExternalInput").ap()
    dsc = nc.dram_tensor("dsc", [128, 16], F32, kind="ExternalInput").ap()
    osc = nc.dram_tensor("osc", [1], F32, kind="ExternalInput").ap()
    xout = nc.dram_tensor("xout", [C, MYT], F32, kind="ExternalOutput").ap()
    kind = dict(kind="ExternalOutput") if debug else {}
    lnx3_d = nc.dram_tensor("lnx3_d", [C, 1024], BF16, **kind).ap()
    hid_d = nc.dram_tensor("hid_d", [128, 32 * 1024], BF16, **kind).ap()
    acc_d = nc.dram_tensor("acc_d", [128, 16 * 1024], BF16, **kind).ap()

    st = ExitStack()
    with st:
        cx = Ctx(nc, st, ("sw", "sg", "st", "sv", "sa", "ss"))
        K = LNKit(nc, cx, st)
        BAR = (nc.tensor, nc.vector, nc.scalar, nc.sync, nc.gpsimd)
        sbb = lambda name, shape, dt=BF16: st.enter_context(nc.sbuf_tensor(name, shape, dt))

        us_sb = st.enter_context(nc.sbuf_tensor("us_sb", [128, 64], F32))
        ds2 = st.enter_context(nc.sbuf_tensor("ds2", [128, 16], F32))
        osb = st.enter_context(nc.sbuf_tensor("osb", [1, 1], F32))
        osc_b = st.enter_context(nc.sbuf_tensor("osc_b", [128, 1], F32))
        lnx3 = sbb("lnx3", [128, 16 * 1024])
        hid = sbb("hid", [128, 32 * 1024])
        ffnacc = sbb("ffnacc", [128, 16 * 1024])
        scratch = sbb("scratch", [128, 16384])
        upw = scratch[:, 0:8192]
        upsg = scratch[:, 8192:16384]
        dnw = scratch[:, 0:8192]
        dnsg = scratch[:, 8192:16384]
        xt = scratch[:, 0:8192].bitcast(F32)
        xb = scratch[:, 8192:12288]
        xsq = scratch[:, 12288:16384]

        d = nc.sync.dma_start(out=us_sb[:], in_=usc)
        cx.inc(d, "sw", 16)
        d = nc.sync.dma_start(out=ds2[:], in_=dsc)
        cx.inc(d, "sw", 16)
        d = nc.sync.dma_start(out=osb[:], in_=osc.rearrange("(a o) -> a o", a=1))
        cx.inc(d, "sw", 16)
        with nc.psum_tensor("pb0", [128, 512], F32) as pb0:
            cx.wait(nc.tensor, "sw")
            cx.wait(nc.tensor, "sv")
            mm = nc.tensor.matmul(pb0[:, 0:1], K.ones_row[0:1, :], osb[0:1, 0:1], start=True, stop=True)
            tv = cx.inc(mm, "st")
            cx.wait(nc.vector, "st", tv)
            v = nc.vector.tensor_copy(out=osc_b[:], in_=pb0[:, 0:1])
            vv = cx.inc(v, "sv")
            cx.wait(nc.vector, "sv", vv)
            v = nc.vector.tensor_scalar(out=ds2[:], in0=ds2[:], scalar1=osc_b[:, 0:1],
                                        scalar2=None, op0=ALU.mult)
            cx.inc(v, "sv")

        with (nc.psum_tensor("ph0", [128, 512], F32) as ph0,
              nc.psum_tensor("ph1", [128, 512], F32) as ph1,
              nc.psum_tensor("psm", [128, 512], F32) as psm,
              nc.psum_tensor("psq", [128, 512], F32) as psq,
              nc.psum_tensor("pbc", [128, 512], F32) as pbc):
            phs = [ph0, ph1]
            for th in range(2):
                for tt in range(4):
                    t0 = th * 1024 + tt * 256
                    cx.wait(nc.sync, "st")
                    cx.wait(nc.sync, "sv")
                    d = nc.sync.dma_start(out=sb3(xt[:], 16, 256), in_=feat_tile(xmy, 0, C, t0, 256))
                    wv = cx.inc(d, "sw", 16)
                    cx.wait(nc.vector, "sw", wv)
                    cx.wait(nc.vector, "st")
                    for ccc in range(16):
                        v = nc.vector.tensor_copy(out=xb[:, ccc * 256:(ccc + 1) * 256],
                                                  in_=xt[:, ccc * 256:(ccc + 1) * 256])
                    vv = cx.inc(v, "sv")
                    cx.wait(nc.scalar, "sv", vv)
                    for ccc in range(16):
                        a = nc.scalar.activation(out=xsq[:, ccc * 256:(ccc + 1) * 256],
                                                 in_=xb[:, ccc * 256:(ccc + 1) * 256], func=AF.Square)
                    av = cx.inc(a, "sa")
                    cx.wait(nc.tensor, "sa", av)
                    cx.wait(nc.tensor, "sv", vv)
                    bv = K.ln_tile(lambda ccc: xb[:, ccc * 256:(ccc + 1) * 256],
                                   lambda ccc: xsq[:, ccc * 256:(ccc + 1) * 256],
                                   psm, psq, pbc, 256, 16, 2048)
                    cx.wait(nc.vector, "sv", bv)
                    for ccc in range(16):
                        nc.vector.tensor_tensor(out=K.f32a[:, 0:256], in0=xt[:, ccc * 256:(ccc + 1) * 256],
                                                in1=K.rb[:, 0:256], op=ALU.mult)
                        v = nc.vector.tensor_tensor(
                            out=lnx3[:, ccc * 1024 + tt * 256: ccc * 1024 + tt * 256 + 256],
                            in0=K.f32a[:, 0:256], in1=K.mrb[:, 0:256], op=ALU.subtract)
                    cx.inc(v, "sv")
                cx.barrier(BAR)
                if debug and th == 0:
                    d = nc.sync.dma_start(out=feat_tile(lnx3_d, 0, C, 0, 1024), in_=sb3(lnx3[:], 16, 1024))
                    sv_ = cx.inc(d, "ss", 16)
                    cx.wait(nc.sync, "ss", sv_)
                    if stop_after == "LN3":
                        return nc
                for jh in range(1 if stop_after == "JH0" else 2):
                    cx.barrier(BAR)
                    for jg in range(8):
                        j0 = jh * 4096 + jg * 512
                        cx.wait(nc.gpsimd, "st")
                        cx.wait(nc.gpsimd, "sa")
                        d = nc.gpsimd.dma_start(out=sb3(upw[:], 16, 512),
                                                in_=up_t[:, j0:j0 + 512].rearrange("(cc p) j -> p cc j", p=128))
                        gv = cx.inc(d, "sg", 16)
                        cx.wait(nc.scalar, "sg", gv)
                        cx.wait(nc.scalar, "st")
                        for ccc in range(16):
                            a = nc.scalar.activation(out=upsg[:, ccc * 512:(ccc + 1) * 512],
                                                     in_=upw[:, ccc * 512:(ccc + 1) * 512], func=AF.Sign)
                        av = cx.inc(a, "sa")
                        cx.wait(nc.tensor, "sa", av)
                        cx.wait(nc.tensor, "sv")
                        for jb in range(4):
                            jloc = jg * 4 + jb
                            for half in range(2):
                                for ccc in range(16):
                                    mm = nc.tensor.matmul(
                                        phs[half][:, 0:512],
                                        upsg[:, ccc * 512 + jb * 128: ccc * 512 + jb * 128 + 128],
                                        lnx3[:, ccc * 1024 + half * 512: ccc * 1024 + half * 512 + 512],
                                        start=(ccc == 0), stop=(ccc == 15))
                            tv = cx.inc(mm, "st")
                            cx.wait(nc.scalar, "st", tv)
                            for half in range(2):
                                a = nc.scalar.activation(
                                    out=hid[:, jloc * 1024 + half * 512: jloc * 1024 + half * 512 + 512],
                                    in_=phs[half][:, 0:512], func=AF.Relu,
                                    scale=us_sb[:, (jh * 4096 + jloc * 128) // 128:(jh * 4096 + jloc * 128) // 128 + 1])
                            av = cx.inc(a, "sa")
                            cx.wait(nc.tensor, "sa", av)
                    if debug and th == 0 and jh == 0:
                        cx.barrier(BAR)
                        d = nc.sync.dma_start(out=hid_d[:, :], in_=hid[:])
                        sv_ = cx.inc(d, "ss", 16)
                        cx.wait(nc.sync, "ss", sv_)
                        if stop_after == "UP":
                            return nc
                    cx.barrier(BAR)
                    for cg in range(8):
                        cx.wait(nc.gpsimd, "st")
                        d = nc.gpsimd.dma_start(
                            out=dnw[:].rearrange("p (jb g) -> p jb g", g=256),
                            in_=down_t[jh * 4096:(jh + 1) * 4096, cg * 256:(cg + 1) * 256]
                                .rearrange("(jb p) g -> p jb g", p=128))
                        gv = cx.inc(d, "sg", 16)
                        cx.wait(nc.scalar, "sg", gv)
                        cx.wait(nc.scalar, "st")
                        for jb in range(32):
                            a = nc.scalar.activation(out=dnsg[:, jb * 256:(jb + 1) * 256],
                                                     in_=dnw[:, jb * 256:(jb + 1) * 256], func=AF.Sign)
                        av = cx.inc(a, "sa")
                        cx.wait(nc.tensor, "sa", av)
                        for ci in range(2):
                            ccc = cg * 2 + ci
                            for half in range(2):
                                for jb in range(32):
                                    mm = nc.tensor.matmul(
                                        phs[half][:, 0:512],
                                        dnsg[:, jb * 256 + ci * 128: jb * 256 + ci * 128 + 128],
                                        hid[:, jb * 1024 + half * 512: jb * 1024 + half * 512 + 512],
                                        start=(jb == 0), stop=(jb == 31))
                            tv = cx.inc(mm, "st")
                            cx.wait(nc.vector, "st", tv)
                            for half in range(2):
                                dst = ffnacc[:, ccc * 1024 + half * 512: ccc * 1024 + half * 512 + 512]
                                if jh == 0:
                                    v = nc.vector.tensor_copy(out=dst, in_=phs[half][:, 0:512])
                                else:
                                    nc.vector.tensor_tensor(out=K.f32a[:], in0=phs[half][:, 0:512],
                                                            in1=dst, op=ALU.add)
                                    v = nc.vector.tensor_scalar(out=dst, in0=K.f32a[:],
                                                                scalar1=ds2[:, ccc:ccc + 1],
                                                                scalar2=None, op0=ALU.mult)
                            vv = cx.inc(v, "sv")
                            cx.wait(nc.tensor, "sv", vv)
                if debug and th == 0:
                    cx.barrier(BAR)
                    d = nc.sync.dma_start(out=acc_d[:, :], in_=ffnacc[:])
                    sv_ = cx.inc(d, "ss", 16)
                    cx.wait(nc.sync, "ss", sv_)
                    if stop_after in ("DOWN", "JH0"):
                        return nc
                cx.barrier(BAR)
                for tt in range(4):
                    t0 = th * 1024 + tt * 256
                    cx.wait(nc.scalar, "sv")
                    for ccc in range(16):
                        a = nc.scalar.activation(
                            out=xsq[:, ccc * 256:(ccc + 1) * 256],
                            in_=ffnacc[:, ccc * 1024 + tt * 256: ccc * 1024 + tt * 256 + 256],
                            func=AF.Square)
                    av = cx.inc(a, "sa")
                    cx.wait(nc.tensor, "sa", av)
                    bv = K.ln_tile(lambda ccc: ffnacc[:, ccc * 1024 + tt * 256: ccc * 1024 + tt * 256 + 256],
                                   lambda ccc: xsq[:, ccc * 256:(ccc + 1) * 256],
                                   psm, psq, pbc, 256, 16, 2048)
                    cx.wait(nc.sync, "sv")
                    d = nc.sync.dma_start(out=sb3(xt[:], 16, 256), in_=feat_tile(xmy, 0, C, t0, 256))
                    wv = cx.inc(d, "sw", 16)
                    cx.wait(nc.vector, "sw", wv)
                    cx.wait(nc.vector, "sv", bv)
                    for ccc in range(16):
                        nc.vector.tensor_tensor(out=K.f32a[:, 0:256],
                                                in0=ffnacc[:, ccc * 1024 + tt * 256: ccc * 1024 + tt * 256 + 256],
                                                in1=K.rb[:, 0:256], op=ALU.mult)
                        nc.vector.tensor_tensor(out=K.f32b[:, 0:256], in0=K.f32a[:, 0:256],
                                                in1=K.mrb[:, 0:256], op=ALU.subtract)
                        v = nc.vector.tensor_tensor(out=xt[:, ccc * 256:(ccc + 1) * 256],
                                                    in0=xt[:, ccc * 256:(ccc + 1) * 256],
                                                    in1=K.f32b[:, 0:256], op=ALU.add)
                    lv = cx.inc(v, "sv")
                    cx.wait(nc.sync, "sv", lv)
                    d = nc.sync.dma_start(out=feat_tile(xout, 0, C, t0, 256), in_=sb3(xt[:], 16, 256))
                    sv_ = cx.inc(d, "ss", 16)
                    cx.wait(nc.vector, "ss", sv_)
                cx.barrier(BAR)
        cx.wait(nc.sync, "ss")
    return nc




def build_la1(debug=False):
    nc = bass.Bass("TRN2", target_bir_lowering=False)
    xin = nc.dram_tensor("xfull", [C, T], F32, kind="ExternalInput").ap()
    xmy = nc.dram_tensor("xmy", [C, MYT], F32, kind="ExternalInput").ap()
    wqkv = nc.dram_tensor("wqkv", [C, 3 * C], F32, kind="ExternalInput").ap()
    wout = nc.dram_tensor("wout", [C, C], F32, kind="ExternalInput").ap()
    masks = nc.dram_tensor("masks", [4 * 128, 256], F32, kind="ExternalInput").ap()
    xout = nc.dram_tensor("xout", [C, MYT], F32, kind="ExternalOutput").ap()
    kind = dict(kind="ExternalOutput") if debug else {}
    lnx_d = nc.dram_tensor("lnx_d", [C, T], BF16).ap()
    lnmy_d = nc.dram_tensor("lnmy_d", [C, MYT], BF16).ap()
    vt_d = nc.dram_tensor("vt_d", [T, C], BF16, **kind).ap()
    kf_d = nc.dram_tensor("kf_d", [C, T], BF16, **kind).ap()
    y_d = nc.dram_tensor("y_d", [C, MYT], BF16, **kind).ap()

    st = ExitStack()
    with st:
        cx = Ctx(nc, st, ("sw", "sg", "st", "sv", "sa", "ss"))
        K = LNKit(nc, cx, st)
        BAR = (nc.tensor, nc.vector, nc.scalar, nc.sync, nc.gpsimd)
        sbb = lambda name, shape, dt=BF16: st.enter_context(nc.sbuf_tensor(name, shape, dt))
        mask_sb = sbb("mask_sb", [128, 4 * 256])
        d = nc.gpsimd.dma_start(out=sb3(mask_sb[:], 4, 256),
                                in_=masks.rearrange("(m p) j -> p m j", p=128))
        cx.inc(d, "sg", 16)

        with ExitStack() as sA:
            xt = sA.enter_context(nc.sbuf_tensor("xtA", [128, 16 * 512], F32))
            xb = sA.enter_context(nc.sbuf_tensor("xbA", [128, 16 * 512], BF16))
            xsq = sA.enter_context(nc.sbuf_tensor("xsqA", [128, 16 * 512], BF16))
            with (nc.psum_tensor("psmA", [128, 512], F32) as psm,
                  nc.psum_tensor("psqA", [128, 512], F32) as psq,
                  nc.psum_tensor("pbcA", [128, 512], F32) as pbc):
                for tt in range(12):
                    if tt < 8:
                        src_ap = feat_tile(xin, 0, C, tt * 512, 512)
                    else:
                        src_ap = feat_tile(xmy, 0, C, (tt - 8) * 512, 512)
                    cx.wait(nc.sync, "st")
                    cx.wait(nc.sync, "sv")
                    d = nc.sync.dma_start(out=sb3(xt[:], 16, 512), in_=src_ap)
                    wv = cx.inc(d, "sw", 16)
                    cx.wait(nc.vector, "sw", wv)
                    cx.wait(nc.vector, "st")
                    for ccc in range(16):
                        v = nc.vector.tensor_copy(out=xb[:, ccc * 512:(ccc + 1) * 512],
                                                  in_=xt[:, ccc * 512:(ccc + 1) * 512])
                    vv = cx.inc(v, "sv")
                    cx.wait(nc.scalar, "sv", vv)
                    for ccc in range(16):
                        a = nc.scalar.activation(out=xsq[:, ccc * 512:(ccc + 1) * 512],
                                                 in_=xb[:, ccc * 512:(ccc + 1) * 512], func=AF.Square)
                    av = cx.inc(a, "sa")
                    cx.wait(nc.tensor, "sa", av)
                    cx.wait(nc.tensor, "sv", vv)
                    bv = K.ln_tile(lambda ccc: xb[:, ccc * 512:(ccc + 1) * 512],
                                   lambda ccc: xsq[:, ccc * 512:(ccc + 1) * 512],
                                   psm, psq, pbc, 512, 16, 2048)
                    cx.wait(nc.vector, "sv", bv)
                    for ccc in range(16):
                        nc.vector.tensor_tensor(out=K.f32a[:], in0=xt[:, ccc * 512:(ccc + 1) * 512],
                                                in1=K.rb[:], op=ALU.mult)
                        v = nc.vector.tensor_tensor(out=xb[:, ccc * 512:(ccc + 1) * 512],
                                                    in0=K.f32a[:], in1=K.mrb[:], op=ALU.subtract)
                    lv = cx.inc(v, "sv")
                    cx.wait(nc.sync, "sv", lv)
                    if tt < 8:
                        dst_ap = feat_tile(lnx_d, 0, C, tt * 512, 512)
                    else:
                        dst_ap = feat_tile(lnmy_d, 0, C, (tt - 8) * 512, 512)
                    d = nc.sync.dma_start(out=dst_ap, in_=sb3(xb[:], 16, 512))
                    sv_ = cx.inc(d, "ss", 16)
                    cx.wait(nc.vector, "ss", sv_)

        cx.barrier(BAR)
        with ExitStack() as sB:
            wkv = sB.enter_context(nc.sbuf_tensor("wkv", [128, 16 * 2048], BF16))
            lnt = sB.enter_context(nc.sbuf_tensor("lntB", [128, 16 * 512], BF16))
            vtt = sB.enter_context(nc.sbuf_tensor("vtt", [128, 2048], BF16))
            ktt = sB.enter_context(nc.sbuf_tensor("ktt", [128, NH * 512], BF16))
            d = nc.gpsimd.dma_start(out=sb3(wkv[:], 16, 2048),
                                    in_=wqkv[:, C:2 * C].rearrange("(cc p) o -> p cc o", p=128))
            cx.inc(d, "sg", 16)
            cx.wait(nc.tensor, "sg")
            with nc.psum_tensor("pkB", [128, 512], F32) as pk:
                for tt in range(8):
                    cx.wait(nc.sync, "st")
                    d = nc.sync.dma_start(out=sb3(lnt[:], 16, 512),
                                          in_=feat_tile(lnx_d, 0, C, tt * 512, 512))
                    wv = cx.inc(d, "sw", 16)
                    cx.wait(nc.tensor, "sw", wv)
                    cx.wait(nc.scalar, "ss")
                    for h in range(NH):
                        for ccc in range(16):
                            mm = nc.tensor.matmul(
                                pk[:, 0:512],
                                wkv[:, ccc * 2048 + h * 128: ccc * 2048 + h * 128 + 128],
                                lnt[:, ccc * 512:(ccc + 1) * 512],
                                start=(ccc == 0), stop=(ccc == 15))
                        tv = cx.inc(mm, "st")
                        cx.wait(nc.scalar, "st", tv)
                        a = nc.scalar.activation(out=ktt[:, h * 512:(h + 1) * 512],
                                                 in_=pk[:, 0:512], func=AF.Copy)
                        av = cx.inc(a, "sa")
                        cx.wait(nc.tensor, "sa", av)
                    cx.wait(nc.sync, "sa")
                    d = nc.sync.dma_start(
                        out=kf_d[:, tt * 512:(tt + 1) * 512].rearrange("(h p) t -> p h t", p=128),
                        in_=sb3(ktt[:], NH, 512))
                    sv_ = cx.inc(d, "ss", 16)
            cx.wait(nc.gpsimd, "st")
            d = nc.gpsimd.dma_start(out=sb3(wkv[:], 16, 2048),
                                    in_=wqkv[:, 2 * C:3 * C].rearrange("(cc p) o -> p cc o", p=128))
            gv = cx.inc(d, "sg", 16)
            cx.wait(nc.tensor, "sg", gv)
            with nc.psum_tensor("pvB", [128, 512], F32) as pv:
                for tt in range(8):
                    cx.wait(nc.sync, "st")
                    d = nc.sync.dma_start(out=sb3(lnt[:], 16, 512),
                                          in_=feat_tile(lnx_d, 0, C, tt * 512, 512))
                    wv = cx.inc(d, "sw", 16)
                    cx.wait(nc.tensor, "sw", wv)
                    for tc in range(4):
                        for jg in range(4):
                            for ccc in range(16):
                                mm = nc.tensor.matmul(
                                    pv[:, 0:512],
                                    lnt[:, ccc * 512 + tc * 128: ccc * 512 + tc * 128 + 128],
                                    wkv[:, ccc * 2048 + jg * 512: ccc * 2048 + jg * 512 + 512],
                                    start=(ccc == 0), stop=(ccc == 15))
                            tv = cx.inc(mm, "st")
                            cx.wait(nc.scalar, "st", tv)
                            a = nc.scalar.activation(out=vtt[:, jg * 512:(jg + 1) * 512],
                                                     in_=pv[:, 0:512], func=AF.Copy)
                            av = cx.inc(a, "sa")
                            cx.wait(nc.tensor, "sa", av)
                        cx.wait(nc.sync, "sa")
                        d = nc.sync.dma_start(out=vt_d[tt * 512 + tc * 128: tt * 512 + tc * 128 + 128, :],
                                              in_=vtt[:])
                        sv_ = cx.inc(d, "ss", 16)
                        cx.wait(nc.scalar, "ss", sv_)

        cx.barrier(BAR)
        with ExitStack() as sc:
            kall = sc.enter_context(nc.sbuf_tensor("kallC", [128, 8 * T], BF16))
            wq = sc.enter_context(nc.sbuf_tensor("wqC", [128, 16 * 2048], BF16))
            qsb = sc.enter_context(nc.sbuf_tensor("qsbC", [128, 8 * 256], BF16))
            vtb = sc.enter_context(nc.sbuf_tensor("vtbC", [128, 2 * 2048], BF16))
            exb = sc.enter_context(nc.sbuf_tensor("exbC", [128, 4 * 256], BF16))
            acc = sc.enter_context(nc.sbuf_tensor("accC", [128, 4 * 256], F32))
            ysb = sc.enter_context(nc.sbuf_tensor("ysbC", [128, 8 * 256], BF16))
            lnq = sc.enter_context(nc.sbuf_tensor("lnqC", [128, 16 * 256], BF16))
            sums1 = sc.enter_context(nc.sbuf_tensor("sums1", [1, 256], F32))
            d = nc.gpsimd.dma_start(out=sb3(wq[:], 16, 2048),
                                    in_=wqkv[:, 0:C].rearrange("(cc p) o -> p cc o", p=128))
            cx.inc(d, "sg", 16)
            cx.wait(nc.tensor, "sg")
            with (nc.psum_tensor("py0", [128, 512], F32) as py0,
                  nc.psum_tensor("py1", [128, 512], F32) as py1,
                  nc.psum_tensor("py2", [128, 512], F32) as py2,
                  nc.psum_tensor("py3", [128, 512], F32) as py3,
                  nc.psum_tensor("ps0", [128, 512], F32) as ps0,
                  nc.psum_tensor("ps1", [128, 512], F32) as ps1,
                  nc.psum_tensor("ps2", [128, 512], F32) as ps2,
                  nc.psum_tensor("ps3", [128, 512], F32) as ps3):
                pys = [py0, py1, py2, py3]
                pss = [ps0, ps1, ps2, ps3]
                for hhalf in range(2):
                    hbase = hhalf * 8
                    cx.barrier(BAR)
                    d = nc.sync.dma_start(out=sb3(kall[:], 8, T),
                                          in_=kf_d[hbase * 128:(hbase + 8) * 128, :]
                                          .rearrange("(h p) t -> p h t", p=128))
                    kv_ = cx.inc(d, "sw", 16)
                    cx.wait(nc.tensor, "sw", kv_)
                    for qt in range(8):
                        cx.wait(nc.sync, "st")
                        cx.wait(nc.sync, "sa")
                        d = nc.sync.dma_start(out=sb3(lnq[:], 16, 256),
                                              in_=feat_tile(lnmy_d, 0, C, qt * 256, 256))
                        wv = cx.inc(d, "sw", 16)
                        cx.wait(nc.tensor, "sw", wv)
                        for h_ in range(8):
                            h = hbase + h_
                            for ccc in range(16):
                                mm = nc.tensor.matmul(
                                    ps0[:, 0:256],
                                    wq[:, ccc * 2048 + h * 128: ccc * 2048 + h * 128 + 128],
                                    lnq[:, ccc * 256:(ccc + 1) * 256],
                                    start=(ccc == 0), stop=(ccc == 15))
                            tv = cx.inc(mm, "st")
                            cx.wait(nc.scalar, "st", tv)
                            a = nc.scalar.activation(out=qsb[:, h_ * 256:(h_ + 1) * 256],
                                                     in_=ps0[:, 0:256], func=AF.Copy, scale=ISQ)
                            av = cx.inc(a, "sa")
                            cx.wait(nc.tensor, "sa", av)
                        kbs = [(0, u) for u in range(2 * qt + 2)] + [(1, u) for u in range(2 * qt + 2)]
                        nkb = len(kbs)
                        for hg in range(2):
                            for kb, (half, u) in enumerate(kbs):
                                slot = 16 * half + u
                                vslot = kb % 2
                                cx.wait(nc.sync, "st")
                                d = nc.sync.dma_start(out=vtb[:, vslot * 2048:(vslot + 1) * 2048],
                                                      in_=vt_d[slot * 128:(slot + 1) * 128, :])
                                wv = cx.inc(d, "sw", 16)
                                for hh in range(4):
                                    h_ = hg * 4 + hh
                                    h = hbase + h_
                                    mm = nc.tensor.matmul(
                                        pss[hh][:, 0:256],
                                        kall[:, h_ * T + slot * 128: h_ * T + slot * 128 + 128],
                                        qsb[:, h_ * 256:(h_ + 1) * 256],
                                        start=True, stop=True)
                                    tv = cx.inc(mm, "st")
                                    cx.wait(nc.scalar, "st", tv)
                                    a = nc.scalar.activation(out=exb[:, hh * 256:(hh + 1) * 256],
                                                             in_=pss[hh][:, 0:256], func=AF.Exp)
                                    av = cx.inc(a, "sa")
                                    cx.wait(nc.vector, "sa", av)
                                    off = (2 * half + (u - 2 * qt)) if u >= 2 * qt else -1
                                    ex = exb[:, hh * 256:(hh + 1) * 256]
                                    if off >= 0:
                                        v = nc.vector.tensor_tensor(
                                            out=ex, in0=ex,
                                            in1=mask_sb[:, off * 256:(off + 1) * 256], op=ALU.mult)
                                        cx.inc(v, "sv")
                                    if kb == 0:
                                        v = nc.vector.tensor_copy(out=acc[:, hh * 256:(hh + 1) * 256], in_=ex)
                                    else:
                                        v = nc.vector.tensor_tensor(out=acc[:, hh * 256:(hh + 1) * 256],
                                                                    in0=acc[:, hh * 256:(hh + 1) * 256],
                                                                    in1=ex, op=ALU.add)
                                    ev = cx.inc(v, "sv")
                                    cx.wait(nc.tensor, "sv", ev)
                                    cx.wait(nc.tensor, "sw", wv)
                                    mm = nc.tensor.matmul(
                                        pys[hh][:, 0:256],
                                        vtb[:, vslot * 2048 + h * 128: vslot * 2048 + h * 128 + 128],
                                        ex, start=(kb == 0), stop=(kb == nkb - 1))
                                    cx.inc(mm, "st")
                            for hh in range(4):
                                h_ = hg * 4 + hh
                                mm = nc.tensor.matmul(ps0[0:1, 0:256], K.ones_col_f[:, 0:1],
                                                      acc[:, hh * 256:(hh + 1) * 256],
                                                      start=True, stop=True)
                                tv = cx.inc(mm, "st")
                                cx.wait(nc.vector, "st", tv)
                                v = nc.vector.reciprocal(out=sums1[0:1, 0:256], in_=ps0[0:1, 0:256])
                                rv = cx.inc(v, "sv")
                                cx.wait(nc.tensor, "sv", rv)
                                mm = nc.tensor.matmul(ps0[:, 0:256], K.ones_row[0:1, :],
                                                      sums1[0:1, 0:256], start=True, stop=True)
                                tv = cx.inc(mm, "st")
                                cx.wait(nc.vector, "st", tv)
                                v = nc.vector.tensor_copy(out=K.rb[:, 0:256], in_=ps0[:, 0:256])
                                rv = cx.inc(v, "sv")
                                cx.wait(nc.vector, "sv", rv)
                                v = nc.vector.tensor_tensor(out=ysb[:, h_ * 256:(h_ + 1) * 256],
                                                            in0=pys[hh][:, 0:256],
                                                            in1=K.rb[:, 0:256], op=ALU.mult)
                                yv = cx.inc(v, "sv")
                                cx.wait(nc.tensor, "sv", yv)
                        cx.wait(nc.sync, "sv")
                        d = nc.sync.dma_start(
                            out=y_d[hbase * 128:(hbase + 8) * 128, qt * 256:(qt + 1) * 256]
                                .rearrange("(h p) t -> p h t", p=128),
                            in_=sb3(ysb[:], 8, 256))
                        sv_ = cx.inc(d, "ss", 16)
                        cx.wait(nc.vector, "ss", sv_)

        cx.barrier(BAR)
        with ExitStack() as sD:
            wo = sD.enter_context(nc.sbuf_tensor("woD", [128, 16 * 2048], BF16))
            ytl = sD.enter_context(nc.sbuf_tensor("ytlD", [128, 16 * 256], BF16))
            att = sD.enter_context(nc.sbuf_tensor("attD", [128, 16 * 256], BF16))
            xsq = sD.enter_context(nc.sbuf_tensor("xsqD", [128, 16 * 256], BF16))
            xt = sD.enter_context(nc.sbuf_tensor("xtD", [128, 16 * 256], F32))
            d = nc.gpsimd.dma_start(out=sb3(wo[:], 16, 2048),
                                    in_=wout.rearrange("(cc p) o -> p cc o", p=128))
            cx.inc(d, "sg", 16)
            cx.wait(nc.tensor, "sg")
            with (nc.psum_tensor("poD", [128, 512], F32) as po,
                  nc.psum_tensor("psmD", [128, 512], F32) as psm,
                  nc.psum_tensor("psqD", [128, 512], F32) as psq,
                  nc.psum_tensor("pbcD", [128, 512], F32) as pbc):
                for qt in range(8):
                    cx.wait(nc.sync, "st")
                    cx.wait(nc.sync, "sa")
                    cx.wait(nc.sync, "sv")
                    d = nc.sync.dma_start(out=sb3(ytl[:], 16, 256),
                                          in_=feat_tile(y_d, 0, C, qt * 256, 256))
                    wv = cx.inc(d, "sw", 16)
                    cx.wait(nc.tensor, "sw", wv)
                    for oc in range(16):
                        for ccc in range(16):
                            mm = nc.tensor.matmul(
                                po[:, 0:256],
                                wo[:, ccc * 2048 + oc * 128: ccc * 2048 + oc * 128 + 128],
                                ytl[:, ccc * 256:(ccc + 1) * 256],
                                start=(ccc == 0), stop=(ccc == 15))
                        tv = cx.inc(mm, "st")
                        cx.wait(nc.scalar, "st", tv)
                        a = nc.scalar.activation(out=att[:, oc * 256:(oc + 1) * 256],
                                                 in_=po[:, 0:256], func=AF.Copy)
                        av = cx.inc(a, "sa")
                        cx.wait(nc.tensor, "sa", av)
                    cx.wait(nc.scalar, "sv")
                    for ccc in range(16):
                        a = nc.scalar.activation(out=xsq[:, ccc * 256:(ccc + 1) * 256],
                                                 in_=att[:, ccc * 256:(ccc + 1) * 256], func=AF.Square)
                    av = cx.inc(a, "sa")
                    cx.wait(nc.tensor, "sa", av)
                    bv = K.ln_tile(lambda ccc: att[:, ccc * 256:(ccc + 1) * 256],
                                   lambda ccc: xsq[:, ccc * 256:(ccc + 1) * 256],
                                   psm, psq, pbc, 256, 16, 2048)
                    cx.wait(nc.sync, "sv")
                    d = nc.sync.dma_start(out=sb3(xt[:], 16, 256), in_=feat_tile(xmy, 0, C, qt * 256, 256))
                    wv = cx.inc(d, "sw", 16)
                    cx.wait(nc.vector, "sw", wv)
                    cx.wait(nc.vector, "sv", bv)
                    for ccc in range(16):
                        nc.vector.tensor_tensor(out=K.f32a[:, 0:256],
                                                in0=att[:, ccc * 256:(ccc + 1) * 256],
                                                in1=K.rb[:, 0:256], op=ALU.mult)
                        nc.vector.tensor_tensor(out=K.f32b[:, 0:256], in0=K.f32a[:, 0:256],
                                                in1=K.mrb[:, 0:256], op=ALU.subtract)
                        v = nc.vector.tensor_tensor(out=xt[:, ccc * 256:(ccc + 1) * 256],
                                                    in0=xt[:, ccc * 256:(ccc + 1) * 256],
                                                    in1=K.f32b[:, 0:256], op=ALU.add)
                    lv = cx.inc(v, "sv")
                    cx.wait(nc.sync, "sv", lv)
                    d = nc.sync.dma_start(out=feat_tile(xout, 0, C, qt * 256, 256), in_=sb3(xt[:], 16, 256))
                    sv_ = cx.inc(d, "ss", 16)
                    cx.wait(nc.vector, "ss", sv_)
            cx.wait(nc.sync, "ss")
    return nc


def build_lf1():
    nc = bass.Bass("TRN2", target_bir_lowering=False)
    xmy = nc.dram_tensor("xmy", [C, MYT], F32, kind="ExternalInput").ap()
    w1t = nc.dram_tensor("w1t", [C, 4096], F32, kind="ExternalInput").ap()
    w2t = nc.dram_tensor("w2t", [4096, 4096], F32, kind="ExternalInput").ap()
    w3t = nc.dram_tensor("w3t", [4096, C], F32, kind="ExternalInput").ap()
    b1 = nc.dram_tensor("b1", [128, 32], F32, kind="ExternalInput").ap()
    b2 = nc.dram_tensor("b2", [128, 32], F32, kind="ExternalInput").ap()
    b3 = nc.dram_tensor("b3", [128, 16], F32, kind="ExternalInput").ap()
    xout = nc.dram_tensor("xout", [C, MYT], F32, kind="ExternalOutput").ap()

    st = ExitStack()
    with st:
        cx = Ctx(nc, st, ("sw", "sg", "st", "sv", "sa", "ss"))
        K = LNKit(nc, cx, st)
        BAR = (nc.tensor, nc.vector, nc.scalar, nc.sync, nc.gpsimd)
        sbb = lambda name, shape, dt=BF16: st.enter_context(nc.sbuf_tensor(name, shape, dt))
        b1s = st.enter_context(nc.sbuf_tensor("b1s", [128, 32], F32))
        b2s = st.enter_context(nc.sbuf_tensor("b2s", [128, 32], F32))
        b3s = st.enter_context(nc.sbuf_tensor("b3s", [128, 16], F32))
        for ap_, t_ in ((b1, b1s), (b2, b2s), (b3, b3s)):
            d = nc.sync.dma_start(out=t_[:], in_=ap_)
            cx.inc(d, "sw", 16)
        h1 = sbb("h1", [128, 32 * 1024])
        h2 = sbb("h2", [128, 32 * 1024])
        scratch = sbb("scrF1", [128, 16384])
        xt = scratch[:, 0:8192].bitcast(F32)
        xb = scratch[:, 8192:12288]
        xsq = scratch[:, 12288:16384]
        with (nc.psum_tensor("pF0", [128, 512], F32) as pf0,
              nc.psum_tensor("pF1", [128, 512], F32) as pf1,
              nc.psum_tensor("psmF", [128, 512], F32) as psm,
              nc.psum_tensor("psqF", [128, 512], F32) as psq,
              nc.psum_tensor("pbcF", [128, 512], F32) as pbc):
            pfs = [pf0, pf1]
            for th in range(2):
                with ExitStack() as sA:
                    lnx7 = h2[:, 0:16 * 1024]
                    wbuf = sA.enter_context(nc.sbuf_tensor(f"wbufF{th}", [128, 16 * 512], BF16))
                    cx.barrier(BAR)
                    for tt in range(4):
                        t0 = th * 1024 + tt * 256
                        cx.wait(nc.sync, "st")
                        cx.wait(nc.sync, "sv")
                        d = nc.sync.dma_start(out=sb3(xt[:], 16, 256), in_=feat_tile(xmy, 0, C, t0, 256))
                        wv = cx.inc(d, "sw", 16)
                        cx.wait(nc.vector, "sw", wv)
                        cx.wait(nc.vector, "st")
                        for ccc in range(16):
                            v = nc.vector.tensor_copy(out=xb[:, ccc * 256:(ccc + 1) * 256],
                                                      in_=xt[:, ccc * 256:(ccc + 1) * 256])
                        vv = cx.inc(v, "sv")
                        cx.wait(nc.scalar, "sv", vv)
                        for ccc in range(16):
                            a = nc.scalar.activation(out=xsq[:, ccc * 256:(ccc + 1) * 256],
                                                     in_=xb[:, ccc * 256:(ccc + 1) * 256], func=AF.Square)
                        av = cx.inc(a, "sa")
                        cx.wait(nc.tensor, "sa", av)
                        cx.wait(nc.tensor, "sv", vv)
                        bv = K.ln_tile(lambda ccc: xb[:, ccc * 256:(ccc + 1) * 256],
                                       lambda ccc: xsq[:, ccc * 256:(ccc + 1) * 256],
                                       psm, psq, pbc, 256, 16, 2048)
                        cx.wait(nc.vector, "sv", bv)
                        for ccc in range(16):
                            nc.vector.tensor_tensor(out=K.f32a[:, 0:256],
                                                    in0=xt[:, ccc * 256:(ccc + 1) * 256],
                                                    in1=K.rb[:, 0:256], op=ALU.mult)
                            v = nc.vector.tensor_tensor(
                                out=lnx7[:, ccc * 1024 + tt * 256: ccc * 1024 + tt * 256 + 256],
                                in0=K.f32a[:, 0:256], in1=K.mrb[:, 0:256], op=ALU.subtract)
                        cx.inc(v, "sv")
                    cx.barrier(BAR)
                    for jg in range(8):
                        cx.wait(nc.gpsimd, "st")
                        d = nc.gpsimd.dma_start(out=sb3(wbuf[:], 16, 512),
                                                in_=w1t[:, jg * 512:(jg + 1) * 512]
                                                .rearrange("(cc p) j -> p cc j", p=128))
                        gv = cx.inc(d, "sg", 16)
                        cx.wait(nc.tensor, "sg", gv)
                        for jb in range(4):
                            jc = jg * 4 + jb
                            for half in range(2):
                                for ccc in range(16):
                                    mm = nc.tensor.matmul(
                                        pfs[half][:, 0:512],
                                        wbuf[:, ccc * 512 + jb * 128: ccc * 512 + jb * 128 + 128],
                                        lnx7[:, ccc * 1024 + half * 512: ccc * 1024 + half * 512 + 512],
                                        start=(ccc == 0), stop=(ccc == 15))
                            tv = cx.inc(mm, "st")
                            cx.wait(nc.scalar, "st", tv)
                            for half in range(2):
                                a = nc.scalar.activation(
                                    out=h1[:, jc * 1024 + half * 512: jc * 1024 + half * 512 + 512],
                                    in_=pfs[half][:, 0:512], func=AF.Relu, bias=b1s[:, jc:jc + 1])
                            av = cx.inc(a, "sa")
                            cx.wait(nc.tensor, "sa", av)
                with ExitStack() as sB:
                    w2buf = sB.enter_context(nc.sbuf_tensor(f"w2buf{th}", [128, 32 * 256], BF16))
                    cx.barrier(BAR)
                    for jg in range(16):
                        cx.wait(nc.gpsimd, "st")
                        d = nc.gpsimd.dma_start(out=sb3(w2buf[:], 32, 256),
                                                in_=w2t[:, jg * 256:(jg + 1) * 256]
                                                .rearrange("(cc p) j -> p cc j", p=128))
                        gv = cx.inc(d, "sg", 16)
                        cx.wait(nc.tensor, "sg", gv)
                        for jb in range(2):
                            jc = jg * 2 + jb
                            for half in range(2):
                                for ccc in range(32):
                                    mm = nc.tensor.matmul(
                                        pfs[half][:, 0:512],
                                        w2buf[:, ccc * 256 + jb * 128: ccc * 256 + jb * 128 + 128],
                                        h1[:, ccc * 1024 + half * 512: ccc * 1024 + half * 512 + 512],
                                        start=(ccc == 0), stop=(ccc == 31))
                            tv = cx.inc(mm, "st")
                            cx.wait(nc.scalar, "st", tv)
                            for half in range(2):
                                a = nc.scalar.activation(
                                    out=h2[:, jc * 1024 + half * 512: jc * 1024 + half * 512 + 512],
                                    in_=pfs[half][:, 0:512], func=AF.Relu, bias=b2s[:, jc:jc + 1])
                            av = cx.inc(a, "sa")
                            cx.wait(nc.tensor, "sa", av)
                with ExitStack() as sC:
                    w3buf = sC.enter_context(nc.sbuf_tensor(f"w3buf{th}", [128, 32 * 256], BF16))
                    ffno = h1[:, 0:16 * 1024]
                    cx.barrier(BAR)
                    for cg in range(8):
                        cx.wait(nc.gpsimd, "st")
                        d = nc.gpsimd.dma_start(out=sb3(w3buf[:], 32, 256),
                                                in_=w3t[:, cg * 256:(cg + 1) * 256]
                                                .rearrange("(cc p) j -> p cc j", p=128))
                        gv = cx.inc(d, "sg", 16)
                        cx.wait(nc.tensor, "sg", gv)
                        for cb in range(2):
                            ccx = cg * 2 + cb
                            for half in range(2):
                                for ccc in range(32):
                                    mm = nc.tensor.matmul(
                                        pfs[half][:, 0:512],
                                        w3buf[:, ccc * 256 + cb * 128: ccc * 256 + cb * 128 + 128],
                                        h2[:, ccc * 1024 + half * 512: ccc * 1024 + half * 512 + 512],
                                        start=(ccc == 0), stop=(ccc == 31))
                            tv = cx.inc(mm, "st")
                            cx.wait(nc.scalar, "st", tv)
                            for half in range(2):
                                a = nc.scalar.activation(
                                    out=ffno[:, ccx * 1024 + half * 512: ccx * 1024 + half * 512 + 512],
                                    in_=pfs[half][:, 0:512], func=AF.Identity, bias=b3s[:, ccx:ccx + 1])
                            av = cx.inc(a, "sa")
                            cx.wait(nc.tensor, "sa", av)
                    cx.barrier(BAR)
                    for tt in range(4):
                        t0 = th * 1024 + tt * 256
                        cx.wait(nc.scalar, "sv")
                        for ccc in range(16):
                            a = nc.scalar.activation(
                                out=xsq[:, ccc * 256:(ccc + 1) * 256],
                                in_=ffno[:, ccc * 1024 + tt * 256: ccc * 1024 + tt * 256 + 256],
                                func=AF.Square)
                        av = cx.inc(a, "sa")
                        cx.wait(nc.tensor, "sa", av)
                        bv = K.ln_tile(lambda ccc: ffno[:, ccc * 1024 + tt * 256: ccc * 1024 + tt * 256 + 256],
                                       lambda ccc: xsq[:, ccc * 256:(ccc + 1) * 256],
                                       psm, psq, pbc, 256, 16, 2048)
                        cx.wait(nc.sync, "sv")
                        d = nc.sync.dma_start(out=sb3(xt[:], 16, 256), in_=feat_tile(xmy, 0, C, t0, 256))
                        wv = cx.inc(d, "sw", 16)
                        cx.wait(nc.vector, "sw", wv)
                        cx.wait(nc.vector, "sv", bv)
                        for ccc in range(16):
                            nc.vector.tensor_tensor(out=K.f32a[:, 0:256],
                                                    in0=ffno[:, ccc * 1024 + tt * 256: ccc * 1024 + tt * 256 + 256],
                                                    in1=K.rb[:, 0:256], op=ALU.mult)
                            nc.vector.tensor_tensor(out=K.f32b[:, 0:256], in0=K.f32a[:, 0:256],
                                                    in1=K.mrb[:, 0:256], op=ALU.subtract)
                            v = nc.vector.tensor_tensor(out=xt[:, ccc * 256:(ccc + 1) * 256],
                                                        in0=xt[:, ccc * 256:(ccc + 1) * 256],
                                                        in1=K.f32b[:, 0:256], op=ALU.add)
                        lv = cx.inc(v, "sv")
                        cx.wait(nc.sync, "sv", lv)
                        d = nc.sync.dma_start(out=feat_tile(xout, 0, C, t0, 256), in_=sb3(xt[:], 16, 256))
                        sv_ = cx.inc(d, "ss", 16)
                        cx.wait(nc.vector, "ss", sv_)
        cx.wait(nc.sync, "ss")
    return nc




RG_PAIRS = [[0, 1], [2, 3], [4, 5], [6, 7]]


def build_fused(debug=False):
    nc = bass.Bass("TRN2", target_bir_lowering=False)
    xmy = nc.dram_tensor("xmy", [C, MYT], F32, kind="ExternalInput").ap()
    wq_t = nc.dram_tensor("wq_t", [C, C], F32, kind="ExternalInput").ap()
    wq_n = nc.dram_tensor("wq_n", [C, C], F32, kind="ExternalInput").ap()
    wdkv_t = nc.dram_tensor("wdkv_t", [C, NL], F32, kind="ExternalInput").ap()
    wuk = nc.dram_tensor("wuk", [C, NL], F32, kind="ExternalInput").ap()
    wuv = nc.dram_tensor("wuv", [C, NL], F32, kind="ExternalInput").ap()
    wo_t = nc.dram_tensor("wo_t", [C, C], F32, kind="ExternalInput").ap()
    masks0 = nc.dram_tensor("masks0", [4 * 128, 512], F32, kind="ExternalInput").ap()
    ident = nc.dram_tensor("ident", [128, 128], F32, kind="ExternalInput").ap()
    up_t = nc.dram_tensor("up_t", [C, 8192], F32, kind="ExternalInput").ap()
    down_b = nc.dram_tensor("down_b", [16, 64, 128, 128], F32, kind="ExternalInput").ap()
    usc = nc.dram_tensor("usc", [128, 64], F32, kind="ExternalInput").ap()
    dsc = nc.dram_tensor("dsc", [128, 16], F32, kind="ExternalInput").ap()
    osc = nc.dram_tensor("osc", [1], F32, kind="ExternalInput").ap()
    wqkv = nc.dram_tensor("wqkv", [C, 3 * C], F32, kind="ExternalInput").ap()
    wout = nc.dram_tensor("wout", [C, C], F32, kind="ExternalInput").ap()
    masks1 = nc.dram_tensor("masks1", [4 * 128, 256], F32, kind="ExternalInput").ap()
    w1t = nc.dram_tensor("w1t", [C, 4096], F32, kind="ExternalInput").ap()
    w2t = nc.dram_tensor("w2t", [4096, 4096], F32, kind="ExternalInput").ap()
    w3t = nc.dram_tensor("w3t", [4096, C], F32, kind="ExternalInput").ap()
    b1 = nc.dram_tensor("b1", [128, 32], F32, kind="ExternalInput").ap()
    b2 = nc.dram_tensor("b2", [128, 32], F32, kind="ExternalInput").ap()
    b3 = nc.dram_tensor("b3", [128, 16], F32, kind="ExternalInput").ap()
    xout = nc.dram_tensor("xout", [C, MYT], F32, kind="ExternalOutput").ap()
    kind = dict(kind="ExternalOutput") if debug else {}
    lnmy_d = nc.dram_tensor("lnmy_d", [C, MYT], BF16).ap()
    x_d = nc.dram_tensor("x_d", [C, MYT], F32, **kind).ap()
    qlat_d = nc.dram_tensor("qlat_d", [128, 8 * 4 * NH * 256], BF16).ap()
    y_d = nc.dram_tensor("y_d", [C, MYT], BF16).ap()
    ag0_in = nc.dram_tensor("ag0_in", [MYT, NL], BF16)
    ag0_out = nc.dram_tensor("ag0_out", [T, NL], BF16)
    ag1_in = nc.dram_tensor("ag1_in", [MYT, 2 * C], BF16)
    ag1_out = nc.dram_tensor("ag1_out", [T, 2 * C], BF16)
    upsg_d = nc.dram_tensor("upsg_d", [128, 16 * 8192], BF16).ap()
    dnsg_d = nc.dram_tensor("dnsg_d", [128, 8192 * 16], BF16).ap()

    st = ExitStack()
    with st:
        cx = Ctx(nc, st, ("sw", "sg", "st", "sv", "sa", "ss", "sc"))
        K = LNKit(nc, cx, st)
        BAR = (nc.tensor, nc.vector, nc.scalar, nc.sync, nc.gpsimd)
        sbb = lambda name, shape, dt=BF16: st.enter_context(nc.sbuf_tensor(name, shape, dt))
        m0_sb = sbb("m0_sb", [128, 4 * 512])
        m1_sb = sbb("m1_sb", [128, 4 * 256])
        id_sb = sbb("id_sb", [128, 128])
        d = nc.gpsimd.dma_start(out=sb3(m0_sb[:], 4, 512), in_=masks0.rearrange("(m p) j -> p m j", p=128))
        cx.inc(d, "sg", 16)
        d = nc.gpsimd.dma_start(out=sb3(m1_sb[:], 4, 256), in_=masks1.rearrange("(m p) j -> p m j", p=128))
        cx.inc(d, "sg", 16)
        d = nc.gpsimd.dma_start(out=id_sb[:], in_=ident)
        cx.inc(d, "sg", 16)

        def emit_ln_pass(x_src, ln_dst, xt, xb, xsq, psm, psq, pbc, ntiles=4):
            for tt in range(ntiles):
                cx.wait(nc.sync, "st")
                cx.wait(nc.sync, "sv")
                d = nc.sync.dma_start(out=sb3(xt[:], 16, 512), in_=feat_tile(x_src, 0, C, tt * 512, 512))
                wv = cx.inc(d, "sw", 16)
                cx.wait(nc.vector, "sw", wv)
                cx.wait(nc.vector, "st")
                for ccc in range(16):
                    v = nc.vector.tensor_copy(out=xb[:, ccc * 512:(ccc + 1) * 512],
                                              in_=xt[:, ccc * 512:(ccc + 1) * 512])
                vv = cx.inc(v, "sv")
                cx.wait(nc.scalar, "sv", vv)
                for ccc in range(16):
                    a = nc.scalar.activation(out=xsq[:, ccc * 512:(ccc + 1) * 512],
                                             in_=xb[:, ccc * 512:(ccc + 1) * 512], func=AF.Square)
                av = cx.inc(a, "sa")
                cx.wait(nc.tensor, "sa", av)
                cx.wait(nc.tensor, "sv", vv)
                bv = K.ln_tile(lambda ccc: xb[:, ccc * 512:(ccc + 1) * 512],
                               lambda ccc: xsq[:, ccc * 512:(ccc + 1) * 512],
                               psm, psq, pbc, 512, 16, 2048)
                cx.wait(nc.vector, "sv", bv)
                for ccc in range(16):
                    nc.vector.tensor_tensor(out=K.f32a[:], in0=xt[:, ccc * 512:(ccc + 1) * 512],
                                            in1=K.rb[:], op=ALU.mult)
                    v = nc.vector.tensor_tensor(out=xb[:, ccc * 512:(ccc + 1) * 512],
                                                in0=K.f32a[:], in1=K.mrb[:], op=ALU.subtract)
                lv = cx.inc(v, "sv")
                cx.wait(nc.sync, "sv", lv)
                d = nc.sync.dma_start(out=feat_tile(ln_dst, 0, C, tt * 512, 512), in_=sb3(xb[:], 16, 512))
                sv_ = cx.inc(d, "ss", 16)
                cx.wait(nc.vector, "ss", sv_)

        def emit_combine(y_get, ysq_get, x_src, x_dst, ln_dst, xt, xb, xsq, psm, psq, pbc,
                         tilew, ntiles, base_t=0):
            jw = tilew
            for tt in range(ntiles):
                t0 = base_t + tt * jw
                cx.wait(nc.scalar, "sv")
                cx.wait(nc.scalar, "st")
                for ccc in range(16):
                    a = nc.scalar.activation(out=xsq[:, ccc * jw:(ccc + 1) * jw],
                                             in_=y_get(ccc, tt), func=AF.Square)
                av = cx.inc(a, "sa")
                cx.wait(nc.tensor, "sa", av)
                bv = K.ln_tile(lambda ccc: y_get(ccc, tt),
                               lambda ccc: xsq[:, ccc * jw:(ccc + 1) * jw],
                               psm, psq, pbc, jw, 16, 2048)
                cx.wait(nc.sync, "sv")
                d = nc.sync.dma_start(out=sb3(xt[:, 0:16 * jw], 16, jw), in_=feat_tile(x_src, 0, C, t0, jw))
                wv = cx.inc(d, "sw", 16)
                cx.wait(nc.vector, "sw", wv)
                cx.wait(nc.vector, "sv", bv)
                for ccc in range(16):
                    nc.vector.tensor_tensor(out=K.f32a[:, 0:jw], in0=y_get(ccc, tt),
                                            in1=K.rb[:, 0:jw], op=ALU.mult)
                    nc.vector.tensor_tensor(out=K.f32b[:, 0:jw], in0=K.f32a[:, 0:jw],
                                            in1=K.mrb[:, 0:jw], op=ALU.subtract)
                    v = nc.vector.tensor_tensor(out=xt[:, ccc * jw:(ccc + 1) * jw],
                                                in0=xt[:, ccc * jw:(ccc + 1) * jw],
                                                in1=K.f32b[:, 0:jw], op=ALU.add)
                lv = cx.inc(v, "sv")
                cx.wait(nc.sync, "sv", lv)
                d = nc.sync.dma_start(out=feat_tile(x_dst, 0, C, t0, jw), in_=sb3(xt[:, 0:16 * jw], 16, jw))
                sv_ = cx.inc(d, "ss", 16)
                if ln_dst is None:
                    cx.wait(nc.vector, "ss", sv_)
                    continue
                cx.wait(nc.vector, "st")
                for ccc in range(16):
                    v = nc.vector.tensor_copy(out=xb[:, ccc * jw:(ccc + 1) * jw],
                                              in_=xt[:, ccc * jw:(ccc + 1) * jw])
                vv = cx.inc(v, "sv")
                cx.wait(nc.scalar, "sv", vv)
                for ccc in range(16):
                    a = nc.scalar.activation(out=xsq[:, ccc * jw:(ccc + 1) * jw],
                                             in_=xb[:, ccc * jw:(ccc + 1) * jw], func=AF.Square)
                av = cx.inc(a, "sa")
                cx.wait(nc.tensor, "sa", av)
                cx.wait(nc.tensor, "sv", vv)
                bv = K.ln_tile(lambda ccc: xb[:, ccc * jw:(ccc + 1) * jw],
                               lambda ccc: xsq[:, ccc * jw:(ccc + 1) * jw],
                               psm, psq, pbc, jw, 16, 2048)
                cx.wait(nc.vector, "sv", bv)
                for ccc in range(16):
                    nc.vector.tensor_tensor(out=K.f32a[:, 0:jw], in0=xt[:, ccc * jw:(ccc + 1) * jw],
                                            in1=K.rb[:, 0:jw], op=ALU.mult)
                    v = nc.vector.tensor_tensor(out=xb[:, ccc * jw:(ccc + 1) * jw],
                                                in0=K.f32a[:, 0:jw], in1=K.mrb[:, 0:jw], op=ALU.subtract)
                lv = cx.inc(v, "sv")
                cx.wait(nc.sync, "sv", lv)
                cx.wait(nc.sync, "ss", sv_)
                d = nc.sync.dma_start(out=feat_tile(ln_dst, 0, C, t0, jw), in_=sb3(xb[:, 0:16 * jw], 16, jw))
                sv_ = cx.inc(d, "ss", 16)
                cx.wait(nc.vector, "ss", sv_)

        with ExitStack() as sA:
            keff = sA.enter_context(nc.sbuf_tensor("keff", [128, NH * NL], BF16))
            veff = sA.enter_context(nc.sbuf_tensor("veff", [128, 4 * NH * 128], BF16))
            with ExitStack() as s0:
                wsm = s0.enter_context(nc.sbuf_tensor("wsm0", [128, 16 * 512], BF16))
                wbig = s0.enter_context(nc.sbuf_tensor("wbig0", [128, 16 * 2048], BF16))
                d = nc.gpsimd.dma_start(out=sb3(wsm[:], 16, 512), in_=wuv.rearrange("(cc p) l -> p cc l", p=128))
                cx.inc(d, "sg", 16)
                d = nc.gpsimd.dma_start(out=sb3(wbig[:], 16, 2048), in_=wo_t.rearrange("(cc p) o -> p cc o", p=128))
                cx.inc(d, "sg", 16)
                cx.wait(nc.tensor, "sg")
                with nc.psum_tensor("pv0", [128, 512], F32) as pv:
                    for lc in range(4):
                        for hq in range(4):
                            for ccc in range(16):
                                mm = nc.tensor.matmul(
                                    pv[:, 0:512],
                                    wsm[:, ccc * 512 + lc * 128: ccc * 512 + lc * 128 + 128],
                                    wbig[:, ccc * 2048 + hq * 512: ccc * 2048 + hq * 512 + 512],
                                    start=(ccc == 0), stop=(ccc == 15))
                            tv = cx.inc(mm, "st")
                            cx.wait(nc.scalar, "st", tv)
                            a = nc.scalar.activation(
                                out=veff[:, lc * 2048 + hq * 512: lc * 2048 + hq * 512 + 512],
                                in_=pv[:, 0:512], func=AF.Copy)
                            av = cx.inc(a, "sa")
                            cx.wait(nc.tensor, "sa", av)
                cx.wait(nc.gpsimd, "st")
                d = nc.gpsimd.dma_start(out=sb3(wsm[:], 16, 512), in_=wuk.rearrange("(cc p) l -> p cc l", p=128))
                cx.inc(d, "sg", 16)
                d = nc.gpsimd.dma_start(out=sb3(wbig[:], 16, 2048), in_=wq_n.rearrange("(cc p) o -> p cc o", p=128))
                cx.inc(d, "sg", 16)
                cx.wait(nc.tensor, "sg")
                with nc.psum_tensor("pk0", [128, 512], F32) as pk:
                    for h in range(NH):
                        for ccc in range(16):
                            mm = nc.tensor.matmul(
                                pk[:, 0:512],
                                wbig[:, ccc * 2048 + h * 128: ccc * 2048 + h * 128 + 128],
                                wsm[:, ccc * 512: ccc * 512 + 512],
                                start=(ccc == 0), stop=(ccc == 15))
                        tv = cx.inc(mm, "st")
                        cx.wait(nc.scalar, "st", tv)
                        a = nc.scalar.activation(out=keff[:, h * 512:(h + 1) * 512],
                                                 in_=pk[:, 0:512], func=AF.Copy, scale=ISQ)
                        av = cx.inc(a, "sa")
                        cx.wait(nc.tensor, "sa", av)
                with ExitStack() as s1:
                    xt = s1.enter_context(nc.sbuf_tensor("xt1", [128, 16 * 512], F32))
                    xb = s1.enter_context(nc.sbuf_tensor("xb1", [128, 16 * 512], BF16))
                    xsq = s1.enter_context(nc.sbuf_tensor("xsq1", [128, 16 * 512], BF16))
                    ckt = s1.enter_context(nc.sbuf_tensor("ckt1", [128, 512], BF16))
                    with (nc.psum_tensor("psm1", [128, 512], F32) as psm,
                          nc.psum_tensor("psq1", [128, 512], F32) as psq,
                          nc.psum_tensor("pbc1", [128, 512], F32) as pbc):
                        cx.barrier(BAR)
                        emit_ln_pass(xmy, lnmy_d, xt, xb, xsq, psm, psq, pbc)
                        cx.wait(nc.gpsimd, "st")
                        d = nc.gpsimd.dma_start(out=sb3(wsm[:], 16, 512),
                                                in_=wdkv_t.rearrange("(cc p) l -> p cc l", p=128))
                        gv = cx.inc(d, "sg", 16)
                        cx.wait(nc.tensor, "sg", gv)
                        for tt in range(4):
                            cx.wait(nc.sync, "st")
                            d = nc.sync.dma_start(out=sb3(xb[:], 16, 512),
                                                  in_=feat_tile(lnmy_d, 0, C, tt * 512, 512))
                            wv = cx.inc(d, "sw", 16)
                            cx.wait(nc.tensor, "sw", wv)
                            for tc in range(4):
                                for ccc in range(16):
                                    mm = nc.tensor.matmul(
                                        pbc[:, 0:512],
                                        xb[:, ccc * 512 + tc * 128: ccc * 512 + tc * 128 + 128],
                                        wsm[:, ccc * 512: ccc * 512 + 512],
                                        start=(ccc == 0), stop=(ccc == 15))
                                tv = cx.inc(mm, "st")
                                cx.wait(nc.scalar, "st", tv)
                                a = nc.scalar.activation(out=ckt[:], in_=pbc[:, 0:512], func=AF.Copy)
                                av = cx.inc(a, "sa")
                                cx.wait(nc.tensor, "sa", av)
                                cx.wait(nc.sync, "sa", av)
                                d = nc.sync.dma_start(
                                    out=ag0_in.ap()[tt * 512 + tc * 128: tt * 512 + tc * 128 + 128, :],
                                    in_=ckt[:])
                                sv_ = cx.inc(d, "ss", 16)
                                cx.wait(nc.scalar, "ss", sv_)
                        cx.wait(nc.gpsimd, "ss")
                        cc_ = nc.gpsimd.collective_compute(
                            "AllGather", ALU.bypass, replica_groups=RG_PAIRS,
                            ins=[ag0_in.ap().bitcast(F32).opt()],
                            outs=[ag0_out.ap().bitcast(F32).opt()])
                        cx.inc(cc_, "sc")
            with ExitStack() as s2:
                wbig = s2.enter_context(nc.sbuf_tensor("wbig2", [128, 16 * 2048], BF16))
                lnq = s2.enter_context(nc.sbuf_tensor("lnq2", [128, 16 * 256], BF16))
                qlat = s2.enter_context(nc.sbuf_tensor("qlat2", [128, 4 * NH * 256], BF16))
                qh = s2.enter_context(nc.sbuf_tensor("qh2", [128, 256], BF16))
                cx.wait(nc.gpsimd, "st")
                cx.wait(nc.gpsimd, "ss")
                d = nc.gpsimd.dma_start(out=sb3(wbig[:], 16, 2048), in_=wq_t.rearrange("(cc p) o -> p cc o", p=128))
                cx.inc(d, "sg", 16)
                cx.wait(nc.tensor, "sg")
                with (nc.psum_tensor("pq2", [128, 256], F32) as pq,
                      nc.psum_tensor("pql0", [128, 256], F32) as pql0,
                      nc.psum_tensor("pql1", [128, 256], F32) as pql1,
                      nc.psum_tensor("pql2", [128, 256], F32) as pql2,
                      nc.psum_tensor("pql3", [128, 256], F32) as pql3):
                    pqls = [pql0, pql1, pql2, pql3]
                    for qt in range(8):
                        cx.wait(nc.sync, "st")
                        d = nc.sync.dma_start(out=sb3(lnq[:], 16, 256), in_=feat_tile(lnmy_d, 0, C, qt * 256, 256))
                        wv = cx.inc(d, "sw", 16)
                        cx.wait(nc.tensor, "sw", wv)
                        prev_sa = None
                        for h in range(NH):
                            for ccc in range(16):
                                mm = nc.tensor.matmul(
                                    pq[:, 0:256],
                                    wbig[:, ccc * 2048 + h * 128: ccc * 2048 + h * 128 + 128],
                                    lnq[:, ccc * 256:(ccc + 1) * 256],
                                    start=(ccc == 0), stop=(ccc == 15))
                            tv = cx.inc(mm, "st")
                            cx.wait(nc.scalar, "st", tv)
                            a = nc.scalar.activation(out=qh[:], in_=pq[:, 0:256], func=AF.Copy)
                            av = cx.inc(a, "sa")
                            cx.wait(nc.tensor, "sa", av)
                            for lc in range(4):
                                mm = nc.tensor.matmul(
                                    pqls[lc][:, 0:256],
                                    keff[:, h * 512 + lc * 128: h * 512 + lc * 128 + 128],
                                    qh[:], start=True, stop=True)
                            tv = cx.inc(mm, "st")
                            cx.wait(nc.scalar, "st", tv)
                            for lc in range(4):
                                a = nc.scalar.activation(
                                    out=qlat[:, lc * (NH * 256) + h * 256: lc * (NH * 256) + h * 256 + 256],
                                    in_=pqls[lc][:, 0:256], func=AF.Copy)
                            prev_sa = cx.inc(a, "sa")
                        cx.wait(nc.sync, "sa")
                        d = nc.sync.dma_start(out=qlat_d[:, qt * (4 * NH * 256):(qt + 1) * (4 * NH * 256)],
                                              in_=qlat[:])
                        sv_ = cx.inc(d, "ss", 16)
                        cx.wait(nc.scalar, "ss", sv_)
            with ExitStack() as s3:
                cx.barrier(BAR)
                ckvl = s3.enter_context(nc.sbuf_tensor("ckvl3", [128, 4 * T], BF16))
                ckvt = s3.enter_context(nc.sbuf_tensor("ckvt3", [128, NBLK * NL], BF16))
                qlat = s3.enter_context(nc.sbuf_tensor("qlat3", [128, 4 * NH * 256], BF16))
                expb = s3.enter_context(nc.sbuf_tensor("expb3", [128, 2 * 512], BF16))
                ylat_sb = s3.enter_context(nc.sbuf_tensor("ylat3", [128, 4 * 512], BF16))
                yqt = s3.enter_context(nc.sbuf_tensor("yqt3", [128, NH * 256], BF16))
                acc = s3.enter_context(nc.sbuf_tensor("acc3", [128, 512], F32))
                sums = s3.enter_context(nc.sbuf_tensor("sums3", [1, 512], F32))
                cx.wait(nc.sync, "sc")
                d = nc.sync.dma_start(out=sb3(ckvt[:], NBLK, 512),
                                      in_=ag0_out.ap().rearrange("(s p) l -> p s l", p=128))
                wv = cx.inc(d, "sw", 16)
                for lc in range(4):
                    d = nc.sync.dma_start_transpose(ckvl[:, lc * T:(lc + 1) * T],
                                                    ag0_out.ap()[:, lc * 128:(lc + 1) * 128])
                    wv = cx.inc(d, "sw", 16)
                cx.wait(nc.tensor, "sw", wv)
                with (nc.psum_tensor("pl0", [128, 512], F32) as pl0,
                      nc.psum_tensor("pl1", [128, 512], F32) as pl1,
                      nc.psum_tensor("pl2", [128, 512], F32) as pl2,
                      nc.psum_tensor("pl3", [128, 512], F32) as pl3,
                      nc.psum_tensor("psc0", [128, 512], F32) as psc0,
                      nc.psum_tensor("psc1", [128, 512], F32) as psc1,
                      nc.psum_tensor("paux", [128, 512], F32) as paux):
                    pls = [pl0, pl1, pl2, pl3]
                    pscs = [psc0, psc1]
                    for qt in range(8):
                        cx.wait(nc.sync, "st")
                        d = nc.sync.dma_start(out=qlat[:],
                                              in_=qlat_d[:, qt * (4 * NH * 256):(qt + 1) * (4 * NH * 256)])
                        wv = cx.inc(d, "sw", 16)
                        cx.wait(nc.tensor, "sw", wv)
                        kbs = [(0, u) for u in range(2 * qt + 2)] + [(1, u) for u in range(2 * qt + 2)]
                        nkb = len(kbs)
                        for hp in range(8):
                            h0 = 2 * hp
                            evs = {}
                            exs = {}
                            slots = {}

                            def emit_ylat0(kb):
                                cx.wait(nc.tensor, "sv", evs[kb])
                                for lc in range(4):
                                    mmy = nc.tensor.matmul(
                                        pls[lc][:, 0:512],
                                        ckvt[:, slots[kb] * 512 + lc * 128: slots[kb] * 512 + lc * 128 + 128],
                                        exs[kb], start=(kb == 0), stop=(kb == nkb - 1))
                                cx.inc(mmy, "st")

                            for kb, (half, u) in enumerate(kbs):
                                slot = 16 * half + u
                                slots[kb] = slot
                                psc = pscs[kb % 2]
                                for lc in range(4):
                                    mm = nc.tensor.matmul(
                                        psc[:, 0:512],
                                        ckvl[:, lc * T + slot * 128: lc * T + slot * 128 + 128],
                                        qlat[:, lc * (NH * 256) + h0 * 256: lc * (NH * 256) + h0 * 256 + 512],
                                        start=(lc == 0), stop=(lc == 3))
                                tv = cx.inc(mm, "st")
                                cx.wait(nc.scalar, "st", tv)
                                eslot = kb % 2
                                ex = expb[:, eslot * 512:(eslot + 1) * 512]
                                exs[kb] = ex
                                a = nc.scalar.activation(out=ex, in_=psc[:, 0:512], func=AF.Exp)
                                av = cx.inc(a, "sa")
                                cx.wait(nc.vector, "sa", av)
                                off = (2 * half + (u - 2 * qt)) if u >= 2 * qt else -1
                                if off >= 0:
                                    v = nc.vector.tensor_tensor(out=ex, in0=ex,
                                                                in1=m0_sb[:, off * 512:(off + 1) * 512],
                                                                op=ALU.mult)
                                    cx.inc(v, "sv")
                                if kb == 0:
                                    v = nc.vector.tensor_copy(out=acc[:], in_=ex)
                                else:
                                    v = nc.vector.tensor_tensor(out=acc[:], in0=acc[:], in1=ex, op=ALU.add)
                                evs[kb] = cx.inc(v, "sv")
                                if kb >= 1:
                                    emit_ylat0(kb - 1)
                            emit_ylat0(nkb - 1)
                            mm = nc.tensor.matmul(paux[0:1, 0:512], K.ones_col_f[:, 0:1], acc[:],
                                                  start=True, stop=True)
                            tv = cx.inc(mm, "st")
                            cx.wait(nc.scalar, "st", tv)
                            a = nc.scalar.activation(out=sums[0:1, 0:512], in_=paux[0:1, 0:512], func=AF.Ln)
                            cx.wait(nc.scalar, "sa", cx.inc(a, "sa"))
                            a = nc.scalar.activation(out=sums[0:1, 0:512], in_=sums[0:1, 0:512],
                                                     func=AF.Exp, scale=-1.0)
                            rv = cx.inc(a, "sa")
                            cx.wait(nc.tensor, "sa", rv)
                            mm = nc.tensor.matmul(paux[:, 0:512], K.ones_row[0:1, :],
                                                  sums[0:1, 0:512], start=True, stop=True)
                            tv = cx.inc(mm, "st")
                            cx.wait(nc.vector, "st", tv)
                            v = nc.vector.tensor_copy(out=K.rb[:], in_=paux[:, 0:512])
                            rv = cx.inc(v, "sv")
                            cx.wait(nc.scalar, "st", tv)
                            for lc in range(4):
                                a = nc.scalar.activation(out=ylat_sb[:, lc * 512:(lc + 1) * 512],
                                                         in_=pls[lc][:, 0:512], func=AF.Copy)
                            av = cx.inc(a, "sa")
                            cx.wait(nc.tensor, "sa", av)
                            cx.wait(nc.tensor, "sv", rv)
                            for hh in range(2):
                                h = h0 + hh
                                for lc in range(4):
                                    mm = nc.tensor.matmul(
                                        paux[:, 0:256],
                                        veff[:, lc * 2048 + h * 128: lc * 2048 + h * 128 + 128],
                                        ylat_sb[:, lc * 512 + hh * 256: lc * 512 + hh * 256 + 256],
                                        start=(lc == 0), stop=(lc == 3))
                                tv = cx.inc(mm, "st")
                                cx.wait(nc.vector, "st", tv)
                                v = nc.vector.tensor_tensor(
                                    out=yqt[:, h * 256:(h + 1) * 256],
                                    in0=paux[:, 0:256], in1=K.rb[:, hh * 256: hh * 256 + 256], op=ALU.mult)
                                yv = cx.inc(v, "sv")
                                cx.wait(nc.tensor, "sv", yv)
                        cx.wait(nc.sync, "sv")
                        d = nc.sync.dma_start(
                            out=y_d[:, qt * 256:(qt + 1) * 256].rearrange("(h p) t -> p h t", p=128),
                            in_=sb3(yqt[:], NH, 256))
                        sv_ = cx.inc(d, "ss", 16)
                        cx.wait(nc.vector, "ss", sv_)
        with ExitStack() as s4:
            cx.barrier(BAR)
            xt = s4.enter_context(nc.sbuf_tensor("xt4", [128, 16 * 512], F32))
            xb = s4.enter_context(nc.sbuf_tensor("xb4", [128, 16 * 512], BF16))
            xsq = s4.enter_context(nc.sbuf_tensor("xsq4", [128, 16 * 512], BF16))
            yb = s4.enter_context(nc.sbuf_tensor("yb4", [128, 16 * 512], BF16))
            with (nc.psum_tensor("psm4", [128, 512], F32) as psm,
                  nc.psum_tensor("psq4", [128, 512], F32) as psq,
                  nc.psum_tensor("pbc4", [128, 512], F32) as pbc):
                for tt4 in range(4):
                    cx.wait(nc.sync, "st")
                    cx.wait(nc.sync, "sa")
                    d = nc.sync.dma_start(out=sb3(yb[:], 16, 512), in_=feat_tile(y_d, 0, C, tt4 * 512, 512))
                    wv = cx.inc(d, "sw", 16)
                    cx.wait(nc.scalar, "sw", wv)
                    cx.wait(nc.tensor, "sw", wv)
                    emit_combine(lambda ccc, _t: yb[:, ccc * 512:(ccc + 1) * 512],
                                 None, xmy, x_d, lnmy_d, xt, xb, xsq, psm, psq, pbc,
                                 512, 1, base_t=tt4 * 512)

        with ExitStack() as sF:
            cx.barrier(BAR)
            us_sb = sF.enter_context(nc.sbuf_tensor("us_sbF", [128, 64], F32))
            ds2 = sF.enter_context(nc.sbuf_tensor("ds2F", [128, 16], F32))
            osb = sF.enter_context(nc.sbuf_tensor("osbF", [1, 1], F32))
            osc_b = sF.enter_context(nc.sbuf_tensor("osc_bF", [128, 1], F32))
            lnx3 = sF.enter_context(nc.sbuf_tensor("lnx3F", [128, 16 * 1024], BF16))
            hid = sF.enter_context(nc.sbuf_tensor("hidF", [128, 32 * 1024], BF16))
            ffnacc = sF.enter_context(nc.sbuf_tensor("ffnaccF", [128, 16 * 1024], BF16))
            scratch = sF.enter_context(nc.sbuf_tensor("scratchF", [128, 16384], BF16))
            upw = scratch[:, 0:8192]
            upsg = scratch[:, 8192:16384]
            dnw = scratch[:, 0:8192]
            dnsg = scratch[:, 8192:16384]
            xtf = scratch[:, 0:8192].bitcast(F32)
            xbf = scratch[:, 8192:12288]
            xsqf = scratch[:, 12288:16384]
            xt512 = hid[:, 0:16384].bitcast(F32)
            xb512 = hid[:, 16384:24576]
            xsq512 = hid[:, 24576:32768]
            d = nc.sync.dma_start(out=us_sb[:], in_=usc)
            cx.inc(d, "sw", 16)
            d = nc.sync.dma_start(out=ds2[:], in_=dsc)
            cx.inc(d, "sw", 16)
            d = nc.sync.dma_start(out=osb[:], in_=osc.rearrange("(a o) -> a o", a=1))
            cx.inc(d, "sw", 16)
            with (nc.psum_tensor("ph0F", [128, 512], F32) as ph0,
                  nc.psum_tensor("ph1F", [128, 512], F32) as ph1,
                  nc.psum_tensor("ph2F", [128, 512], F32) as ph2,
                  nc.psum_tensor("ph3F", [128, 512], F32) as ph3,
                  nc.psum_tensor("psmF0", [128, 512], F32) as psm,
                  nc.psum_tensor("psqF0", [128, 512], F32) as psq,
                  nc.psum_tensor("pbcF0", [128, 512], F32) as pbc):
                phs = [ph0, ph1]
                phs2 = [ph2, ph3]
                cx.wait(nc.tensor, "sw")
                cx.wait(nc.tensor, "sv")
                mm = nc.tensor.matmul(pbc[:, 0:1], K.ones_row[0:1, :], osb[0:1, 0:1], start=True, stop=True)
                tv = cx.inc(mm, "st")
                cx.wait(nc.vector, "st", tv)
                v = nc.vector.tensor_copy(out=osc_b[:], in_=pbc[:, 0:1])
                vv = cx.inc(v, "sv")
                cx.wait(nc.vector, "sv", vv)
                v = nc.vector.tensor_scalar(out=ds2[:], in0=ds2[:], scalar1=osc_b[:, 0:1],
                                            scalar2=None, op0=ALU.mult)
                cx.inc(v, "sv")
                for th in range(2):
                    cx.barrier(BAR)
                    d = nc.sync.dma_start(out=sb3(lnx3[:], 16, 1024),
                                          in_=feat_tile(lnmy_d, 0, C, th * 1024, 1024))
                    wv = cx.inc(d, "sw", 16)
                    cx.wait(nc.tensor, "sw", wv)
                    cx.wait(nc.scalar, "sw", wv)
                    for jh in range(2):
                        cx.barrier(BAR)
                        jgN = 16
                        sgv = {}

                        def up_load(jg):
                            j0 = jh * 4096 + jg * 256
                            sl = jg % 2
                            d = nc.gpsimd.dma_start(
                                out=sb3(upw[:, sl * 4096:(sl + 1) * 4096], 16, 256),
                                in_=up_t[:, j0:j0 + 256].rearrange("(cc p) j -> p cc j", p=128))
                            sgv[jg] = cx.inc(d, "sg", 16)

                        up_load(0)
                        mmv = {}
                        for jg in range(jgN):
                            sl = jg % 2
                            cx.wait(nc.scalar, "sg", sgv[jg])
                            if jg >= 2:
                                cx.wait(nc.scalar, "st", mmv[jg - 2])
                            for ccc in range(16):
                                a = nc.scalar.activation(
                                    out=upsg[:, sl * 4096 + ccc * 256: sl * 4096 + ccc * 256 + 256],
                                    in_=upw[:, sl * 4096 + ccc * 256: sl * 4096 + ccc * 256 + 256],
                                    func=AF.Sign)
                            av = cx.inc(a, "sa")
                            if jg + 1 < jgN:
                                cx.wait(nc.gpsimd, "sa", av)
                                up_load(jg + 1)
                            cx.wait(nc.tensor, "sa", av)
                            cx.wait(nc.tensor, "sv")
                            for jb in range(2):
                                jloc = jg * 2 + jb
                                pp = phs if jloc % 2 == 0 else phs2
                                for half in range(2):
                                    for ccc in range(16):
                                        mm = nc.tensor.matmul(
                                            pp[half][:, 0:512],
                                            upsg[:, sl * 4096 + ccc * 256 + jb * 128: sl * 4096 + ccc * 256 + jb * 128 + 128],
                                            lnx3[:, ccc * 1024 + half * 512: ccc * 1024 + half * 512 + 512],
                                            start=(ccc == 0), stop=(ccc == 15))
                                tv = cx.inc(mm, "st")
                                mmv[jg] = tv
                                cx.wait(nc.scalar, "st", tv)
                                for half in range(2):
                                    a = nc.scalar.activation(
                                        out=hid[:, jloc * 1024 + half * 512: jloc * 1024 + half * 512 + 512],
                                        in_=pp[half][:, 0:512], func=AF.Relu,
                                        scale=us_sb[:, jh * 32 + jloc: jh * 32 + jloc + 1])
                                av = cx.inc(a, "sa")
                        cx.barrier(BAR)
                        dgv = {}
                        dmmv = {}

                        def dn_load(cg):
                            sl = cg % 2
                            d = nc.gpsimd.dma_start(
                                out=dnw[:, sl * 4096:(sl + 1) * 4096].rearrange("p (jb g) -> p jb g", g=128),
                                in_=down_b[cg, jh * 32:(jh + 1) * 32].rearrange("jb p g -> p jb g"))
                            dgv[cg] = cx.inc(d, "sg", 16)

                        dn_load(0)
                        for cg in range(16):
                            sl = cg % 2
                            cx.wait(nc.scalar, "sg", dgv[cg])
                            if cg >= 2:
                                cx.wait(nc.scalar, "st", dmmv[cg - 2])
                            for q8 in range(8):
                                a = nc.scalar.activation(
                                    out=dnsg[:, sl * 4096 + q8 * 512: sl * 4096 + q8 * 512 + 512],
                                    in_=dnw[:, sl * 4096 + q8 * 512: sl * 4096 + q8 * 512 + 512], func=AF.Sign)
                            av = cx.inc(a, "sa")
                            if cg + 1 < 16:
                                cx.wait(nc.gpsimd, "sa", av)
                                dn_load(cg + 1)
                            cx.wait(nc.tensor, "sa", av)
                            ccc = cg
                            pp = phs if cg % 2 == 0 else phs2
                            for half in range(2):
                                for jb in range(32):
                                    mm = nc.tensor.matmul(
                                        pp[half][:, 0:512],
                                        dnsg[:, sl * 4096 + jb * 128: sl * 4096 + jb * 128 + 128],
                                        hid[:, jb * 1024 + half * 512: jb * 1024 + half * 512 + 512],
                                        start=(jb == 0), stop=(jb == 31))
                            tv = cx.inc(mm, "st")
                            dmmv[cg] = tv
                            cx.wait(nc.vector, "st", tv)
                            for half in range(2):
                                dst = ffnacc[:, ccc * 1024 + half * 512: ccc * 1024 + half * 512 + 512]
                                if jh == 0:
                                    v = nc.vector.tensor_copy(out=dst, in_=pp[half][:, 0:512])
                                else:
                                    nc.vector.tensor_tensor(out=K.f32a[:], in0=pp[half][:, 0:512],
                                                            in1=dst, op=ALU.add)
                                    v = nc.vector.tensor_scalar(out=dst, in0=K.f32a[:],
                                                                scalar1=ds2[:, ccc:ccc + 1],
                                                                scalar2=None, op0=ALU.mult)
                            vv = cx.inc(v, "sv")
                    cx.barrier(BAR)
                    emit_combine(lambda ccc, tt: ffnacc[:, ccc * 1024 + tt * 512: ccc * 1024 + tt * 512 + 512],
                                 None, x_d, x_d, lnmy_d, xt512, xb512, xsq512, psm, psq, pbc,
                                 512, 2, base_t=th * 1024)

        q_d = nc.dram_tensor("q_d", [C, MYT], BF16).ap()
        with ExitStack() as sG:
            cx.barrier(BAR)
            with ExitStack() as sg1:
                wkv = sg1.enter_context(nc.sbuf_tensor("wkvG", [128, 16 * 2048], BF16))
                lnt = sg1.enter_context(nc.sbuf_tensor("lntG", [128, 16 * 512], BF16))
                kvt = sg1.enter_context(nc.sbuf_tensor("kvtG", [128, 4096], BF16))
                with nc.psum_tensor("pkG", [128, 512], F32) as pk:
                    for sel in range(2):
                        cx.wait(nc.gpsimd, "st")
                        d = nc.gpsimd.dma_start(out=sb3(wkv[:], 16, 2048),
                                                in_=wqkv[:, (1 + sel) * C:(2 + sel) * C]
                                                .rearrange("(cc p) o -> p cc o", p=128))
                        gv = cx.inc(d, "sg", 16)
                        cx.wait(nc.tensor, "sg", gv)
                        for tt in range(4):
                            cx.wait(nc.sync, "st")
                            d = nc.sync.dma_start(out=sb3(lnt[:], 16, 512),
                                                  in_=feat_tile(lnmy_d, 0, C, tt * 512, 512))
                            wv = cx.inc(d, "sw", 16)
                            cx.wait(nc.tensor, "sw", wv)
                            for tc in range(4):
                                for jg in range(4):
                                    for ccc in range(16):
                                        mm = nc.tensor.matmul(
                                            pk[:, 0:512],
                                            lnt[:, ccc * 512 + tc * 128: ccc * 512 + tc * 128 + 128],
                                            wkv[:, ccc * 2048 + jg * 512: ccc * 2048 + jg * 512 + 512],
                                            start=(ccc == 0), stop=(ccc == 15))
                                    tv = cx.inc(mm, "st")
                                    cx.wait(nc.scalar, "st", tv)
                                    a = nc.scalar.activation(out=kvt[:, jg * 512:(jg + 1) * 512],
                                                             in_=pk[:, 0:512], func=AF.Copy)
                                    av = cx.inc(a, "sa")
                                    cx.wait(nc.tensor, "sa", av)
                                cx.wait(nc.sync, "sa")
                                r0 = tt * 512 + tc * 128
                                d = nc.sync.dma_start(out=ag1_in.ap()[r0:r0 + 128, sel * C:(sel + 1) * C],
                                                      in_=kvt[:, 0:2048])
                                sv_ = cx.inc(d, "ss", 16)
                                cx.wait(nc.scalar, "ss", sv_)
                cx.wait(nc.gpsimd, "ss")
                cc_ = nc.gpsimd.collective_compute(
                    "AllGather", ALU.bypass, replica_groups=RG_PAIRS,
                    ins=[ag1_in.ap().bitcast(F32).opt()],
                    outs=[ag1_out.ap().bitcast(F32).opt()])
                cx.inc(cc_, "sc")
                cx.wait(nc.gpsimd, "st")
                d = nc.gpsimd.dma_start(out=sb3(wkv[:], 16, 2048),
                                        in_=wqkv[:, 0:C].rearrange("(cc p) o -> p cc o", p=128))
                gv = cx.inc(d, "sg", 16)
                cx.wait(nc.tensor, "sg", gv)
                with nc.psum_tensor("pqG", [128, 512], F32) as pq:
                    for tt in range(4):
                        cx.wait(nc.sync, "st")
                        d = nc.sync.dma_start(out=sb3(lnt[:], 16, 512),
                                              in_=feat_tile(lnmy_d, 0, C, tt * 512, 512))
                        wv = cx.inc(d, "sw", 16)
                        cx.wait(nc.tensor, "sw", wv)
                        for h in range(NH):
                            for ccc in range(16):
                                mm = nc.tensor.matmul(
                                    pq[:, 0:512],
                                    wkv[:, ccc * 2048 + h * 128: ccc * 2048 + h * 128 + 128],
                                    lnt[:, ccc * 512:(ccc + 1) * 512],
                                    start=(ccc == 0), stop=(ccc == 15))
                            tv = cx.inc(mm, "st")
                            cx.wait(nc.scalar, "st", tv)
                            a = nc.scalar.activation(out=kvt[:, 0:512], in_=pq[:, 0:512],
                                                     func=AF.Copy, scale=ISQ)
                            av = cx.inc(a, "sa")
                            cx.wait(nc.sync, "sa", av)
                            d = nc.sync.dma_start(
                                out=q_d[h * 128:(h + 1) * 128, tt * 512:(tt + 1) * 512],
                                in_=kvt[:, 0:512])
                            sv_ = cx.inc(d, "ss", 16)
                            cx.wait(nc.scalar, "ss", sv_)
                            cx.wait(nc.tensor, "ss", sv_)
            with ExitStack() as sg2:
                cx.barrier(BAR)
                kall = sg2.enter_context(nc.sbuf_tensor("kallG", [128, 8 * T], BF16))
                qsb = sg2.enter_context(nc.sbuf_tensor("qsbG", [128, 8 * 256], BF16))
                vtb = sg2.enter_context(nc.sbuf_tensor("vtbG", [128, 2 * 2048], BF16))
                exb = sg2.enter_context(nc.sbuf_tensor("exbG", [128, 2 * 4 * 256], BF16))
                acc = sg2.enter_context(nc.sbuf_tensor("accG", [128, 4 * 256], F32))
                ysb = sg2.enter_context(nc.sbuf_tensor("ysbG", [128, 8 * 256], BF16))
                sums1 = sg2.enter_context(nc.sbuf_tensor("sums1G", [1, 256], F32))
                cx.wait(nc.sync, "sc")
                with (nc.psum_tensor("py0G", [128, 512], F32) as py0,
                      nc.psum_tensor("py1G", [128, 512], F32) as py1,
                      nc.psum_tensor("py2G", [128, 512], F32) as py2,
                      nc.psum_tensor("py3G", [128, 512], F32) as py3,
                      nc.psum_tensor("ps0G", [128, 512], F32) as ps0,
                      nc.psum_tensor("ps1G", [128, 512], F32) as ps1,
                      nc.psum_tensor("ps2G", [128, 512], F32) as ps2,
                      nc.psum_tensor("ps3G", [128, 512], F32) as ps3):
                    pys = [py0, py1, py2, py3]
                    pss = [ps0, ps1, ps2, ps3]
                    for hhalf in range(2):
                        hbase = hhalf * 8
                        cx.barrier(BAR)
                        for h_ in range(8):
                            d = nc.sync.dma_start_transpose(
                                kall[:, h_ * T:(h_ + 1) * T],
                                ag1_out.ap()[:, (hbase + h_) * 128:(hbase + h_ + 1) * 128])
                            kv_ = cx.inc(d, "sw", 16)
                        cx.wait(nc.tensor, "sw", kv_)
                        for qt in range(8):
                            cx.wait(nc.sync, "st")
                            d = nc.sync.dma_start(out=sb3(qsb[:], 8, 256),
                                                  in_=feat_tile(q_d, hbase * 128, 8 * 128, qt * 256, 256))
                            wv = cx.inc(d, "sw", 16)
                            cx.wait(nc.tensor, "sw", wv)
                            kbs = [(0, u) for u in range(2 * qt + 2)] + [(1, u) for u in range(2 * qt + 2)]
                            nkb = len(kbs)
                            for hg in range(2):
                                evs = {}
                                wvs = {}
                                slots_ = {}

                                def emit_ylat1(kb):
                                    cx.wait(nc.tensor, "sv", evs[kb])
                                    cx.wait(nc.tensor, "sw", wvs[kb])
                                    es = kb % 2
                                    vs = kb % 2
                                    for hh in range(4):
                                        h = hbase + hg * 4 + hh
                                        mmy = nc.tensor.matmul(
                                            pys[hh][:, 0:256],
                                            vtb[:, vs * 2048 + h * 128: vs * 2048 + h * 128 + 128],
                                            exb[:, es * 1024 + hh * 256: es * 1024 + hh * 256 + 256],
                                            start=(kb == 0), stop=(kb == nkb - 1))
                                    cx.inc(mmy, "st")

                                for kb, (half, u) in enumerate(kbs):
                                    slot = 16 * half + u
                                    slots_[kb] = slot
                                    vslot = kb % 2
                                    eslot = kb % 2
                                    cx.wait(nc.sync, "st")
                                    d = nc.sync.dma_start(out=vtb[:, vslot * 2048:(vslot + 1) * 2048],
                                                          in_=ag1_out.ap()[slot * 128:(slot + 1) * 128, C:2 * C])
                                    wvs[kb] = cx.inc(d, "sw", 16)
                                    if kb >= 1:
                                        emit_ylat1(kb - 1)
                                    tvs = []
                                    for hh in range(4):
                                        h_ = hg * 4 + hh
                                        mm = nc.tensor.matmul(
                                            pss[hh][:, 0:256],
                                            kall[:, h_ * T + slot * 128: h_ * T + slot * 128 + 128],
                                            qsb[:, h_ * 256:(h_ + 1) * 256],
                                            start=True, stop=True)
                                        tvs.append(cx.inc(mm, "st"))
                                    ev = None
                                    for hh in range(4):
                                        cx.wait(nc.scalar, "st", tvs[hh])
                                        ex = exb[:, eslot * 1024 + hh * 256: eslot * 1024 + hh * 256 + 256]
                                        a = nc.scalar.activation(out=ex, in_=pss[hh][:, 0:256], func=AF.Exp)
                                        av = cx.inc(a, "sa")
                                        cx.wait(nc.vector, "sa", av)
                                        off = (2 * half + (u - 2 * qt)) if u >= 2 * qt else -1
                                        if off >= 0:
                                            v = nc.vector.tensor_tensor(
                                                out=ex, in0=ex,
                                                in1=m1_sb[:, off * 256:(off + 1) * 256], op=ALU.mult)
                                            cx.inc(v, "sv")
                                        if kb == 0:
                                            v = nc.vector.tensor_copy(out=acc[:, hh * 256:(hh + 1) * 256], in_=ex)
                                        else:
                                            v = nc.vector.tensor_tensor(out=acc[:, hh * 256:(hh + 1) * 256],
                                                                        in0=acc[:, hh * 256:(hh + 1) * 256],
                                                                        in1=ex, op=ALU.add)
                                        ev = cx.inc(v, "sv")
                                    evs[kb] = ev
                                emit_ylat1(nkb - 1)
                                for hh in range(4):
                                    h_ = hg * 4 + hh
                                    mm = nc.tensor.matmul(ps0[0:1, 0:256], K.ones_col_f[:, 0:1],
                                                          acc[:, hh * 256:(hh + 1) * 256],
                                                          start=True, stop=True)
                                    tv = cx.inc(mm, "st")
                                    cx.wait(nc.scalar, "st", tv)
                                    a = nc.scalar.activation(out=sums1[0:1, 0:256], in_=ps0[0:1, 0:256], func=AF.Ln)
                                    cx.wait(nc.scalar, "sa", cx.inc(a, "sa"))
                                    a = nc.scalar.activation(out=sums1[0:1, 0:256], in_=sums1[0:1, 0:256],
                                                             func=AF.Exp, scale=-1.0)
                                    rv = cx.inc(a, "sa")
                                    cx.wait(nc.tensor, "sa", rv)
                                    mm = nc.tensor.matmul(ps0[:, 0:256], K.ones_row[0:1, :],
                                                          sums1[0:1, 0:256], start=True, stop=True)
                                    tv = cx.inc(mm, "st")
                                    cx.wait(nc.vector, "st", tv)
                                    v = nc.vector.tensor_copy(out=K.rb[:, 0:256], in_=ps0[:, 0:256])
                                    rv = cx.inc(v, "sv")
                                    cx.wait(nc.vector, "sv", rv)
                                    v = nc.vector.tensor_tensor(out=ysb[:, h_ * 256:(h_ + 1) * 256],
                                                                in0=pys[hh][:, 0:256],
                                                                in1=K.rb[:, 0:256], op=ALU.mult)
                                    yv = cx.inc(v, "sv")
                                    cx.wait(nc.tensor, "sv", yv)
                            cx.wait(nc.sync, "sv")
                            d = nc.sync.dma_start(
                                out=y_d[hbase * 128:(hbase + 8) * 128, qt * 256:(qt + 1) * 256]
                                    .rearrange("(h p) t -> p h t", p=128),
                                in_=sb3(ysb[:], 8, 256))
                            sv_ = cx.inc(d, "ss", 16)
                            cx.wait(nc.vector, "ss", sv_)
            with ExitStack() as sg3:
                cx.barrier(BAR)
                wo = sg3.enter_context(nc.sbuf_tensor("woG", [128, 16 * 2048], BF16))
                ytl = sg3.enter_context(nc.sbuf_tensor("ytlG", [128, 16 * 256], BF16))
                att = sg3.enter_context(nc.sbuf_tensor("attG", [128, 16 * 256], BF16))
                xsqg = sg3.enter_context(nc.sbuf_tensor("xsqG", [128, 16 * 256], BF16))
                xtg = sg3.enter_context(nc.sbuf_tensor("xtG", [128, 16 * 256], F32))
                xbg = sg3.enter_context(nc.sbuf_tensor("xbG", [128, 16 * 256], BF16))
                d = nc.gpsimd.dma_start(out=sb3(wo[:], 16, 2048),
                                        in_=wout.rearrange("(cc p) o -> p cc o", p=128))
                cx.inc(d, "sg", 16)
                cx.wait(nc.tensor, "sg")
                with (nc.psum_tensor("poG", [128, 512], F32) as po,
                      nc.psum_tensor("poGb", [128, 512], F32) as pob,
                      nc.psum_tensor("psmG", [128, 512], F32) as psm,
                      nc.psum_tensor("psqG", [128, 512], F32) as psq,
                      nc.psum_tensor("pbcG", [128, 512], F32) as pbc):
                    osavs = {}
                    for qt in range(8):
                        cx.wait(nc.sync, "st")
                        cx.wait(nc.sync, "sa")
                        cx.wait(nc.sync, "sv")
                        d = nc.sync.dma_start(out=sb3(ytl[:], 16, 256),
                                              in_=feat_tile(y_d, 0, C, qt * 256, 256))
                        wv = cx.inc(d, "sw", 16)
                        cx.wait(nc.tensor, "sw", wv)
                        for oc in range(16):
                            g = qt * 16 + oc
                            if g >= 2:
                                cx.wait(nc.tensor, "sa", osavs[g - 2])
                            for ccc in range(16):
                                mm = nc.tensor.matmul(
                                    [po, pob][g % 2][:, 0:256],
                                    wo[:, ccc * 2048 + oc * 128: ccc * 2048 + oc * 128 + 128],
                                    ytl[:, ccc * 256:(ccc + 1) * 256],
                                    start=(ccc == 0), stop=(ccc == 15))
                            tv = cx.inc(mm, "st")
                            cx.wait(nc.scalar, "st", tv)
                            a = nc.scalar.activation(out=att[:, oc * 256:(oc + 1) * 256],
                                                     in_=[po, pob][g % 2][:, 0:256], func=AF.Copy)
                            osavs[g] = cx.inc(a, "sa")
                        emit_combine(lambda ccc, _t: att[:, ccc * 256:(ccc + 1) * 256],
                                     None, x_d, x_d, lnmy_d, xtg, xbg, xsqg, psm, psq, pbc,
                                     256, 1, base_t=qt * 256)

        with ExitStack() as sH:
            cx.barrier(BAR)
            b1s = sH.enter_context(nc.sbuf_tensor("b1sH", [128, 32], F32))
            b2s = sH.enter_context(nc.sbuf_tensor("b2sH", [128, 32], F32))
            b3s = sH.enter_context(nc.sbuf_tensor("b3sH", [128, 16], F32))
            for ap_, t_ in ((b1, b1s), (b2, b2s), (b3, b3s)):
                d = nc.sync.dma_start(out=t_[:], in_=ap_)
                cx.inc(d, "sw", 16)
            h1 = sH.enter_context(nc.sbuf_tensor("h1H", [128, 32 * 1024], BF16))
            h2 = sH.enter_context(nc.sbuf_tensor("h2H", [128, 32 * 1024], BF16))
            scratch = sH.enter_context(nc.sbuf_tensor("scrH", [128, 16384], BF16))
            xtf = scratch[:, 0:8192].bitcast(F32)
            xbf = scratch[:, 8192:12288]
            xsqf = scratch[:, 12288:16384]
            with (nc.psum_tensor("pH0", [128, 512], F32) as pf0,
                  nc.psum_tensor("pH1", [128, 512], F32) as pf1,
                  nc.psum_tensor("pH2", [128, 512], F32) as pf2,
                  nc.psum_tensor("pH3", [128, 512], F32) as pf3,
                  nc.psum_tensor("psmH", [128, 512], F32) as psm,
                  nc.psum_tensor("psqH", [128, 512], F32) as psq,
                  nc.psum_tensor("pbcH", [128, 512], F32) as pbc):
                pfs = [pf0, pf1]
                pfs2 = [pf2, pf3]
                for th in range(2):
                    with ExitStack() as shA:
                        lnx7 = h2[:, 0:16 * 1024]
                        wbuf = shA.enter_context(nc.sbuf_tensor(f"wbufH{th}", [128, 16 * 512], BF16))
                        cx.barrier(BAR)
                        d = nc.sync.dma_start(out=sb3(lnx7, 16, 1024),
                                              in_=feat_tile(lnmy_d, 0, C, th * 1024, 1024))
                        wv = cx.inc(d, "sw", 16)
                        cx.wait(nc.tensor, "sw", wv)
                        wgv = {}
                        wmv = {}

                        def w1_load(jg):
                            sl = jg % 2
                            d = nc.gpsimd.dma_start(out=sb3(wbuf[:, sl * 4096:(sl + 1) * 4096], 16, 256),
                                                    in_=w1t[:, jg * 256:(jg + 1) * 256]
                                                    .rearrange("(cc p) j -> p cc j", p=128))
                            wgv[jg] = cx.inc(d, "sg", 16)

                        w1_load(0)
                        f1sav = {}
                        for jg in range(16):
                            sl = jg % 2
                            if jg + 1 < 16:
                                if jg >= 1:
                                    cx.wait(nc.gpsimd, "st", wmv[jg - 1])
                                w1_load(jg + 1)
                            cx.wait(nc.tensor, "sg", wgv[jg])
                            for jb in range(2):
                                jc = jg * 2 + jb
                                pp = pfs if jc % 2 == 0 else pfs2
                                if jc >= 2:
                                    cx.wait(nc.tensor, "sa", f1sav[jc - 2])
                                for half in range(2):
                                    for ccc in range(16):
                                        mm = nc.tensor.matmul(
                                            pp[half][:, 0:512],
                                            wbuf[:, sl * 4096 + ccc * 256 + jb * 128: sl * 4096 + ccc * 256 + jb * 128 + 128],
                                            lnx7[:, ccc * 1024 + half * 512: ccc * 1024 + half * 512 + 512],
                                            start=(ccc == 0), stop=(ccc == 15))
                                tv = cx.inc(mm, "st")
                                wmv[jg] = tv
                                cx.wait(nc.scalar, "st", tv)
                                for half in range(2):
                                    a = nc.scalar.activation(
                                        out=h1[:, jc * 1024 + half * 512: jc * 1024 + half * 512 + 512],
                                        in_=pp[half][:, 0:512], func=AF.Relu, bias=b1s[:, jc:jc + 1])
                                f1sav[jc] = cx.inc(a, "sa")
                    with ExitStack() as shB:
                        w2buf = shB.enter_context(nc.sbuf_tensor(f"w2bufH{th}", [128, 32 * 256], BF16))
                        cx.barrier(BAR)
                        wgv = {}
                        wmv = {}

                        def w2_load(jg):
                            sl = jg % 2
                            d = nc.gpsimd.dma_start(out=sb3(w2buf[:, sl * 4096:(sl + 1) * 4096], 32, 128),
                                                    in_=w2t[:, jg * 128:(jg + 1) * 128]
                                                    .rearrange("(cc p) j -> p cc j", p=128))
                            wgv[jg] = cx.inc(d, "sg", 16)

                        w2_load(0)
                        f2sav = {}
                        for jg in range(32):
                            sl = jg % 2
                            if jg + 1 < 32:
                                if jg >= 1:
                                    cx.wait(nc.gpsimd, "st", wmv[jg - 1])
                                w2_load(jg + 1)
                            cx.wait(nc.tensor, "sg", wgv[jg])
                            for jb in range(1):
                                jc = jg
                                pp = pfs if jc % 2 == 0 else pfs2
                                if jc >= 2:
                                    cx.wait(nc.tensor, "sa", f2sav[jc - 2])
                                for half in range(2):
                                    for ccc in range(32):
                                        mm = nc.tensor.matmul(
                                            pp[half][:, 0:512],
                                            w2buf[:, sl * 4096 + ccc * 128: sl * 4096 + ccc * 128 + 128],
                                            h1[:, ccc * 1024 + half * 512: ccc * 1024 + half * 512 + 512],
                                            start=(ccc == 0), stop=(ccc == 31))
                                tv = cx.inc(mm, "st")
                                wmv[jg] = tv
                                cx.wait(nc.scalar, "st", tv)
                                for half in range(2):
                                    a = nc.scalar.activation(
                                        out=h2[:, jc * 1024 + half * 512: jc * 1024 + half * 512 + 512],
                                        in_=pp[half][:, 0:512], func=AF.Relu, bias=b2s[:, jc:jc + 1])
                                f2sav[jc] = cx.inc(a, "sa")
                    with ExitStack() as shC:
                        w3buf = shC.enter_context(nc.sbuf_tensor(f"w3bufH{th}", [128, 32 * 256], BF16))
                        ffno = h1[:, 0:16 * 1024]
                        cx.barrier(BAR)
                        wgv = {}
                        wmv = {}

                        def w3_load(cg):
                            sl = cg % 2
                            d = nc.gpsimd.dma_start(out=sb3(w3buf[:, sl * 4096:(sl + 1) * 4096], 32, 128),
                                                    in_=w3t[:, cg * 128:(cg + 1) * 128]
                                                    .rearrange("(cc p) j -> p cc j", p=128))
                            wgv[cg] = cx.inc(d, "sg", 16)

                        w3_load(0)
                        f3sav = {}
                        for cg in range(16):
                            sl = cg % 2
                            if cg + 1 < 16:
                                if cg >= 1:
                                    cx.wait(nc.gpsimd, "st", wmv[cg - 1])
                                w3_load(cg + 1)
                            cx.wait(nc.tensor, "sg", wgv[cg])
                            for cb in range(1):
                                ccx = cg
                                pp = pfs if ccx % 2 == 0 else pfs2
                                if ccx >= 2:
                                    cx.wait(nc.tensor, "sa", f3sav[ccx - 2])
                                for half in range(2):
                                    for ccc in range(32):
                                        mm = nc.tensor.matmul(
                                            pp[half][:, 0:512],
                                            w3buf[:, sl * 4096 + ccc * 128: sl * 4096 + ccc * 128 + 128],
                                            h2[:, ccc * 1024 + half * 512: ccc * 1024 + half * 512 + 512],
                                            start=(ccc == 0), stop=(ccc == 31))
                                tv = cx.inc(mm, "st")
                                wmv[cg] = tv
                                cx.wait(nc.scalar, "st", tv)
                                for half in range(2):
                                    a = nc.scalar.activation(
                                        out=ffno[:, ccx * 1024 + half * 512: ccx * 1024 + half * 512 + 512],
                                        in_=pp[half][:, 0:512], func=AF.Identity, bias=b3s[:, ccx:ccx + 1])
                                f3sav[ccx] = cx.inc(a, "sa")
                            cx.barrier(BAR)
                        cx.barrier(BAR)
                        xt5 = h2[:, 0:16384].bitcast(F32)
                        xb5 = h2[:, 16384:24576]
                        xsq5 = h2[:, 24576:32768]
                        emit_combine(lambda ccc, tt: ffno[:, ccc * 1024 + tt * 512: ccc * 1024 + tt * 512 + 512],
                                     None, x_d, xout, None, xt5, xb5, xsq5, psm, psq, pbc,
                                     512, 2, base_t=th * 1024)
        cx.barrier(BAR)
        cx.wait(nc.sync, "ss")
    return nc



STORE_PERM = np.concatenate([np.arange(0, NBLK, 2), np.arange(1, NBLK, 2)])


def tok_perm():
    idx = np.arange(T).reshape(NBLK, 128)
    return idx[STORE_PERM].reshape(-1)


MASK_CLASS_ORDER = (0, 2, 1, 3)


def make_masks_l0(parity):
    out = np.zeros((4, 128, 512), np.float32)
    for o in range(4):
        m = np.zeros((128, 256), np.float32)
        for half, goff in ((0, 0), (1, 2)):
            gc = goff + parity
            for col in range(128):
                cc = half * 128 + col
                if o < gc:
                    m[:, cc] = 1.0
                elif o == gc:
                    m[: col + 1, cc] = 1.0
        out[o, :, 0:256] = m
        out[o, :, 256:512] = m
    return np.ascontiguousarray(out[list(MASK_CLASS_ORDER)].reshape(4 * 128, 512))


def make_masks_l1(parity):
    out = np.zeros((4, 128, 256), np.float32)
    for o in range(4):
        for half, goff in ((0, 0), (1, 2)):
            gc = goff + parity
            for col in range(128):
                cc = half * 128 + col
                if o < gc:
                    out[o, :, cc] = 1.0
                elif o == gc:
                    out[o, : col + 1, cc] = 1.0
    return np.ascontiguousarray(out[list(MASK_CLASS_ORDER)].reshape(4 * 128, 256))


def _f(x):
    return np.ascontiguousarray(np.asarray(x, dtype=np.float32))


_PROGRAMS = {}


def _programs():
    if not _PROGRAMS:
        _PROGRAMS["fused"] = build_fused()
    return _PROGRAMS


def kernel(**inputs):
    inp = {k: np.ascontiguousarray(np.asarray(v, dtype=np.float32)) for k, v in inputs.items()}
    perm = tok_perm()
    nc = _programs()["fused"]
    masks0 = [make_masks_l0(p) for p in (0, 1)]
    masks1 = [make_masks_l1(p) for p in (0, 1)]
    ident = np.eye(128, dtype=np.float32)
    x = inp["L0_attn_x"].reshape(B, T, C)
    in_maps = []
    for core in range(8):
        b, p = core // 2, core % 2
        xf = np.ascontiguousarray(x[b].T[:, perm])
        in_maps.append({
            "xmy": np.ascontiguousarray(xf[:, p * MYT:(p + 1) * MYT]),
            "wq_t": np.ascontiguousarray(inp["L0_attn_Wq_w"].T),
            "wq_n": inp["L0_attn_Wq_w"],
            "wdkv_t": np.ascontiguousarray(inp["L0_attn_Wdkv_w"].T),
            "wuk": inp["L0_attn_Wuk_w"], "wuv": inp["L0_attn_Wuv_w"],
            "wo_t": np.ascontiguousarray(inp["L0_attn_Wo_w"].T),
            "masks0": masks0[p], "ident": ident,
            "up_t": np.ascontiguousarray(inp["L0_ffn_up_weight"].T),
            "down_b": np.ascontiguousarray(
                inp["L0_ffn_down_weight"].reshape(16, 128, 64, 128).transpose(0, 2, 3, 1)),
            "usc": np.ascontiguousarray(inp["L0_ffn_up_scale"].reshape(64, 128).T),
            "dsc": np.ascontiguousarray(inp["L0_ffn_down_scale"].reshape(16, 128).T),
            "osc": inp["L0_ffn_output_scale"].reshape(1),
            "wqkv": inp["L1_attn_w_qkv"], "wout": inp["L1_attn_w_out"],
            "masks1": masks1[p],
            "w1t": np.ascontiguousarray(inp["L1_ffn_W1"].T),
            "w2t": np.ascontiguousarray(inp["L1_ffn_W2"].T),
            "w3t": np.ascontiguousarray(inp["L1_ffn_W3"].T),
            "b1": np.ascontiguousarray(inp["L1_ffn_b1"].reshape(32, 128).T),
            "b2": np.ascontiguousarray(inp["L1_ffn_b2"].reshape(32, 128).T),
            "b3": np.ascontiguousarray(inp["L1_ffn_b3"].reshape(16, 128).T),
        })
    res = run_bass_kernel_spmd(nc, in_maps, core_ids=list(range(8)))
    out = np.zeros((B, T, C), np.float32)
    for core in range(8):
        b, p = core // 2, core % 2
        gtok = perm[p * MYT:(p + 1) * MYT]
        out[b, gtok, :] = res.results[core]["xout"].T
    return out


def _kernel_unfused(**inputs):
    inp = {k: np.ascontiguousarray(np.asarray(v, dtype=np.float32)) for k, v in inputs.items()}
    perm = tok_perm()
    progs = {"a0": build_la0(), "f0": build_lf0(), "a1": build_la1(), "f1": build_lf1()}
    masks0 = [make_masks_l0(p) for p in (0, 1)]
    masks1 = [make_masks_l1(p) for p in (0, 1)]
    ident = np.eye(128, dtype=np.float32)

    def shards(xbtc):
        out = []
        for core in range(8):
            b, p = core // 2, core % 2
            xf = np.ascontiguousarray(xbtc[b].T[:, perm])
            out.append((xf, np.ascontiguousarray(xf[:, p * MYT:(p + 1) * MYT])))
        return out

    def gather(res):
        x = np.zeros((B, T, C), np.float32)
        for core in range(8):
            b, p = core // 2, core % 2
            gtok = perm[p * MYT:(p + 1) * MYT]
            x[b, gtok, :] = res.results[core]["xout"].T
        return x

    sh = shards(inp["L0_attn_x"])
    wq_t = np.ascontiguousarray(inp["L0_attn_Wq_w"].T)
    in_maps = [{"xfull": xf, "xmy": xm, "wq_t": wq_t,
                "wq_n": inp["L0_attn_Wq_w"],
                "wdkv_t": np.ascontiguousarray(inp["L0_attn_Wdkv_w"].T),
                "wuk": inp["L0_attn_Wuk_w"], "wuv": inp["L0_attn_Wuv_w"],
                "wo_t": np.ascontiguousarray(inp["L0_attn_Wo_w"].T),
                "masks": masks0[c % 2], "ident": ident}
               for c, (xf, xm) in enumerate(sh)]
    x1 = gather(run_bass_kernel_spmd(progs["a0"], in_maps, core_ids=list(range(8))))

    sh = shards(x1)
    in_maps = [{"xmy": xm,
                "up_t": np.ascontiguousarray(inp["L0_ffn_up_weight"].T),
                "down_t": np.ascontiguousarray(inp["L0_ffn_down_weight"].T),
                "usc": np.ascontiguousarray(inp["L0_ffn_up_scale"].reshape(64, 128).T),
                "dsc": np.ascontiguousarray(inp["L0_ffn_down_scale"].reshape(16, 128).T),
                "osc": inp["L0_ffn_output_scale"].reshape(1)}
               for c, (xf, xm) in enumerate(sh)]
    x2 = gather(run_bass_kernel_spmd(progs["f0"], in_maps, core_ids=list(range(8))))

    sh = shards(x2)
    in_maps = [{"xfull": xf, "xmy": xm,
                "wqkv": inp["L1_attn_w_qkv"], "wout": inp["L1_attn_w_out"],
                "masks": masks1[c % 2]}
               for c, (xf, xm) in enumerate(sh)]
    x3 = gather(run_bass_kernel_spmd(progs["a1"], in_maps, core_ids=list(range(8))))

    sh = shards(x3)
    in_maps = [{"xmy": xm,
                "w1t": np.ascontiguousarray(inp["L1_ffn_W1"].T),
                "w2t": np.ascontiguousarray(inp["L1_ffn_W2"].T),
                "w3t": np.ascontiguousarray(inp["L1_ffn_W3"].T),
                "b1": np.ascontiguousarray(inp["L1_ffn_b1"].reshape(32, 128).T),
                "b2": np.ascontiguousarray(inp["L1_ffn_b2"].reshape(32, 128).T),
                "b3": np.ascontiguousarray(inp["L1_ffn_b3"].reshape(16, 128).T)}
               for c, (xf, xm) in enumerate(sh)]
    x4 = gather(run_bass_kernel_spmd(progs["f1"], in_maps, core_ids=list(range(8))))
    return x4


# version 05 (final)
# speedup vs baseline: 1.7040x; 1.0337x over previous
import numpy as np
import concourse.bass as bass
import concourse.mybir as mybir
from concourse.bass_utils import run_bass_kernel_spmd
from contextlib import ExitStack

F32 = mybir.dt.float32
BF16 = mybir.dt.bfloat16
AF = mybir.ActivationFunctionType
ALU = mybir.AluOpType

B, T, C = 4, 4096, 2048
NH, HS, NL = 16, 128, 512
NBLK = T // 128
MYT = 2048
EPS = 1e-5
ISQ = float(1.0 / np.sqrt(128.0))


def sslot(j):
    return (j % 2) * 16 + j // 2


class Ctx:
    def __init__(self, nc, stack, names):
        self.nc = nc
        self.cnt = {}
        for nm in names:
            s = stack.enter_context(nc.semaphore(nm))
            self.cnt[nm] = [s, 0]

    def inc(self, instr, name, n=1):
        s, v = self.cnt[name]
        instr.then_inc(s, n)
        self.cnt[name][1] = v + n
        return self.cnt[name][1]

    def val(self, name):
        return self.cnt[name][1]

    def wait(self, eng, name, val=None):
        s, v = self.cnt[name]
        eng.wait_ge(s, v if val is None else val)

    def barrier(self, engines):
        for eng in engines:
            for nm in self.cnt:
                s, v = self.cnt[nm]
                if v > 0:
                    eng.wait_ge(s, v)


def _col(ap):
    return ap


def feat_tile(dram_ap, c0, cn, t0, tn):
    return dram_ap[c0:c0 + cn, t0:t0 + tn].rearrange("(cc p) t -> p cc t", p=128)


def sb3(sb_ap, cc, t):
    return sb_ap.rearrange("p (cc t) -> p cc t", cc=cc)


def emit_ln_combine(nc, cx, qw, n_cc, get_y, get_x_dma, get_out_dma, bufs,
                    residual=True, out_lnx=None, y_is_bf16=True):
    raise NotImplementedError


def build_la0(debug=False, stop_after=None):
    nc = bass.Bass("TRN2", target_bir_lowering=False)
    xin = nc.dram_tensor("xfull", [C, T], F32, kind="ExternalInput").ap()
    xmy = nc.dram_tensor("xmy", [C, MYT], F32, kind="ExternalInput").ap()
    wq_t = nc.dram_tensor("wq_t", [C, C], F32, kind="ExternalInput").ap()
    wq_n = nc.dram_tensor("wq_n", [C, C], F32, kind="ExternalInput").ap()
    wdkv_t = nc.dram_tensor("wdkv_t", [C, NL], F32, kind="ExternalInput").ap()
    wuk = nc.dram_tensor("wuk", [C, NL], F32, kind="ExternalInput").ap()
    wuv = nc.dram_tensor("wuv", [C, NL], F32, kind="ExternalInput").ap()
    wo_t = nc.dram_tensor("wo_t", [C, C], F32, kind="ExternalInput").ap()
    masks = nc.dram_tensor("masks", [4 * 128, 512], F32, kind="ExternalInput").ap()
    ident = nc.dram_tensor("ident", [128, 128], F32, kind="ExternalInput").ap()
    xout = nc.dram_tensor("xout", [C, MYT], F32, kind="ExternalOutput").ap()
    kind = dict(kind="ExternalOutput") if debug else {}
    xdum = nc.dram_tensor("xdum", [C, MYT], F32)
    lnx_d = nc.dram_tensor("lnx_d", [C, T], BF16, **kind)
    lnmy_d = nc.dram_tensor("lnmy_d", [C, MYT], BF16, **kind)
    qlat_d = nc.dram_tensor("qlat_d", [128, 8 * 4 * NH * 256], BF16, **kind)
    y_d = nc.dram_tensor("y_d", [C, MYT], BF16, **kind)
    if debug:
        lnx_d, lnmy_d, qlat_d, y_d = lnx_d.ap(), lnmy_d.ap(), qlat_d.ap(), y_d.ap()
    else:
        lnx_d, lnmy_d, qlat_d, y_d = lnx_d.ap(), lnmy_d.ap(), qlat_d.ap(), y_d.ap()

    st = ExitStack()
    with st:
        cx = Ctx(nc, st, ("sw", "sg", "st", "sv", "sa", "ss"))
        sbb = lambda name, shape, dt=BF16: st.enter_context(nc.sbuf_tensor(name, shape, dt))

        ones_col = sbb("ones_col", [128, 1])
        ones_row = st.enter_context(nc.sbuf_tensor("ones_row", [1, 128], F32))
        eps_sb = st.enter_context(nc.sbuf_tensor("eps_sb", [1, 1], F32))
        mask_sb = sbb("mask_sb", [128, 4 * 512])
        id_sb = sbb("id_sb", [128, 128])
        veff = sbb("veff", [128, 4 * NH * 128])
        qh = sbb("qh", [128, 256])
        expb = sbb("expb", [128, 2 * 512])
        ylat_sb = sbb("ylat_sb", [128, 4 * 512])
        yqt = sbb("yqt", [128, NH * 256])
        acc = st.enter_context(nc.sbuf_tensor("acc", [128, 512], F32))
        ones_col_f = st.enter_context(nc.sbuf_tensor("ones_col_f", [128, 1], F32))
        sums = st.enter_context(nc.sbuf_tensor("sums", [1, 512], F32))
        rb = st.enter_context(nc.sbuf_tensor("rb", [128, 512], F32))
        mrb = st.enter_context(nc.sbuf_tensor("mrb", [128, 512], F32))
        rstd = st.enter_context(nc.sbuf_tensor("rstd", [1, 512], F32))
        mrs = st.enter_context(nc.sbuf_tensor("mrs", [1, 512], F32))
        scr = st.enter_context(nc.sbuf_tensor("scr", [1, 2048], F32))
        f32a = st.enter_context(nc.sbuf_tensor("f32a", [128, 512], F32))
        f32b = st.enter_context(nc.sbuf_tensor("f32b", [128, 512], F32))

        nc.gpsimd.memset(ones_col[:], 1.0)
        nc.gpsimd.memset(ones_col_f[:], 1.0)
        nc.gpsimd.memset(ones_row[:], 1.0)
        g = nc.gpsimd.memset(eps_sb[:], EPS)
        cx.inc(g, "sv")
        d = nc.gpsimd.dma_start(out=sb3(mask_sb[:], 4, 512), in_=masks.rearrange("(m p) j -> p m j", p=128))
        cx.inc(d, "sg", 16)
        d = nc.gpsimd.dma_start(out=id_sb[:], in_=ident)
        cx.inc(d, "sg", 16)

        def ln_small(jw):
            mu = scr[0:1, 0:jw]
            ex2 = scr[0:1, 512:512 + jw]
            tmp = scr[0:1, 1024:1024 + jw]
            nc.vector.tensor_tensor(out=tmp, in0=mu, in1=mu, op=ALU.mult)
            v = nc.vector.tensor_tensor(out=ex2, in0=ex2, in1=tmp, op=ALU.subtract)
            cx.inc(v, "sv")
            cx.wait(nc.scalar, "sv")
            a = nc.scalar.activation(out=ex2, in_=ex2, func=AF.Ln, bias=eps_sb[0:1, 0:1])
            a = nc.scalar.activation(out=rstd[0:1, 0:jw], in_=ex2, func=AF.Exp, scale=-0.5)
            cx.inc(a, "sa")
            cx.wait(nc.vector, "sa")
            v = nc.vector.tensor_tensor(out=mrs[0:1, 0:jw], in0=mu, in1=rstd[0:1, 0:jw], op=ALU.mult)
            return cx.inc(v, "sv")

        def bcast(vec_ap, dst_sb, jw, pb):
            cx.wait(nc.tensor, "sv")
            mm = nc.tensor.matmul(pb[:, 0:jw], ones_row[0:1, :], vec_ap, start=True, stop=True)
            tv = cx.inc(mm, "st")
            cx.wait(nc.vector, "st", tv)
            v = nc.vector.tensor_copy(out=dst_sb[:, 0:jw], in_=pb[:, 0:jw])
            return cx.inc(v, "sv")

        with ExitStack() as sA:
            keff = sA.enter_context(nc.sbuf_tensor("keff", [128, NH * NL], BF16))
            wbig = sA.enter_context(nc.sbuf_tensor("wbig", [128, 16 * 2048], BF16))
            with ExitStack() as sa:
                wsm = sa.enter_context(nc.sbuf_tensor("wsm", [128, 16 * 512], BF16))
                d = nc.gpsimd.dma_start(out=sb3(wsm[:], 16, 512), in_=wuv.rearrange("(cc p) l -> p cc l", p=128))
                cx.inc(d, "sg", 16)
                d = nc.gpsimd.dma_start(out=sb3(wbig[:], 16, 2048), in_=wo_t.rearrange("(cc p) o -> p cc o", p=128))
                cx.inc(d, "sg", 16)
                cx.wait(nc.tensor, "sg")
                with nc.psum_tensor("pv", [128, 512], F32) as pv:
                    for lc in range(4):
                        for hq in range(4):
                            for ccc in range(16):
                                mm = nc.tensor.matmul(
                                    pv[:, 0:512],
                                    wsm[:, ccc * 512 + lc * 128: ccc * 512 + lc * 128 + 128],
                                    wbig[:, ccc * 2048 + hq * 512: ccc * 2048 + hq * 512 + 512],
                                    start=(ccc == 0), stop=(ccc == 15))
                            tv = cx.inc(mm, "st")
                            cx.wait(nc.scalar, "st", tv)
                            a = nc.scalar.activation(
                                out=veff[:, lc * 2048 + hq * 512: lc * 2048 + hq * 512 + 512],
                                in_=pv[:, 0:512], func=AF.Copy)
                            av = cx.inc(a, "sa")
                            cx.wait(nc.tensor, "sa", av)
                cx.wait(nc.gpsimd, "st")
                d = nc.gpsimd.dma_start(out=sb3(wsm[:], 16, 512), in_=wuk.rearrange("(cc p) l -> p cc l", p=128))
                cx.inc(d, "sg", 16)
                d = nc.gpsimd.dma_start(out=sb3(wbig[:], 16, 2048), in_=wq_n.rearrange("(cc p) o -> p cc o", p=128))
                cx.inc(d, "sg", 16)
                cx.wait(nc.tensor, "sg")
                with nc.psum_tensor("pk", [128, 512], F32) as pk:
                    for h in range(NH):
                        for ccc in range(16):
                            mm = nc.tensor.matmul(
                                pk[:, 0:512],
                                wbig[:, ccc * 2048 + h * 128: ccc * 2048 + h * 128 + 128],
                                wsm[:, ccc * 512: ccc * 512 + 512],
                                start=(ccc == 0), stop=(ccc == 15))
                        tv = cx.inc(mm, "st")
                        cx.wait(nc.scalar, "st", tv)
                        a = nc.scalar.activation(out=keff[:, h * 512:(h + 1) * 512],
                                                 in_=pk[:, 0:512], func=AF.Copy, scale=ISQ)
                        av = cx.inc(a, "sa")
                        cx.wait(nc.tensor, "sa", av)
                cx.wait(nc.gpsimd, "st")
                d = nc.gpsimd.dma_start(out=sb3(wbig[:], 16, 2048), in_=wq_t.rearrange("(cc p) o -> p cc o", p=128))
                cx.inc(d, "sg", 16)
                cx.wait(nc.tensor, "sg")

            cx.barrier((nc.tensor, nc.vector, nc.scalar, nc.sync, nc.gpsimd))
            with ExitStack() as sb_:
                xt = sb_.enter_context(nc.sbuf_tensor("xt", [128, 16 * 512], F32))
                xb = sb_.enter_context(nc.sbuf_tensor("xb", [128, 16 * 512], BF16))
                xsq = sb_.enter_context(nc.sbuf_tensor("xsq", [128, 16 * 512], BF16))
                with (nc.psum_tensor("psm", [128, 512], F32) as psm,
                      nc.psum_tensor("psq", [128, 512], F32) as psq,
                      nc.psum_tensor("pbc", [128, 512], F32) as pbc):
                    for tt in range(12):
                        if tt < 8:
                            src_ap = feat_tile(xin, 0, C, tt * 512, 512)
                        else:
                            src_ap = feat_tile(xmy, 0, C, (tt - 8) * 512, 512)
                        d = nc.sync.dma_start(out=sb3(xt[:], 16, 512), in_=src_ap)
                        wv = cx.inc(d, "sw", 16)
                        cx.wait(nc.vector, "sw", wv)
                        cx.wait(nc.vector, "st")
                        for ccc in range(16):
                            v = nc.vector.tensor_copy(out=xb[:, ccc * 512:(ccc + 1) * 512],
                                                      in_=xt[:, ccc * 512:(ccc + 1) * 512])
                        vv = cx.inc(v, "sv")
                        cx.wait(nc.scalar, "sv", vv)
                        for ccc in range(16):
                            a = nc.scalar.activation(out=xsq[:, ccc * 512:(ccc + 1) * 512],
                                                     in_=xb[:, ccc * 512:(ccc + 1) * 512], func=AF.Square)
                        av = cx.inc(a, "sa")
                        cx.wait(nc.tensor, "sa", av)
                        cx.wait(nc.tensor, "sv", vv)
                        for ccc in range(16):
                            nc.tensor.matmul(psm[0:1, 0:512], ones_col[:, 0:1],
                                             xb[:, ccc * 512:(ccc + 1) * 512],
                                             start=(ccc == 0), stop=(ccc == 15))
                        for ccc in range(16):
                            mm = nc.tensor.matmul(psq[0:1, 0:512], ones_col[:, 0:1],
                                                  xsq[:, ccc * 512:(ccc + 1) * 512],
                                                  start=(ccc == 0), stop=(ccc == 15))
                        tv = cx.inc(mm, "st")
                        cx.wait(nc.vector, "st", tv)
                        nc.vector.tensor_scalar(out=scr[0:1, 0:512], in0=psm[0:1, 0:512],
                                                scalar1=1.0 / 2048, scalar2=None, op0=ALU.mult)
                        nc.vector.tensor_scalar(out=scr[0:1, 512:1024], in0=psq[0:1, 0:512],
                                                scalar1=1.0 / 2048, scalar2=None, op0=ALU.mult)
                        ln_small(512)
                        bcast(rstd[0:1, 0:512], rb, 512, pbc)
                        bv = bcast(mrs[0:1, 0:512], mrb, 512, pbc)
                        cx.wait(nc.vector, "sv", bv)
                        for ccc in range(16):
                            nc.vector.tensor_tensor(out=f32a[:], in0=xt[:, ccc * 512:(ccc + 1) * 512],
                                                    in1=rb[:], op=ALU.mult)
                            v = nc.vector.tensor_tensor(out=xb[:, ccc * 512:(ccc + 1) * 512],
                                                        in0=f32a[:], in1=mrb[:], op=ALU.subtract)
                        lv = cx.inc(v, "sv")
                        cx.wait(nc.sync, "sv", lv)
                        if tt < 8:
                            dst_ap = feat_tile(lnx_d, 0, C, tt * 512, 512)
                        else:
                            dst_ap = feat_tile(lnmy_d, 0, C, (tt - 8) * 512, 512)
                        d = nc.sync.dma_start(out=dst_ap, in_=sb3(xb[:], 16, 512))
                        sv_ = cx.inc(d, "ss", 16)
                        cx.wait(nc.vector, "ss", sv_)

            cx.barrier((nc.tensor, nc.vector, nc.scalar, nc.sync, nc.gpsimd))
            if stop_after == "P1":
                return nc
            with ExitStack() as sc:
                lnq = sc.enter_context(nc.sbuf_tensor("lnq", [128, 16 * 256], BF16))
                qlat = sc.enter_context(nc.sbuf_tensor("qlat", [128, 4 * NH * 256], BF16))
                with (nc.psum_tensor("pq", [128, 256], F32) as pq,
                      nc.psum_tensor("pql", [128, 256], F32) as pql):
                    for qt in range(8):
                        cx.wait(nc.sync, "st")
                        d = nc.sync.dma_start(out=sb3(lnq[:], 16, 256), in_=feat_tile(lnmy_d, 0, C, qt * 256, 256))
                        wv = cx.inc(d, "sw", 16)
                        cx.wait(nc.tensor, "sw", wv)
                        for h in range(NH):
                            for ccc in range(16):
                                mm = nc.tensor.matmul(
                                    pq[:, 0:256],
                                    wbig[:, ccc * 2048 + h * 128: ccc * 2048 + h * 128 + 128],
                                    lnq[:, ccc * 256:(ccc + 1) * 256],
                                    start=(ccc == 0), stop=(ccc == 15))
                            tv = cx.inc(mm, "st")
                            cx.wait(nc.scalar, "st", tv)
                            a = nc.scalar.activation(out=qh[:], in_=pq[:, 0:256], func=AF.Copy)
                            av = cx.inc(a, "sa")
                            cx.wait(nc.tensor, "sa", av)
                            for lc in range(4):
                                mm = nc.tensor.matmul(
                                    pql[:, 0:256],
                                    keff[:, h * 512 + lc * 128: h * 512 + lc * 128 + 128],
                                    qh[:], start=True, stop=True)
                                tv = cx.inc(mm, "st")
                                cx.wait(nc.scalar, "st", tv)
                                a = nc.scalar.activation(
                                    out=qlat[:, lc * (NH * 256) + h * 256: lc * (NH * 256) + h * 256 + 256],
                                    in_=pql[:, 0:256], func=AF.Copy)
                                av = cx.inc(a, "sa")
                                cx.wait(nc.tensor, "sa", av)
                        cx.wait(nc.sync, "sa")
                        d = nc.sync.dma_start(out=qlat_d[:, qt * (4 * NH * 256):(qt + 1) * (4 * NH * 256)],
                                              in_=qlat[:])
                        sv_ = cx.inc(d, "ss", 16)
                        cx.wait(nc.scalar, "ss", sv_)

        if stop_after == "P3":
            return nc
        with ExitStack() as sB:
            cx.barrier((nc.tensor, nc.vector, nc.scalar, nc.sync, nc.gpsimd))
            ckvl = sB.enter_context(nc.sbuf_tensor("ckvl", [128, 4 * T], BF16))
            ckvt = sB.enter_context(nc.sbuf_tensor("ckvt", [128, NBLK * NL], BF16))
            with ExitStack() as sd:
                wsm = sd.enter_context(nc.sbuf_tensor("wsm2", [128, 16 * 512], BF16))
                xb = sd.enter_context(nc.sbuf_tensor("xb2", [128, 16 * 512], BF16))
                d = nc.gpsimd.dma_start(out=sb3(wsm[:], 16, 512), in_=wdkv_t.rearrange("(cc p) l -> p cc l", p=128))
                cx.inc(d, "sg", 16)
                cx.wait(nc.tensor, "sg")
                with (nc.psum_tensor("pc", [128, 512], F32) as pc,
                      nc.psum_tensor("ptr", [128, 128], BF16) as ptr):
                    for tt in range(8):
                        cx.wait(nc.sync, "st")
                        d = nc.sync.dma_start(out=sb3(xb[:], 16, 512), in_=feat_tile(lnx_d, 0, C, tt * 512, 512))
                        wv = cx.inc(d, "sw", 16)
                        cx.wait(nc.tensor, "sw", wv)
                        for lc in range(4):
                            for ccc in range(16):
                                mm = nc.tensor.matmul(
                                    pc[:, 0:512],
                                    wsm[:, ccc * 512 + lc * 128: ccc * 512 + lc * 128 + 128],
                                    xb[:, ccc * 512:(ccc + 1) * 512],
                                    start=(ccc == 0), stop=(ccc == 15))
                            tv = cx.inc(mm, "st")
                            cx.wait(nc.scalar, "st", tv)
                            a = nc.scalar.activation(out=ckvl[:, lc * T + tt * 512: lc * T + tt * 512 + 512],
                                                     in_=pc[:, 0:512], func=AF.Copy)
                            av = cx.inc(a, "sa")
                            cx.wait(nc.tensor, "sa", av)
                        for tb in range(4):
                            blk = tt * 4 + tb
                            for lc in range(4):
                                mm = nc.tensor.transpose(
                                    ptr[:, 0:128],
                                    ckvl[:, lc * T + blk * 128: lc * T + blk * 128 + 128],
                                    id_sb[:])
                                tv = cx.inc(mm, "st")
                                cx.wait(nc.vector, "st", tv)
                                v = nc.vector.tensor_copy(
                                    out=ckvt[:, blk * 512 + lc * 128: blk * 512 + lc * 128 + 128],
                                    in_=ptr[:, 0:128])
                                vv = cx.inc(v, "sv")
                                cx.wait(nc.tensor, "sv", vv)

            cx.barrier((nc.tensor, nc.vector, nc.scalar, nc.sync, nc.gpsimd))
            if stop_after == "P2":
                return nc
            with ExitStack() as se:
                qlat = se.enter_context(nc.sbuf_tensor("qlat2", [128, 4 * NH * 256], BF16))
                with (nc.psum_tensor("pl0", [128, 512], F32) as pl0,
                      nc.psum_tensor("pl1", [128, 512], F32) as pl1,
                      nc.psum_tensor("pl2", [128, 512], F32) as pl2,
                      nc.psum_tensor("pl3", [128, 512], F32) as pl3,
                      nc.psum_tensor("psc0", [128, 512], F32) as psc0,
                      nc.psum_tensor("psc1", [128, 512], F32) as psc1,
                      nc.psum_tensor("psc2", [128, 512], F32) as psc2,
                      nc.psum_tensor("paux", [128, 512], F32) as paux):
                    pls = [pl0, pl1, pl2, pl3]
                    pscs = [psc0, psc1, psc2]
                    for qt in range(8):
                        cx.wait(nc.sync, "st")
                        d = nc.sync.dma_start(out=qlat[:],
                                              in_=qlat_d[:, qt * (4 * NH * 256):(qt + 1) * (4 * NH * 256)])
                        wv = cx.inc(d, "sw", 16)
                        cx.wait(nc.tensor, "sw", wv)
                        kbs = [(0, u) for u in range(2 * qt + 2)] + [(1, u) for u in range(2 * qt + 2)]
                        nkb = len(kbs)
                        for hp in range(8):
                            h0 = 2 * hp
                            for kb, (half, u) in enumerate(kbs):
                                slot = 16 * half + u
                                psc = pscs[kb % 2]
                                for lc in range(4):
                                    mm = nc.tensor.matmul(
                                        psc[:, 0:512],
                                        ckvl[:, lc * T + slot * 128: lc * T + slot * 128 + 128],
                                        qlat[:, lc * (NH * 256) + h0 * 256: lc * (NH * 256) + h0 * 256 + 512],
                                        start=(lc == 0), stop=(lc == 3))
                                tv = cx.inc(mm, "st")
                                cx.wait(nc.scalar, "st", tv)
                                eslot = kb % 2
                                ex = expb[:, eslot * 512:(eslot + 1) * 512]
                                a = nc.scalar.activation(out=ex, in_=psc[:, 0:512], func=AF.Exp)
                                av = cx.inc(a, "sa")
                                cx.wait(nc.vector, "sa", av)
                                off = (2 * half + (u - 2 * qt)) if u >= 2 * qt else -1
                                if off >= 0:
                                    v = nc.vector.tensor_tensor(out=ex, in0=ex,
                                                                in1=mask_sb[:, off * 512:(off + 1) * 512],
                                                                op=ALU.mult)
                                    mv = cx.inc(v, "sv")
                                    cx.wait(nc.vector, "sv", mv)
                                if kb == 0:
                                    v = nc.vector.tensor_copy(out=acc[:], in_=ex)
                                else:
                                    v = nc.vector.tensor_tensor(out=acc[:], in0=acc[:], in1=ex, op=ALU.add)
                                ev = cx.inc(v, "sv")
                                cx.wait(nc.tensor, "sv", ev)
                                for lc in range(4):
                                    mm = nc.tensor.matmul(
                                        pls[lc][:, 0:512],
                                        ckvt[:, slot * 512 + lc * 128: slot * 512 + lc * 128 + 128],
                                        ex, start=(kb == 0), stop=(kb == nkb - 1))
                                cx.inc(mm, "st")
                            mm = nc.tensor.matmul(paux[0:1, 0:512], ones_col_f[:, 0:1], acc[:],
                                                  start=True, stop=True)
                            tv = cx.inc(mm, "st")
                            cx.wait(nc.vector, "st", tv)
                            v = nc.vector.reciprocal(out=sums[0:1, 0:512], in_=paux[0:1, 0:512])
                            rv = cx.inc(v, "sv")
                            cx.wait(nc.tensor, "sv", rv)
                            mm = nc.tensor.matmul(paux[:, 0:512], ones_row[0:1, :],
                                                  sums[0:1, 0:512], start=True, stop=True)
                            tv = cx.inc(mm, "st")
                            cx.wait(nc.vector, "st", tv)
                            v = nc.vector.tensor_copy(out=rb[:], in_=paux[:, 0:512])
                            rv = cx.inc(v, "sv")
                            cx.wait(nc.scalar, "st", tv)
                            for lc in range(4):
                                a = nc.scalar.activation(out=ylat_sb[:, lc * 512:(lc + 1) * 512],
                                                         in_=pls[lc][:, 0:512], func=AF.Copy)
                            av = cx.inc(a, "sa")
                            cx.wait(nc.tensor, "sa", av)
                            cx.wait(nc.tensor, "sv", rv)
                            for hh in range(2):
                                h = h0 + hh
                                for lc in range(4):
                                    mm = nc.tensor.matmul(
                                        paux[:, 0:256],
                                        veff[:, lc * 2048 + h * 128: lc * 2048 + h * 128 + 128],
                                        ylat_sb[:, lc * 512 + hh * 256: lc * 512 + hh * 256 + 256],
                                        start=(lc == 0), stop=(lc == 3))
                                tv = cx.inc(mm, "st")
                                cx.wait(nc.vector, "st", tv)
                                v = nc.vector.tensor_tensor(
                                    out=yqt[:, h * 256:(h + 1) * 256],
                                    in0=paux[:, 0:256], in1=rb[:, hh * 256: hh * 256 + 256], op=ALU.mult)
                                yv = cx.inc(v, "sv")
                                cx.wait(nc.tensor, "sv", yv)
                        cx.wait(nc.sync, "sv")
                        d = nc.sync.dma_start(
                            out=y_d[:, qt * 256:(qt + 1) * 256].rearrange("(h p) t -> p h t", p=128),
                            in_=sb3(yqt[:], NH, 256))
                        sv_ = cx.inc(d, "ss", 16)
                        cx.wait(nc.vector, "ss", sv_)

        if stop_after == "P4":
            return nc
        xout_tgt = xdum.ap() if stop_after == "P5NOSTORE" else xout
        with ExitStack() as sC:
            cx.barrier((nc.tensor, nc.vector, nc.scalar, nc.sync, nc.gpsimd))
            xt = sC.enter_context(nc.sbuf_tensor("xt5", [128, 16 * 512], F32))
            yb = sC.enter_context(nc.sbuf_tensor("yb5", [128, 16 * 512], BF16))
            xsq = sC.enter_context(nc.sbuf_tensor("xsq5", [128, 16 * 512], BF16))
            with (nc.psum_tensor("psm2", [128, 512], F32) as psm,
                  nc.psum_tensor("psq2", [128, 512], F32) as psq,
                  nc.psum_tensor("pbc2", [128, 512], F32) as pbc):
                for tt in range(4):
                    cx.wait(nc.sync, "st")
                    d = nc.sync.dma_start(out=sb3(yb[:], 16, 512), in_=feat_tile(y_d, 0, C, tt * 512, 512))
                    wv = cx.inc(d, "sw", 16)
                    cx.wait(nc.scalar, "sw", wv)
                    cx.wait(nc.scalar, "st")
                    for ccc in range(16):
                        a = nc.scalar.activation(out=xsq[:, ccc * 512:(ccc + 1) * 512],
                                                 in_=yb[:, ccc * 512:(ccc + 1) * 512], func=AF.Square)
                    av = cx.inc(a, "sa")
                    cx.wait(nc.tensor, "sa", av)
                    cx.wait(nc.tensor, "sw", wv)
                    for ccc in range(16):
                        nc.tensor.matmul(psm[0:1, 0:512], ones_col[:, 0:1],
                                         yb[:, ccc * 512:(ccc + 1) * 512],
                                         start=(ccc == 0), stop=(ccc == 15))
                    for ccc in range(16):
                        mm = nc.tensor.matmul(psq[0:1, 0:512], ones_col[:, 0:1],
                                              xsq[:, ccc * 512:(ccc + 1) * 512],
                                              start=(ccc == 0), stop=(ccc == 15))
                    tv = cx.inc(mm, "st")
                    cx.wait(nc.vector, "st", tv)
                    nc.vector.tensor_scalar(out=scr[0:1, 0:512], in0=psm[0:1, 0:512],
                                            scalar1=1.0 / 2048, scalar2=None, op0=ALU.mult)
                    nc.vector.tensor_scalar(out=scr[0:1, 512:1024], in0=psq[0:1, 0:512],
                                            scalar1=1.0 / 2048, scalar2=None, op0=ALU.mult)
                    ln_small(512)
                    bcast(rstd[0:1, 0:512], rb, 512, pbc)
                    bv = bcast(mrs[0:1, 0:512], mrb, 512, pbc)
                    d = nc.sync.dma_start(out=sb3(xt[:], 16, 512), in_=feat_tile(xmy, 0, C, tt * 512, 512))
                    wv = cx.inc(d, "sw", 16)
                    cx.wait(nc.vector, "sw", wv)
                    cx.wait(nc.vector, "sv", bv)
                    for ccc in range(16):
                        nc.vector.tensor_tensor(out=f32a[:], in0=yb[:, ccc * 512:(ccc + 1) * 512],
                                                in1=rb[:], op=ALU.mult)
                        nc.vector.tensor_tensor(out=f32b[:], in0=f32a[:], in1=mrb[:], op=ALU.subtract)
                        v = nc.vector.tensor_tensor(out=xt[:, ccc * 512:(ccc + 1) * 512],
                                                    in0=xt[:, ccc * 512:(ccc + 1) * 512],
                                                    in1=f32b[:], op=ALU.add)
                    lv = cx.inc(v, "sv")
                    cx.wait(nc.sync, "sv", lv)
                    d = nc.sync.dma_start(out=feat_tile(xout_tgt, 0, C, tt * 512, 512), in_=sb3(xt[:], 16, 512))
                    sv_ = cx.inc(d, "ss", 16)
                    cx.wait(nc.vector, "ss", sv_)
            cx.wait(nc.sync, "ss")
    return nc




class LNKit:

    def __init__(self, nc, cx, st):
        self.nc, self.cx = nc, cx
        sb = lambda name, shape, dt: st.enter_context(nc.sbuf_tensor(name, shape, dt))
        self.ones_col = sb("k_ones_col", [128, 1], BF16)
        self.ones_col_f = sb("k_ones_col_f", [128, 1], F32)
        self.ones_row = sb("k_ones_row", [1, 128], F32)
        self.eps = sb("k_eps", [1, 1], F32)
        self.rstd = sb("k_rstd", [1, 512], F32)
        self.mrs = sb("k_mrs", [1, 512], F32)
        self.scr = sb("k_scr", [1, 2048], F32)
        self.rb = sb("k_rb", [128, 512], F32)
        self.mrb = sb("k_mrb", [128, 512], F32)
        self.f32a = sb("k_f32a", [128, 512], F32)
        self.f32b = sb("k_f32b", [128, 512], F32)
        nc.gpsimd.memset(self.ones_col[:], 1.0)
        nc.gpsimd.memset(self.ones_col_f[:], 1.0)
        nc.gpsimd.memset(self.ones_row[:], 1.0)
        g = nc.gpsimd.memset(self.eps[:], EPS)
        cx.inc(g, "sv")

    def ln_small(self, jw, n):
        nc, cx = self.nc, self.cx
        mu = self.scr[0:1, 0:jw]
        ex2 = self.scr[0:1, 512:512 + jw]
        tmp = self.scr[0:1, 1024:1024 + jw]
        v = nc.vector.tensor_scalar(out=mu, in0=mu, scalar1=1.0 / n, scalar2=None, op0=ALU.mult)
        v2 = nc.vector.tensor_scalar(out=ex2, in0=ex2, scalar1=1.0 / n, scalar2=None, op0=ALU.mult)
        cx.wait(nc.vector, "sv", cx.inc(v2, "sv"))
        v = nc.vector.tensor_tensor(out=tmp, in0=mu, in1=mu, op=ALU.mult)
        cx.wait(nc.vector, "sv", cx.inc(v, "sv"))
        v = nc.vector.tensor_tensor(out=ex2, in0=ex2, in1=tmp, op=ALU.subtract)
        cx.inc(v, "sv")
        cx.wait(nc.scalar, "sv")
        a = nc.scalar.activation(out=ex2, in_=ex2, func=AF.Ln, bias=self.eps[0:1, 0:1])
        cx.wait(nc.scalar, "sa", cx.inc(a, "sa"))
        a = nc.scalar.activation(out=self.rstd[0:1, 0:jw], in_=ex2, func=AF.Exp, scale=-0.5)
        cx.inc(a, "sa")
        cx.wait(nc.vector, "sa")
        v = nc.vector.tensor_tensor(out=self.mrs[0:1, 0:jw], in0=mu,
                                    in1=self.rstd[0:1, 0:jw], op=ALU.mult)
        return cx.inc(v, "sv")

    def bcast(self, vec_ap, dst_sb, jw, pb):
        nc, cx = self.nc, self.cx
        cx.wait(nc.tensor, "sv")
        mm = nc.tensor.matmul(pb[:, 0:jw], self.ones_row[0:1, :], vec_ap, start=True, stop=True)
        tv = cx.inc(mm, "st")
        cx.wait(nc.vector, "st", tv)
        v = nc.vector.tensor_copy(out=dst_sb[:, 0:jw], in_=pb[:, 0:jw])
        return cx.inc(v, "sv")

    def stats_tile(self, xb_get, xsq_get, psm, psq, jw, n_cc):
        nc, cx = self.nc, self.cx
        for ccc in range(n_cc):
            nc.tensor.matmul(psm[0:1, 0:jw], self.ones_col[:, 0:1], xb_get(ccc),
                             start=(ccc == 0), stop=(ccc == n_cc - 1))
        for ccc in range(n_cc):
            mm = nc.tensor.matmul(psq[0:1, 0:jw], self.ones_col[:, 0:1], xsq_get(ccc),
                                  start=(ccc == 0), stop=(ccc == n_cc - 1))
        tv = cx.inc(mm, "st")
        cx.wait(nc.vector, "st", tv)
        nc.vector.tensor_copy(out=self.scr[0:1, 0:jw], in_=psm[0:1, 0:jw])
        v = nc.vector.tensor_copy(out=self.scr[0:1, 512:512 + jw], in_=psq[0:1, 0:jw])
        cx.wait(nc.vector, "sv", cx.inc(v, "sv"))
        return tv

    def ln_tile(self, xb_get, xsq_get, psm, psq, pbc, jw, n_cc, n_feat):
        self.stats_tile(xb_get, xsq_get, psm, psq, jw, n_cc)
        self.ln_small(jw, n_feat)
        self.bcast(self.rstd[0:1, 0:jw], self.rb, jw, pbc)
        return self.bcast(self.mrs[0:1, 0:jw], self.mrb, jw, pbc)


def build_lf0(debug=False, stop_after=None):
    nc = bass.Bass("TRN2", target_bir_lowering=False)
    xmy = nc.dram_tensor("xmy", [C, MYT], F32, kind="ExternalInput").ap()
    up_t = nc.dram_tensor("up_t", [C, 8192], F32, kind="ExternalInput").ap()
    down_b = nc.dram_tensor("down_b", [16, 64, 128, 128], F32, kind="ExternalInput").ap()
    usc = nc.dram_tensor("usc", [128, 64], F32, kind="ExternalInput").ap()
    dsc = nc.dram_tensor("dsc", [128, 16], F32, kind="ExternalInput").ap()
    osc = nc.dram_tensor("osc", [1], F32, kind="ExternalInput").ap()
    xout = nc.dram_tensor("xout", [C, MYT], F32, kind="ExternalOutput").ap()
    kind = dict(kind="ExternalOutput") if debug else {}
    lnx3_d = nc.dram_tensor("lnx3_d", [C, 1024], BF16, **kind).ap()
    hid_d = nc.dram_tensor("hid_d", [128, 32 * 1024], BF16, **kind).ap()
    acc_d = nc.dram_tensor("acc_d", [128, 16 * 1024], BF16, **kind).ap()

    st = ExitStack()
    with st:
        cx = Ctx(nc, st, ("sw", "sg", "st", "sv", "sa", "ss"))
        K = LNKit(nc, cx, st)
        BAR = (nc.tensor, nc.vector, nc.scalar, nc.sync, nc.gpsimd)
        sbb = lambda name, shape, dt=BF16: st.enter_context(nc.sbuf_tensor(name, shape, dt))

        us_sb = st.enter_context(nc.sbuf_tensor("us_sb", [128, 64], F32))
        ds2 = st.enter_context(nc.sbuf_tensor("ds2", [128, 16], F32))
        osb = st.enter_context(nc.sbuf_tensor("osb", [1, 1], F32))
        osc_b = st.enter_context(nc.sbuf_tensor("osc_b", [128, 1], F32))
        lnx3 = sbb("lnx3", [128, 16 * 1024])
        hid = sbb("hid", [128, 32 * 1024])
        ffnacc = sbb("ffnacc", [128, 16 * 1024])
        scratch = sbb("scratch", [128, 16384])
        upw = scratch[:, 0:8192]
        upsg = scratch[:, 8192:16384]
        dnw = scratch[:, 0:8192]
        dnsg = scratch[:, 8192:16384]
        xt = scratch[:, 0:8192].bitcast(F32)
        xb = scratch[:, 8192:12288]
        xsq = scratch[:, 12288:16384]

        d = nc.sync.dma_start(out=us_sb[:], in_=usc)
        cx.inc(d, "sw", 16)
        d = nc.sync.dma_start(out=ds2[:], in_=dsc)
        cx.inc(d, "sw", 16)
        d = nc.sync.dma_start(out=osb[:], in_=osc.rearrange("(a o) -> a o", a=1))
        cx.inc(d, "sw", 16)
        with nc.psum_tensor("pb0", [128, 512], F32) as pb0:
            cx.wait(nc.tensor, "sw")
            cx.wait(nc.tensor, "sv")
            mm = nc.tensor.matmul(pb0[:, 0:1], K.ones_row[0:1, :], osb[0:1, 0:1], start=True, stop=True)
            tv = cx.inc(mm, "st")
            cx.wait(nc.vector, "st", tv)
            v = nc.vector.tensor_copy(out=osc_b[:], in_=pb0[:, 0:1])
            vv = cx.inc(v, "sv")
            cx.wait(nc.vector, "sv", vv)
            v = nc.vector.tensor_scalar(out=ds2[:], in0=ds2[:], scalar1=osc_b[:, 0:1],
                                        scalar2=None, op0=ALU.mult)
            cx.inc(v, "sv")

        with (nc.psum_tensor("ph0", [128, 512], F32) as ph0,
              nc.psum_tensor("ph1", [128, 512], F32) as ph1,
              nc.psum_tensor("psm", [128, 512], F32) as psm,
              nc.psum_tensor("psq", [128, 512], F32) as psq,
              nc.psum_tensor("pbc", [128, 512], F32) as pbc):
            phs = [ph0, ph1]
            for th in range(2):
                for tt in range(4):
                    t0 = th * 1024 + tt * 256
                    cx.wait(nc.sync, "st")
                    cx.wait(nc.sync, "sv")
                    d = nc.sync.dma_start(out=sb3(xt[:], 16, 256), in_=feat_tile(xmy, 0, C, t0, 256))
                    wv = cx.inc(d, "sw", 16)
                    cx.wait(nc.vector, "sw", wv)
                    cx.wait(nc.vector, "st")
                    for ccc in range(16):
                        v = nc.vector.tensor_copy(out=xb[:, ccc * 256:(ccc + 1) * 256],
                                                  in_=xt[:, ccc * 256:(ccc + 1) * 256])
                    vv = cx.inc(v, "sv")
                    cx.wait(nc.scalar, "sv", vv)
                    for ccc in range(16):
                        a = nc.scalar.activation(out=xsq[:, ccc * 256:(ccc + 1) * 256],
                                                 in_=xb[:, ccc * 256:(ccc + 1) * 256], func=AF.Square)
                    av = cx.inc(a, "sa")
                    cx.wait(nc.tensor, "sa", av)
                    cx.wait(nc.tensor, "sv", vv)
                    bv = K.ln_tile(lambda ccc: xb[:, ccc * 256:(ccc + 1) * 256],
                                   lambda ccc: xsq[:, ccc * 256:(ccc + 1) * 256],
                                   psm, psq, pbc, 256, 16, 2048)
                    cx.wait(nc.vector, "sv", bv)
                    for ccc in range(16):
                        nc.vector.tensor_tensor(out=K.f32a[:, 0:256], in0=xt[:, ccc * 256:(ccc + 1) * 256],
                                                in1=K.rb[:, 0:256], op=ALU.mult)
                        v = nc.vector.tensor_tensor(
                            out=lnx3[:, ccc * 1024 + tt * 256: ccc * 1024 + tt * 256 + 256],
                            in0=K.f32a[:, 0:256], in1=K.mrb[:, 0:256], op=ALU.subtract)
                    cx.inc(v, "sv")
                cx.barrier(BAR)
                if debug and th == 0:
                    d = nc.sync.dma_start(out=feat_tile(lnx3_d, 0, C, 0, 1024), in_=sb3(lnx3[:], 16, 1024))
                    sv_ = cx.inc(d, "ss", 16)
                    cx.wait(nc.sync, "ss", sv_)
                    if stop_after == "LN3":
                        return nc
                for jh in range(1 if stop_after == "JH0" else 2):
                    cx.barrier(BAR)
                    for jg in range(8):
                        j0 = jh * 4096 + jg * 512
                        cx.wait(nc.gpsimd, "st")
                        cx.wait(nc.gpsimd, "sa")
                        d = nc.gpsimd.dma_start(out=sb3(upw[:], 16, 512),
                                                in_=up_t[:, j0:j0 + 512].rearrange("(cc p) j -> p cc j", p=128))
                        gv = cx.inc(d, "sg", 16)
                        cx.wait(nc.scalar, "sg", gv)
                        cx.wait(nc.scalar, "st")
                        for ccc in range(16):
                            a = nc.scalar.activation(out=upsg[:, ccc * 512:(ccc + 1) * 512],
                                                     in_=upw[:, ccc * 512:(ccc + 1) * 512], func=AF.Sign)
                        av = cx.inc(a, "sa")
                        cx.wait(nc.tensor, "sa", av)
                        cx.wait(nc.tensor, "sv")
                        for jb in range(4):
                            jloc = jg * 4 + jb
                            for half in range(2):
                                for ccc in range(16):
                                    mm = nc.tensor.matmul(
                                        phs[half][:, 0:512],
                                        upsg[:, ccc * 512 + jb * 128: ccc * 512 + jb * 128 + 128],
                                        lnx3[:, ccc * 1024 + half * 512: ccc * 1024 + half * 512 + 512],
                                        start=(ccc == 0), stop=(ccc == 15))
                            tv = cx.inc(mm, "st")
                            cx.wait(nc.scalar, "st", tv)
                            for half in range(2):
                                a = nc.scalar.activation(
                                    out=hid[:, jloc * 1024 + half * 512: jloc * 1024 + half * 512 + 512],
                                    in_=phs[half][:, 0:512], func=AF.Relu,
                                    scale=us_sb[:, (jh * 4096 + jloc * 128) // 128:(jh * 4096 + jloc * 128) // 128 + 1])
                            av = cx.inc(a, "sa")
                            cx.wait(nc.tensor, "sa", av)
                    if debug and th == 0 and jh == 0:
                        cx.barrier(BAR)
                        d = nc.sync.dma_start(out=hid_d[:, :], in_=hid[:])
                        sv_ = cx.inc(d, "ss", 16)
                        cx.wait(nc.sync, "ss", sv_)
                        if stop_after == "UP":
                            return nc
                    cx.barrier(BAR)
                    for cg in range(8):
                        cx.wait(nc.gpsimd, "st")
                        d = nc.gpsimd.dma_start(
                            out=dnw[:].rearrange("p (jb g) -> p jb g", g=256),
                            in_=down_t[jh * 4096:(jh + 1) * 4096, cg * 256:(cg + 1) * 256]
                                .rearrange("(jb p) g -> p jb g", p=128))
                        gv = cx.inc(d, "sg", 16)
                        cx.wait(nc.scalar, "sg", gv)
                        cx.wait(nc.scalar, "st")
                        for jb in range(32):
                            a = nc.scalar.activation(out=dnsg[:, jb * 256:(jb + 1) * 256],
                                                     in_=dnw[:, jb * 256:(jb + 1) * 256], func=AF.Sign)
                        av = cx.inc(a, "sa")
                        cx.wait(nc.tensor, "sa", av)
                        for ci in range(2):
                            ccc = cg * 2 + ci
                            for half in range(2):
                                for jb in range(32):
                                    mm = nc.tensor.matmul(
                                        phs[half][:, 0:512],
                                        dnsg[:, jb * 256 + ci * 128: jb * 256 + ci * 128 + 128],
                                        hid[:, jb * 1024 + half * 512: jb * 1024 + half * 512 + 512],
                                        start=(jb == 0), stop=(jb == 31))
                            tv = cx.inc(mm, "st")
                            cx.wait(nc.vector, "st", tv)
                            for half in range(2):
                                dst = ffnacc[:, ccc * 1024 + half * 512: ccc * 1024 + half * 512 + 512]
                                if jh == 0:
                                    v = nc.vector.tensor_copy(out=dst, in_=phs[half][:, 0:512])
                                else:
                                    nc.vector.tensor_tensor(out=K.f32a[:], in0=phs[half][:, 0:512],
                                                            in1=dst, op=ALU.add)
                                    v = nc.vector.tensor_scalar(out=dst, in0=K.f32a[:],
                                                                scalar1=ds2[:, ccc:ccc + 1],
                                                                scalar2=None, op0=ALU.mult)
                            vv = cx.inc(v, "sv")
                            cx.wait(nc.tensor, "sv", vv)
                if debug and th == 0:
                    cx.barrier(BAR)
                    d = nc.sync.dma_start(out=acc_d[:, :], in_=ffnacc[:])
                    sv_ = cx.inc(d, "ss", 16)
                    cx.wait(nc.sync, "ss", sv_)
                    if stop_after in ("DOWN", "JH0"):
                        return nc
                cx.barrier(BAR)
                for tt in range(4):
                    t0 = th * 1024 + tt * 256
                    cx.wait(nc.scalar, "sv")
                    for ccc in range(16):
                        a = nc.scalar.activation(
                            out=xsq[:, ccc * 256:(ccc + 1) * 256],
                            in_=ffnacc[:, ccc * 1024 + tt * 256: ccc * 1024 + tt * 256 + 256],
                            func=AF.Square)
                    av = cx.inc(a, "sa")
                    cx.wait(nc.tensor, "sa", av)
                    bv = K.ln_tile(lambda ccc: ffnacc[:, ccc * 1024 + tt * 256: ccc * 1024 + tt * 256 + 256],
                                   lambda ccc: xsq[:, ccc * 256:(ccc + 1) * 256],
                                   psm, psq, pbc, 256, 16, 2048)
                    cx.wait(nc.sync, "sv")
                    d = nc.sync.dma_start(out=sb3(xt[:], 16, 256), in_=feat_tile(xmy, 0, C, t0, 256))
                    wv = cx.inc(d, "sw", 16)
                    cx.wait(nc.vector, "sw", wv)
                    cx.wait(nc.vector, "sv", bv)
                    for ccc in range(16):
                        nc.vector.tensor_tensor(out=K.f32a[:, 0:256],
                                                in0=ffnacc[:, ccc * 1024 + tt * 256: ccc * 1024 + tt * 256 + 256],
                                                in1=K.rb[:, 0:256], op=ALU.mult)
                        nc.vector.tensor_tensor(out=K.f32b[:, 0:256], in0=K.f32a[:, 0:256],
                                                in1=K.mrb[:, 0:256], op=ALU.subtract)
                        v = nc.vector.tensor_tensor(out=xt[:, ccc * 256:(ccc + 1) * 256],
                                                    in0=xt[:, ccc * 256:(ccc + 1) * 256],
                                                    in1=K.f32b[:, 0:256], op=ALU.add)
                    lv = cx.inc(v, "sv")
                    cx.wait(nc.sync, "sv", lv)
                    d = nc.sync.dma_start(out=feat_tile(xout, 0, C, t0, 256), in_=sb3(xt[:], 16, 256))
                    sv_ = cx.inc(d, "ss", 16)
                    cx.wait(nc.vector, "ss", sv_)
                cx.barrier(BAR)
        cx.wait(nc.sync, "ss")
    return nc




def build_la1(debug=False):
    nc = bass.Bass("TRN2", target_bir_lowering=False)
    xin = nc.dram_tensor("xfull", [C, T], F32, kind="ExternalInput").ap()
    xmy = nc.dram_tensor("xmy", [C, MYT], F32, kind="ExternalInput").ap()
    wqkv = nc.dram_tensor("wqkv", [C, 3 * C], F32, kind="ExternalInput").ap()
    wout = nc.dram_tensor("wout", [C, C], F32, kind="ExternalInput").ap()
    masks = nc.dram_tensor("masks", [4 * 128, 256], F32, kind="ExternalInput").ap()
    xout = nc.dram_tensor("xout", [C, MYT], F32, kind="ExternalOutput").ap()
    kind = dict(kind="ExternalOutput") if debug else {}
    lnx_d = nc.dram_tensor("lnx_d", [C, T], BF16).ap()
    lnmy_d = nc.dram_tensor("lnmy_d", [C, MYT], BF16).ap()
    vt_d = nc.dram_tensor("vt_d", [T, C], BF16, **kind).ap()
    kf_d = nc.dram_tensor("kf_d", [C, T], BF16, **kind).ap()
    y_d = nc.dram_tensor("y_d", [C, MYT], BF16, **kind).ap()

    st = ExitStack()
    with st:
        cx = Ctx(nc, st, ("sw", "sg", "st", "sv", "sa", "ss"))
        K = LNKit(nc, cx, st)
        BAR = (nc.tensor, nc.vector, nc.scalar, nc.sync, nc.gpsimd)
        sbb = lambda name, shape, dt=BF16: st.enter_context(nc.sbuf_tensor(name, shape, dt))
        mask_sb = sbb("mask_sb", [128, 4 * 256])
        d = nc.gpsimd.dma_start(out=sb3(mask_sb[:], 4, 256),
                                in_=masks.rearrange("(m p) j -> p m j", p=128))
        cx.inc(d, "sg", 16)

        with ExitStack() as sA:
            xt = sA.enter_context(nc.sbuf_tensor("xtA", [128, 16 * 512], F32))
            xb = sA.enter_context(nc.sbuf_tensor("xbA", [128, 16 * 512], BF16))
            xsq = sA.enter_context(nc.sbuf_tensor("xsqA", [128, 16 * 512], BF16))
            with (nc.psum_tensor("psmA", [128, 512], F32) as psm,
                  nc.psum_tensor("psqA", [128, 512], F32) as psq,
                  nc.psum_tensor("pbcA", [128, 512], F32) as pbc):
                for tt in range(12):
                    if tt < 8:
                        src_ap = feat_tile(xin, 0, C, tt * 512, 512)
                    else:
                        src_ap = feat_tile(xmy, 0, C, (tt - 8) * 512, 512)
                    cx.wait(nc.sync, "st")
                    cx.wait(nc.sync, "sv")
                    d = nc.sync.dma_start(out=sb3(xt[:], 16, 512), in_=src_ap)
                    wv = cx.inc(d, "sw", 16)
                    cx.wait(nc.vector, "sw", wv)
                    cx.wait(nc.vector, "st")
                    for ccc in range(16):
                        v = nc.vector.tensor_copy(out=xb[:, ccc * 512:(ccc + 1) * 512],
                                                  in_=xt[:, ccc * 512:(ccc + 1) * 512])
                    vv = cx.inc(v, "sv")
                    cx.wait(nc.scalar, "sv", vv)
                    for ccc in range(16):
                        a = nc.scalar.activation(out=xsq[:, ccc * 512:(ccc + 1) * 512],
                                                 in_=xb[:, ccc * 512:(ccc + 1) * 512], func=AF.Square)
                    av = cx.inc(a, "sa")
                    cx.wait(nc.tensor, "sa", av)
                    cx.wait(nc.tensor, "sv", vv)
                    bv = K.ln_tile(lambda ccc: xb[:, ccc * 512:(ccc + 1) * 512],
                                   lambda ccc: xsq[:, ccc * 512:(ccc + 1) * 512],
                                   psm, psq, pbc, 512, 16, 2048)
                    cx.wait(nc.vector, "sv", bv)
                    for ccc in range(16):
                        nc.vector.tensor_tensor(out=K.f32a[:], in0=xt[:, ccc * 512:(ccc + 1) * 512],
                                                in1=K.rb[:], op=ALU.mult)
                        v = nc.vector.tensor_tensor(out=xb[:, ccc * 512:(ccc + 1) * 512],
                                                    in0=K.f32a[:], in1=K.mrb[:], op=ALU.subtract)
                    lv = cx.inc(v, "sv")
                    cx.wait(nc.sync, "sv", lv)
                    if tt < 8:
                        dst_ap = feat_tile(lnx_d, 0, C, tt * 512, 512)
                    else:
                        dst_ap = feat_tile(lnmy_d, 0, C, (tt - 8) * 512, 512)
                    d = nc.sync.dma_start(out=dst_ap, in_=sb3(xb[:], 16, 512))
                    sv_ = cx.inc(d, "ss", 16)
                    cx.wait(nc.vector, "ss", sv_)

        cx.barrier(BAR)
        with ExitStack() as sB:
            wkv = sB.enter_context(nc.sbuf_tensor("wkv", [128, 16 * 2048], BF16))
            lnt = sB.enter_context(nc.sbuf_tensor("lntB", [128, 16 * 512], BF16))
            vtt = sB.enter_context(nc.sbuf_tensor("vtt", [128, 2048], BF16))
            ktt = sB.enter_context(nc.sbuf_tensor("ktt", [128, NH * 512], BF16))
            d = nc.gpsimd.dma_start(out=sb3(wkv[:], 16, 2048),
                                    in_=wqkv[:, C:2 * C].rearrange("(cc p) o -> p cc o", p=128))
            cx.inc(d, "sg", 16)
            cx.wait(nc.tensor, "sg")
            with nc.psum_tensor("pkB", [128, 512], F32) as pk:
                for tt in range(8):
                    cx.wait(nc.sync, "st")
                    d = nc.sync.dma_start(out=sb3(lnt[:], 16, 512),
                                          in_=feat_tile(lnx_d, 0, C, tt * 512, 512))
                    wv = cx.inc(d, "sw", 16)
                    cx.wait(nc.tensor, "sw", wv)
                    cx.wait(nc.scalar, "ss")
                    for h in range(NH):
                        for ccc in range(16):
                            mm = nc.tensor.matmul(
                                pk[:, 0:512],
                                wkv[:, ccc * 2048 + h * 128: ccc * 2048 + h * 128 + 128],
                                lnt[:, ccc * 512:(ccc + 1) * 512],
                                start=(ccc == 0), stop=(ccc == 15))
                        tv = cx.inc(mm, "st")
                        cx.wait(nc.scalar, "st", tv)
                        a = nc.scalar.activation(out=ktt[:, h * 512:(h + 1) * 512],
                                                 in_=pk[:, 0:512], func=AF.Copy)
                        av = cx.inc(a, "sa")
                        cx.wait(nc.tensor, "sa", av)
                    cx.wait(nc.sync, "sa")
                    d = nc.sync.dma_start(
                        out=kf_d[:, tt * 512:(tt + 1) * 512].rearrange("(h p) t -> p h t", p=128),
                        in_=sb3(ktt[:], NH, 512))
                    sv_ = cx.inc(d, "ss", 16)
            cx.wait(nc.gpsimd, "st")
            d = nc.gpsimd.dma_start(out=sb3(wkv[:], 16, 2048),
                                    in_=wqkv[:, 2 * C:3 * C].rearrange("(cc p) o -> p cc o", p=128))
            gv = cx.inc(d, "sg", 16)
            cx.wait(nc.tensor, "sg", gv)
            with nc.psum_tensor("pvB", [128, 512], F32) as pv:
                for tt in range(8):
                    cx.wait(nc.sync, "st")
                    d = nc.sync.dma_start(out=sb3(lnt[:], 16, 512),
                                          in_=feat_tile(lnx_d, 0, C, tt * 512, 512))
                    wv = cx.inc(d, "sw", 16)
                    cx.wait(nc.tensor, "sw", wv)
                    for tc in range(4):
                        for jg in range(4):
                            for ccc in range(16):
                                mm = nc.tensor.matmul(
                                    pv[:, 0:512],
                                    lnt[:, ccc * 512 + tc * 128: ccc * 512 + tc * 128 + 128],
                                    wkv[:, ccc * 2048 + jg * 512: ccc * 2048 + jg * 512 + 512],
                                    start=(ccc == 0), stop=(ccc == 15))
                            tv = cx.inc(mm, "st")
                            cx.wait(nc.scalar, "st", tv)
                            a = nc.scalar.activation(out=vtt[:, jg * 512:(jg + 1) * 512],
                                                     in_=pv[:, 0:512], func=AF.Copy)
                            av = cx.inc(a, "sa")
                            cx.wait(nc.tensor, "sa", av)
                        cx.wait(nc.sync, "sa")
                        d = nc.sync.dma_start(out=vt_d[tt * 512 + tc * 128: tt * 512 + tc * 128 + 128, :],
                                              in_=vtt[:])
                        sv_ = cx.inc(d, "ss", 16)
                        cx.wait(nc.scalar, "ss", sv_)

        cx.barrier(BAR)
        with ExitStack() as sc:
            kall = sc.enter_context(nc.sbuf_tensor("kallC", [128, 8 * T], BF16))
            wq = sc.enter_context(nc.sbuf_tensor("wqC", [128, 16 * 2048], BF16))
            qsb = sc.enter_context(nc.sbuf_tensor("qsbC", [128, 8 * 256], BF16))
            vtb = sc.enter_context(nc.sbuf_tensor("vtbC", [128, 2 * 2048], BF16))
            exb = sc.enter_context(nc.sbuf_tensor("exbC", [128, 4 * 256], BF16))
            acc = sc.enter_context(nc.sbuf_tensor("accC", [128, 4 * 256], F32))
            ysb = sc.enter_context(nc.sbuf_tensor("ysbC", [128, 8 * 256], BF16))
            lnq = sc.enter_context(nc.sbuf_tensor("lnqC", [128, 16 * 256], BF16))
            sums1 = sc.enter_context(nc.sbuf_tensor("sums1", [1, 256], F32))
            d = nc.gpsimd.dma_start(out=sb3(wq[:], 16, 2048),
                                    in_=wqkv[:, 0:C].rearrange("(cc p) o -> p cc o", p=128))
            cx.inc(d, "sg", 16)
            cx.wait(nc.tensor, "sg")
            with (nc.psum_tensor("py0", [128, 512], F32) as py0,
                  nc.psum_tensor("py1", [128, 512], F32) as py1,
                  nc.psum_tensor("py2", [128, 512], F32) as py2,
                  nc.psum_tensor("py3", [128, 512], F32) as py3,
                  nc.psum_tensor("ps0", [128, 512], F32) as ps0,
                  nc.psum_tensor("ps1", [128, 512], F32) as ps1,
                  nc.psum_tensor("ps2", [128, 512], F32) as ps2,
                  nc.psum_tensor("ps3", [128, 512], F32) as ps3):
                pys = [py0, py1, py2, py3]
                pss = [ps0, ps1, ps2, ps3]
                for hhalf in range(2):
                    hbase = hhalf * 8
                    cx.barrier(BAR)
                    d = nc.sync.dma_start(out=sb3(kall[:], 8, T),
                                          in_=kf_d[hbase * 128:(hbase + 8) * 128, :]
                                          .rearrange("(h p) t -> p h t", p=128))
                    kv_ = cx.inc(d, "sw", 16)
                    cx.wait(nc.tensor, "sw", kv_)
                    for qt in range(8):
                        cx.wait(nc.sync, "st")
                        cx.wait(nc.sync, "sa")
                        d = nc.sync.dma_start(out=sb3(lnq[:], 16, 256),
                                              in_=feat_tile(lnmy_d, 0, C, qt * 256, 256))
                        wv = cx.inc(d, "sw", 16)
                        cx.wait(nc.tensor, "sw", wv)
                        for h_ in range(8):
                            h = hbase + h_
                            for ccc in range(16):
                                mm = nc.tensor.matmul(
                                    ps0[:, 0:256],
                                    wq[:, ccc * 2048 + h * 128: ccc * 2048 + h * 128 + 128],
                                    lnq[:, ccc * 256:(ccc + 1) * 256],
                                    start=(ccc == 0), stop=(ccc == 15))
                            tv = cx.inc(mm, "st")
                            cx.wait(nc.scalar, "st", tv)
                            a = nc.scalar.activation(out=qsb[:, h_ * 256:(h_ + 1) * 256],
                                                     in_=ps0[:, 0:256], func=AF.Copy, scale=ISQ)
                            av = cx.inc(a, "sa")
                            cx.wait(nc.tensor, "sa", av)
                        kbs = [(0, u) for u in range(2 * qt + 2)] + [(1, u) for u in range(2 * qt + 2)]
                        nkb = len(kbs)
                        for hg in range(2):
                            for kb, (half, u) in enumerate(kbs):
                                slot = 16 * half + u
                                vslot = kb % 2
                                cx.wait(nc.sync, "st")
                                d = nc.sync.dma_start(out=vtb[:, vslot * 2048:(vslot + 1) * 2048],
                                                      in_=vt_d[slot * 128:(slot + 1) * 128, :])
                                wv = cx.inc(d, "sw", 16)
                                for hh in range(4):
                                    h_ = hg * 4 + hh
                                    h = hbase + h_
                                    mm = nc.tensor.matmul(
                                        pss[hh][:, 0:256],
                                        kall[:, h_ * T + slot * 128: h_ * T + slot * 128 + 128],
                                        qsb[:, h_ * 256:(h_ + 1) * 256],
                                        start=True, stop=True)
                                    tv = cx.inc(mm, "st")
                                    cx.wait(nc.scalar, "st", tv)
                                    a = nc.scalar.activation(out=exb[:, hh * 256:(hh + 1) * 256],
                                                             in_=pss[hh][:, 0:256], func=AF.Exp)
                                    av = cx.inc(a, "sa")
                                    cx.wait(nc.vector, "sa", av)
                                    off = (2 * half + (u - 2 * qt)) if u >= 2 * qt else -1
                                    ex = exb[:, hh * 256:(hh + 1) * 256]
                                    if off >= 0:
                                        v = nc.vector.tensor_tensor(
                                            out=ex, in0=ex,
                                            in1=mask_sb[:, off * 256:(off + 1) * 256], op=ALU.mult)
                                        cx.inc(v, "sv")
                                    if kb == 0:
                                        v = nc.vector.tensor_copy(out=acc[:, hh * 256:(hh + 1) * 256], in_=ex)
                                    else:
                                        v = nc.vector.tensor_tensor(out=acc[:, hh * 256:(hh + 1) * 256],
                                                                    in0=acc[:, hh * 256:(hh + 1) * 256],
                                                                    in1=ex, op=ALU.add)
                                    ev = cx.inc(v, "sv")
                                    cx.wait(nc.tensor, "sv", ev)
                                    cx.wait(nc.tensor, "sw", wv)
                                    mm = nc.tensor.matmul(
                                        pys[hh][:, 0:256],
                                        vtb[:, vslot * 2048 + h * 128: vslot * 2048 + h * 128 + 128],
                                        ex, start=(kb == 0), stop=(kb == nkb - 1))
                                    cx.inc(mm, "st")
                            for hh in range(4):
                                h_ = hg * 4 + hh
                                mm = nc.tensor.matmul(ps0[0:1, 0:256], K.ones_col_f[:, 0:1],
                                                      acc[:, hh * 256:(hh + 1) * 256],
                                                      start=True, stop=True)
                                tv = cx.inc(mm, "st")
                                cx.wait(nc.vector, "st", tv)
                                v = nc.vector.reciprocal(out=sums1[0:1, 0:256], in_=ps0[0:1, 0:256])
                                rv = cx.inc(v, "sv")
                                cx.wait(nc.tensor, "sv", rv)
                                mm = nc.tensor.matmul(ps0[:, 0:256], K.ones_row[0:1, :],
                                                      sums1[0:1, 0:256], start=True, stop=True)
                                tv = cx.inc(mm, "st")
                                cx.wait(nc.vector, "st", tv)
                                v = nc.vector.tensor_copy(out=K.rb[:, 0:256], in_=ps0[:, 0:256])
                                rv = cx.inc(v, "sv")
                                cx.wait(nc.vector, "sv", rv)
                                v = nc.vector.tensor_tensor(out=ysb[:, h_ * 256:(h_ + 1) * 256],
                                                            in0=pys[hh][:, 0:256],
                                                            in1=K.rb[:, 0:256], op=ALU.mult)
                                yv = cx.inc(v, "sv")
                                cx.wait(nc.tensor, "sv", yv)
                        cx.wait(nc.sync, "sv")
                        d = nc.sync.dma_start(
                            out=y_d[hbase * 128:(hbase + 8) * 128, qt * 256:(qt + 1) * 256]
                                .rearrange("(h p) t -> p h t", p=128),
                            in_=sb3(ysb[:], 8, 256))
                        sv_ = cx.inc(d, "ss", 16)
                        cx.wait(nc.vector, "ss", sv_)

        cx.barrier(BAR)
        with ExitStack() as sD:
            wo = sD.enter_context(nc.sbuf_tensor("woD", [128, 16 * 2048], BF16))
            ytl = sD.enter_context(nc.sbuf_tensor("ytlD", [128, 16 * 256], BF16))
            att = sD.enter_context(nc.sbuf_tensor("attD", [128, 16 * 256], BF16))
            xsq = sD.enter_context(nc.sbuf_tensor("xsqD", [128, 16 * 256], BF16))
            xt = sD.enter_context(nc.sbuf_tensor("xtD", [128, 16 * 256], F32))
            d = nc.gpsimd.dma_start(out=sb3(wo[:], 16, 2048),
                                    in_=wout.rearrange("(cc p) o -> p cc o", p=128))
            cx.inc(d, "sg", 16)
            cx.wait(nc.tensor, "sg")
            with (nc.psum_tensor("poD", [128, 512], F32) as po,
                  nc.psum_tensor("psmD", [128, 512], F32) as psm,
                  nc.psum_tensor("psqD", [128, 512], F32) as psq,
                  nc.psum_tensor("pbcD", [128, 512], F32) as pbc):
                for qt in range(8):
                    cx.wait(nc.sync, "st")
                    cx.wait(nc.sync, "sa")
                    cx.wait(nc.sync, "sv")
                    d = nc.sync.dma_start(out=sb3(ytl[:], 16, 256),
                                          in_=feat_tile(y_d, 0, C, qt * 256, 256))
                    wv = cx.inc(d, "sw", 16)
                    cx.wait(nc.tensor, "sw", wv)
                    for oc in range(16):
                        for ccc in range(16):
                            mm = nc.tensor.matmul(
                                po[:, 0:256],
                                wo[:, ccc * 2048 + oc * 128: ccc * 2048 + oc * 128 + 128],
                                ytl[:, ccc * 256:(ccc + 1) * 256],
                                start=(ccc == 0), stop=(ccc == 15))
                        tv = cx.inc(mm, "st")
                        cx.wait(nc.scalar, "st", tv)
                        a = nc.scalar.activation(out=att[:, oc * 256:(oc + 1) * 256],
                                                 in_=po[:, 0:256], func=AF.Copy)
                        av = cx.inc(a, "sa")
                        cx.wait(nc.tensor, "sa", av)
                    cx.wait(nc.scalar, "sv")
                    for ccc in range(16):
                        a = nc.scalar.activation(out=xsq[:, ccc * 256:(ccc + 1) * 256],
                                                 in_=att[:, ccc * 256:(ccc + 1) * 256], func=AF.Square)
                    av = cx.inc(a, "sa")
                    cx.wait(nc.tensor, "sa", av)
                    bv = K.ln_tile(lambda ccc: att[:, ccc * 256:(ccc + 1) * 256],
                                   lambda ccc: xsq[:, ccc * 256:(ccc + 1) * 256],
                                   psm, psq, pbc, 256, 16, 2048)
                    cx.wait(nc.sync, "sv")
                    d = nc.sync.dma_start(out=sb3(xt[:], 16, 256), in_=feat_tile(xmy, 0, C, qt * 256, 256))
                    wv = cx.inc(d, "sw", 16)
                    cx.wait(nc.vector, "sw", wv)
                    cx.wait(nc.vector, "sv", bv)
                    for ccc in range(16):
                        nc.vector.tensor_tensor(out=K.f32a[:, 0:256],
                                                in0=att[:, ccc * 256:(ccc + 1) * 256],
                                                in1=K.rb[:, 0:256], op=ALU.mult)
                        nc.vector.tensor_tensor(out=K.f32b[:, 0:256], in0=K.f32a[:, 0:256],
                                                in1=K.mrb[:, 0:256], op=ALU.subtract)
                        v = nc.vector.tensor_tensor(out=xt[:, ccc * 256:(ccc + 1) * 256],
                                                    in0=xt[:, ccc * 256:(ccc + 1) * 256],
                                                    in1=K.f32b[:, 0:256], op=ALU.add)
                    lv = cx.inc(v, "sv")
                    cx.wait(nc.sync, "sv", lv)
                    d = nc.sync.dma_start(out=feat_tile(xout, 0, C, qt * 256, 256), in_=sb3(xt[:], 16, 256))
                    sv_ = cx.inc(d, "ss", 16)
                    cx.wait(nc.vector, "ss", sv_)
            cx.wait(nc.sync, "ss")
    return nc


def build_lf1():
    nc = bass.Bass("TRN2", target_bir_lowering=False)
    xmy = nc.dram_tensor("xmy", [C, MYT], F32, kind="ExternalInput").ap()
    w1t = nc.dram_tensor("w1t", [C, 4096], F32, kind="ExternalInput").ap()
    w2t = nc.dram_tensor("w2t", [4096, 4096], F32, kind="ExternalInput").ap()
    w3t = nc.dram_tensor("w3t", [4096, C], F32, kind="ExternalInput").ap()
    b1 = nc.dram_tensor("b1", [128, 32], F32, kind="ExternalInput").ap()
    b2 = nc.dram_tensor("b2", [128, 32], F32, kind="ExternalInput").ap()
    b3 = nc.dram_tensor("b3", [128, 16], F32, kind="ExternalInput").ap()
    xout = nc.dram_tensor("xout", [C, MYT], F32, kind="ExternalOutput").ap()

    st = ExitStack()
    with st:
        cx = Ctx(nc, st, ("sw", "sg", "st", "sv", "sa", "ss"))
        K = LNKit(nc, cx, st)
        BAR = (nc.tensor, nc.vector, nc.scalar, nc.sync, nc.gpsimd)
        sbb = lambda name, shape, dt=BF16: st.enter_context(nc.sbuf_tensor(name, shape, dt))
        b1s = st.enter_context(nc.sbuf_tensor("b1s", [128, 32], F32))
        b2s = st.enter_context(nc.sbuf_tensor("b2s", [128, 32], F32))
        b3s = st.enter_context(nc.sbuf_tensor("b3s", [128, 16], F32))
        for ap_, t_ in ((b1, b1s), (b2, b2s), (b3, b3s)):
            d = nc.sync.dma_start(out=t_[:], in_=ap_)
            cx.inc(d, "sw", 16)
        h1 = sbb("h1", [128, 32 * 1024])
        h2 = sbb("h2", [128, 32 * 1024])
        scratch = sbb("scrF1", [128, 16384])
        xt = scratch[:, 0:8192].bitcast(F32)
        xb = scratch[:, 8192:12288]
        xsq = scratch[:, 12288:16384]
        with (nc.psum_tensor("pF0", [128, 512], F32) as pf0,
              nc.psum_tensor("pF1", [128, 512], F32) as pf1,
              nc.psum_tensor("psmF", [128, 512], F32) as psm,
              nc.psum_tensor("psqF", [128, 512], F32) as psq,
              nc.psum_tensor("pbcF", [128, 512], F32) as pbc):
            pfs = [pf0, pf1]
            for th in range(2):
                with ExitStack() as sA:
                    lnx7 = h2[:, 0:16 * 1024]
                    wbuf = sA.enter_context(nc.sbuf_tensor(f"wbufF{th}", [128, 16 * 512], BF16))
                    cx.barrier(BAR)
                    for tt in range(4):
                        t0 = th * 1024 + tt * 256
                        cx.wait(nc.sync, "st")
                        cx.wait(nc.sync, "sv")
                        d = nc.sync.dma_start(out=sb3(xt[:], 16, 256), in_=feat_tile(xmy, 0, C, t0, 256))
                        wv = cx.inc(d, "sw", 16)
                        cx.wait(nc.vector, "sw", wv)
                        cx.wait(nc.vector, "st")
                        for ccc in range(16):
                            v = nc.vector.tensor_copy(out=xb[:, ccc * 256:(ccc + 1) * 256],
                                                      in_=xt[:, ccc * 256:(ccc + 1) * 256])
                        vv = cx.inc(v, "sv")
                        cx.wait(nc.scalar, "sv", vv)
                        for ccc in range(16):
                            a = nc.scalar.activation(out=xsq[:, ccc * 256:(ccc + 1) * 256],
                                                     in_=xb[:, ccc * 256:(ccc + 1) * 256], func=AF.Square)
                        av = cx.inc(a, "sa")
                        cx.wait(nc.tensor, "sa", av)
                        cx.wait(nc.tensor, "sv", vv)
                        bv = K.ln_tile(lambda ccc: xb[:, ccc * 256:(ccc + 1) * 256],
                                       lambda ccc: xsq[:, ccc * 256:(ccc + 1) * 256],
                                       psm, psq, pbc, 256, 16, 2048)
                        cx.wait(nc.vector, "sv", bv)
                        for ccc in range(16):
                            nc.vector.tensor_tensor(out=K.f32a[:, 0:256],
                                                    in0=xt[:, ccc * 256:(ccc + 1) * 256],
                                                    in1=K.rb[:, 0:256], op=ALU.mult)
                            v = nc.vector.tensor_tensor(
                                out=lnx7[:, ccc * 1024 + tt * 256: ccc * 1024 + tt * 256 + 256],
                                in0=K.f32a[:, 0:256], in1=K.mrb[:, 0:256], op=ALU.subtract)
                        cx.inc(v, "sv")
                    cx.barrier(BAR)
                    for jg in range(8):
                        cx.wait(nc.gpsimd, "st")
                        d = nc.gpsimd.dma_start(out=sb3(wbuf[:], 16, 512),
                                                in_=w1t[:, jg * 512:(jg + 1) * 512]
                                                .rearrange("(cc p) j -> p cc j", p=128))
                        gv = cx.inc(d, "sg", 16)
                        cx.wait(nc.tensor, "sg", gv)
                        for jb in range(4):
                            jc = jg * 4 + jb
                            for half in range(2):
                                for ccc in range(16):
                                    mm = nc.tensor.matmul(
                                        pfs[half][:, 0:512],
                                        wbuf[:, ccc * 512 + jb * 128: ccc * 512 + jb * 128 + 128],
                                        lnx7[:, ccc * 1024 + half * 512: ccc * 1024 + half * 512 + 512],
                                        start=(ccc == 0), stop=(ccc == 15))
                            tv = cx.inc(mm, "st")
                            cx.wait(nc.scalar, "st", tv)
                            for half in range(2):
                                a = nc.scalar.activation(
                                    out=h1[:, jc * 1024 + half * 512: jc * 1024 + half * 512 + 512],
                                    in_=pfs[half][:, 0:512], func=AF.Relu, bias=b1s[:, jc:jc + 1])
                            av = cx.inc(a, "sa")
                            cx.wait(nc.tensor, "sa", av)
                with ExitStack() as sB:
                    w2buf = sB.enter_context(nc.sbuf_tensor(f"w2buf{th}", [128, 32 * 256], BF16))
                    cx.barrier(BAR)
                    for jg in range(16):
                        cx.wait(nc.gpsimd, "st")
                        d = nc.gpsimd.dma_start(out=sb3(w2buf[:], 32, 256),
                                                in_=w2t[:, jg * 256:(jg + 1) * 256]
                                                .rearrange("(cc p) j -> p cc j", p=128))
                        gv = cx.inc(d, "sg", 16)
                        cx.wait(nc.tensor, "sg", gv)
                        for jb in range(2):
                            jc = jg * 2 + jb
                            for half in range(2):
                                for ccc in range(32):
                                    mm = nc.tensor.matmul(
                                        pfs[half][:, 0:512],
                                        w2buf[:, ccc * 256 + jb * 128: ccc * 256 + jb * 128 + 128],
                                        h1[:, ccc * 1024 + half * 512: ccc * 1024 + half * 512 + 512],
                                        start=(ccc == 0), stop=(ccc == 31))
                            tv = cx.inc(mm, "st")
                            cx.wait(nc.scalar, "st", tv)
                            for half in range(2):
                                a = nc.scalar.activation(
                                    out=h2[:, jc * 1024 + half * 512: jc * 1024 + half * 512 + 512],
                                    in_=pfs[half][:, 0:512], func=AF.Relu, bias=b2s[:, jc:jc + 1])
                            av = cx.inc(a, "sa")
                            cx.wait(nc.tensor, "sa", av)
                with ExitStack() as sC:
                    w3buf = sC.enter_context(nc.sbuf_tensor(f"w3buf{th}", [128, 32 * 256], BF16))
                    ffno = h1[:, 0:16 * 1024]
                    cx.barrier(BAR)
                    for cg in range(8):
                        cx.wait(nc.gpsimd, "st")
                        d = nc.gpsimd.dma_start(out=sb3(w3buf[:], 32, 256),
                                                in_=w3t[:, cg * 256:(cg + 1) * 256]
                                                .rearrange("(cc p) j -> p cc j", p=128))
                        gv = cx.inc(d, "sg", 16)
                        cx.wait(nc.tensor, "sg", gv)
                        for cb in range(2):
                            ccx = cg * 2 + cb
                            for half in range(2):
                                for ccc in range(32):
                                    mm = nc.tensor.matmul(
                                        pfs[half][:, 0:512],
                                        w3buf[:, ccc * 256 + cb * 128: ccc * 256 + cb * 128 + 128],
                                        h2[:, ccc * 1024 + half * 512: ccc * 1024 + half * 512 + 512],
                                        start=(ccc == 0), stop=(ccc == 31))
                            tv = cx.inc(mm, "st")
                            cx.wait(nc.scalar, "st", tv)
                            for half in range(2):
                                a = nc.scalar.activation(
                                    out=ffno[:, ccx * 1024 + half * 512: ccx * 1024 + half * 512 + 512],
                                    in_=pfs[half][:, 0:512], func=AF.Identity, bias=b3s[:, ccx:ccx + 1])
                            av = cx.inc(a, "sa")
                            cx.wait(nc.tensor, "sa", av)
                    cx.barrier(BAR)
                    for tt in range(4):
                        t0 = th * 1024 + tt * 256
                        cx.wait(nc.scalar, "sv")
                        for ccc in range(16):
                            a = nc.scalar.activation(
                                out=xsq[:, ccc * 256:(ccc + 1) * 256],
                                in_=ffno[:, ccc * 1024 + tt * 256: ccc * 1024 + tt * 256 + 256],
                                func=AF.Square)
                        av = cx.inc(a, "sa")
                        cx.wait(nc.tensor, "sa", av)
                        bv = K.ln_tile(lambda ccc: ffno[:, ccc * 1024 + tt * 256: ccc * 1024 + tt * 256 + 256],
                                       lambda ccc: xsq[:, ccc * 256:(ccc + 1) * 256],
                                       psm, psq, pbc, 256, 16, 2048)
                        cx.wait(nc.sync, "sv")
                        d = nc.sync.dma_start(out=sb3(xt[:], 16, 256), in_=feat_tile(xmy, 0, C, t0, 256))
                        wv = cx.inc(d, "sw", 16)
                        cx.wait(nc.vector, "sw", wv)
                        cx.wait(nc.vector, "sv", bv)
                        for ccc in range(16):
                            nc.vector.tensor_tensor(out=K.f32a[:, 0:256],
                                                    in0=ffno[:, ccc * 1024 + tt * 256: ccc * 1024 + tt * 256 + 256],
                                                    in1=K.rb[:, 0:256], op=ALU.mult)
                            nc.vector.tensor_tensor(out=K.f32b[:, 0:256], in0=K.f32a[:, 0:256],
                                                    in1=K.mrb[:, 0:256], op=ALU.subtract)
                            v = nc.vector.tensor_tensor(out=xt[:, ccc * 256:(ccc + 1) * 256],
                                                        in0=xt[:, ccc * 256:(ccc + 1) * 256],
                                                        in1=K.f32b[:, 0:256], op=ALU.add)
                        lv = cx.inc(v, "sv")
                        cx.wait(nc.sync, "sv", lv)
                        d = nc.sync.dma_start(out=feat_tile(xout, 0, C, t0, 256), in_=sb3(xt[:], 16, 256))
                        sv_ = cx.inc(d, "ss", 16)
                        cx.wait(nc.vector, "ss", sv_)
        cx.wait(nc.sync, "ss")
    return nc




RG_PAIRS = [[0, 1], [2, 3], [4, 5], [6, 7]]


def build_fused(debug=False):
    nc = bass.Bass("TRN2", target_bir_lowering=False)
    xmy = nc.dram_tensor("xmy", [C, MYT], F32, kind="ExternalInput").ap()
    wq_t = nc.dram_tensor("wq_t", [C, C], F32, kind="ExternalInput").ap()
    wq_n = nc.dram_tensor("wq_n", [C, C], F32, kind="ExternalInput").ap()
    wdkv_t = nc.dram_tensor("wdkv_t", [C, NL], F32, kind="ExternalInput").ap()
    wuk = nc.dram_tensor("wuk", [C, NL], F32, kind="ExternalInput").ap()
    wuv = nc.dram_tensor("wuv", [C, NL], F32, kind="ExternalInput").ap()
    wo_t = nc.dram_tensor("wo_t", [C, C], F32, kind="ExternalInput").ap()
    masks0 = nc.dram_tensor("masks0", [4 * 128, 512], F32, kind="ExternalInput").ap()
    ident = nc.dram_tensor("ident", [128, 128], F32, kind="ExternalInput").ap()
    up_t = nc.dram_tensor("up_t", [C, 8192], F32, kind="ExternalInput").ap()
    down_b = nc.dram_tensor("down_b", [16, 64, 128, 128], F32, kind="ExternalInput").ap()
    usc = nc.dram_tensor("usc", [128, 64], F32, kind="ExternalInput").ap()
    dsc = nc.dram_tensor("dsc", [128, 16], F32, kind="ExternalInput").ap()
    osc = nc.dram_tensor("osc", [1], F32, kind="ExternalInput").ap()
    wqkv = nc.dram_tensor("wqkv", [C, 3 * C], F32, kind="ExternalInput").ap()
    wout = nc.dram_tensor("wout", [C, C], F32, kind="ExternalInput").ap()
    masks1 = nc.dram_tensor("masks1", [4 * 128, 256], F32, kind="ExternalInput").ap()
    w1t = nc.dram_tensor("w1t", [C, 4096], F32, kind="ExternalInput").ap()
    w2t = nc.dram_tensor("w2t", [4096, 4096], F32, kind="ExternalInput").ap()
    w3t = nc.dram_tensor("w3t", [4096, C], F32, kind="ExternalInput").ap()
    b1 = nc.dram_tensor("b1", [128, 32], F32, kind="ExternalInput").ap()
    b2 = nc.dram_tensor("b2", [128, 32], F32, kind="ExternalInput").ap()
    b3 = nc.dram_tensor("b3", [128, 16], F32, kind="ExternalInput").ap()
    xout = nc.dram_tensor("xout", [C, MYT], F32, kind="ExternalOutput").ap()
    kind = dict(kind="ExternalOutput") if debug else {}
    lnmy_d = nc.dram_tensor("lnmy_d", [C, MYT], BF16).ap()
    x_d = nc.dram_tensor("x_d", [C, MYT], F32, **kind).ap()
    qlat_d = nc.dram_tensor("qlat_d", [128, 8 * 4 * NH * 256], BF16).ap()
    y_d = nc.dram_tensor("y_d", [C, MYT], BF16).ap()
    ag0_in = nc.dram_tensor("ag0_in", [MYT, NL], BF16)
    ag0_out = nc.dram_tensor("ag0_out", [T, NL], BF16)
    ag1_in = nc.dram_tensor("ag1_in", [MYT, 2 * C], BF16)
    ag1_out = nc.dram_tensor("ag1_out", [T, 2 * C], BF16)
    upsg_d = nc.dram_tensor("upsg_d", [128, 16 * 8192], BF16).ap()
    dnsg_d = nc.dram_tensor("dnsg_d", [128, 8192 * 16], BF16).ap()

    st = ExitStack()
    with st:
        cx = Ctx(nc, st, ("sw", "sg", "st", "sv", "sa", "ss", "sc"))
        K = LNKit(nc, cx, st)
        BAR = (nc.tensor, nc.vector, nc.scalar, nc.sync, nc.gpsimd)
        sbb = lambda name, shape, dt=BF16: st.enter_context(nc.sbuf_tensor(name, shape, dt))
        m0_sb = sbb("m0_sb", [128, 4 * 512])
        m1_sb = sbb("m1_sb", [128, 4 * 256])
        id_sb = sbb("id_sb", [128, 128])
        d = nc.gpsimd.dma_start(out=sb3(m0_sb[:], 4, 512), in_=masks0.rearrange("(m p) j -> p m j", p=128))
        cx.inc(d, "sg", 16)
        d = nc.gpsimd.dma_start(out=sb3(m1_sb[:], 4, 256), in_=masks1.rearrange("(m p) j -> p m j", p=128))
        cx.inc(d, "sg", 16)
        d = nc.gpsimd.dma_start(out=id_sb[:], in_=ident)
        cx.inc(d, "sg", 16)

        def emit_ln_pass(x_src, ln_dst, xt, xb, xsq, psm, psq, pbc, ntiles=4):
            for tt in range(ntiles):
                cx.wait(nc.sync, "st")
                cx.wait(nc.sync, "sv")
                d = nc.sync.dma_start(out=sb3(xt[:], 16, 512), in_=feat_tile(x_src, 0, C, tt * 512, 512))
                wv = cx.inc(d, "sw", 16)
                cx.wait(nc.vector, "sw", wv)
                cx.wait(nc.vector, "st")
                for ccc in range(16):
                    v = nc.vector.tensor_copy(out=xb[:, ccc * 512:(ccc + 1) * 512],
                                              in_=xt[:, ccc * 512:(ccc + 1) * 512])
                vv = cx.inc(v, "sv")
                cx.wait(nc.scalar, "sv", vv)
                for ccc in range(16):
                    a = nc.scalar.activation(out=xsq[:, ccc * 512:(ccc + 1) * 512],
                                             in_=xb[:, ccc * 512:(ccc + 1) * 512], func=AF.Square)
                av = cx.inc(a, "sa")
                cx.wait(nc.tensor, "sa", av)
                cx.wait(nc.tensor, "sv", vv)
                bv = K.ln_tile(lambda ccc: xb[:, ccc * 512:(ccc + 1) * 512],
                               lambda ccc: xsq[:, ccc * 512:(ccc + 1) * 512],
                               psm, psq, pbc, 512, 16, 2048)
                cx.wait(nc.vector, "sv", bv)
                for ccc in range(16):
                    nc.vector.tensor_tensor(out=K.f32a[:], in0=xt[:, ccc * 512:(ccc + 1) * 512],
                                            in1=K.rb[:], op=ALU.mult)
                    v = nc.vector.tensor_tensor(out=xb[:, ccc * 512:(ccc + 1) * 512],
                                                in0=K.f32a[:], in1=K.mrb[:], op=ALU.subtract)
                lv = cx.inc(v, "sv")
                cx.wait(nc.sync, "sv", lv)
                d = nc.sync.dma_start(out=feat_tile(ln_dst, 0, C, tt * 512, 512), in_=sb3(xb[:], 16, 512))
                sv_ = cx.inc(d, "ss", 16)
                cx.wait(nc.vector, "ss", sv_)

        def emit_combine(y_get, ysq_get, x_src, x_dst, ln_dst, xt, xb, xsq, psm, psq, pbc,
                         tilew, ntiles, base_t=0):
            jw = tilew
            for tt in range(ntiles):
                t0 = base_t + tt * jw
                cx.wait(nc.scalar, "sv")
                cx.wait(nc.scalar, "st")
                for ccc in range(16):
                    a = nc.scalar.activation(out=xsq[:, ccc * jw:(ccc + 1) * jw],
                                             in_=y_get(ccc, tt), func=AF.Square)
                av = cx.inc(a, "sa")
                cx.wait(nc.tensor, "sa", av)
                bv = K.ln_tile(lambda ccc: y_get(ccc, tt),
                               lambda ccc: xsq[:, ccc * jw:(ccc + 1) * jw],
                               psm, psq, pbc, jw, 16, 2048)
                cx.wait(nc.sync, "sv")
                d = nc.sync.dma_start(out=sb3(xt[:, 0:16 * jw], 16, jw), in_=feat_tile(x_src, 0, C, t0, jw))
                wv = cx.inc(d, "sw", 16)
                cx.wait(nc.vector, "sw", wv)
                cx.wait(nc.vector, "sv", bv)
                for ccc in range(16):
                    nc.vector.tensor_tensor(out=K.f32a[:, 0:jw], in0=y_get(ccc, tt),
                                            in1=K.rb[:, 0:jw], op=ALU.mult)
                    nc.vector.tensor_tensor(out=K.f32b[:, 0:jw], in0=K.f32a[:, 0:jw],
                                            in1=K.mrb[:, 0:jw], op=ALU.subtract)
                    v = nc.vector.tensor_tensor(out=xt[:, ccc * jw:(ccc + 1) * jw],
                                                in0=xt[:, ccc * jw:(ccc + 1) * jw],
                                                in1=K.f32b[:, 0:jw], op=ALU.add)
                lv = cx.inc(v, "sv")
                cx.wait(nc.sync, "sv", lv)
                d = nc.sync.dma_start(out=feat_tile(x_dst, 0, C, t0, jw), in_=sb3(xt[:, 0:16 * jw], 16, jw))
                sv_ = cx.inc(d, "ss", 16)
                if ln_dst is None:
                    cx.wait(nc.vector, "ss", sv_)
                    continue
                cx.wait(nc.vector, "st")
                for ccc in range(16):
                    v = nc.vector.tensor_copy(out=xb[:, ccc * jw:(ccc + 1) * jw],
                                              in_=xt[:, ccc * jw:(ccc + 1) * jw])
                vv = cx.inc(v, "sv")
                cx.wait(nc.scalar, "sv", vv)
                for ccc in range(16):
                    a = nc.scalar.activation(out=xsq[:, ccc * jw:(ccc + 1) * jw],
                                             in_=xb[:, ccc * jw:(ccc + 1) * jw], func=AF.Square)
                av = cx.inc(a, "sa")
                cx.wait(nc.tensor, "sa", av)
                cx.wait(nc.tensor, "sv", vv)
                bv = K.ln_tile(lambda ccc: xb[:, ccc * jw:(ccc + 1) * jw],
                               lambda ccc: xsq[:, ccc * jw:(ccc + 1) * jw],
                               psm, psq, pbc, jw, 16, 2048)
                cx.wait(nc.vector, "sv", bv)
                for ccc in range(16):
                    nc.vector.tensor_tensor(out=K.f32a[:, 0:jw], in0=xt[:, ccc * jw:(ccc + 1) * jw],
                                            in1=K.rb[:, 0:jw], op=ALU.mult)
                    v = nc.vector.tensor_tensor(out=xb[:, ccc * jw:(ccc + 1) * jw],
                                                in0=K.f32a[:, 0:jw], in1=K.mrb[:, 0:jw], op=ALU.subtract)
                lv = cx.inc(v, "sv")
                cx.wait(nc.sync, "sv", lv)
                cx.wait(nc.sync, "ss", sv_)
                d = nc.sync.dma_start(out=feat_tile(ln_dst, 0, C, t0, jw), in_=sb3(xb[:, 0:16 * jw], 16, jw))
                sv_ = cx.inc(d, "ss", 16)
                cx.wait(nc.vector, "ss", sv_)

        with ExitStack() as sA:
            keff = sA.enter_context(nc.sbuf_tensor("keff", [128, NH * NL], BF16))
            veff = sA.enter_context(nc.sbuf_tensor("veff", [128, 4 * NH * 128], BF16))
            with ExitStack() as s0:
                wsm = s0.enter_context(nc.sbuf_tensor("wsm0", [128, 16 * 512], BF16))
                wbig = s0.enter_context(nc.sbuf_tensor("wbig0", [128, 16 * 2048], BF16))
                d = nc.gpsimd.dma_start(out=sb3(wsm[:], 16, 512), in_=wuv.rearrange("(cc p) l -> p cc l", p=128))
                cx.inc(d, "sg", 16)
                d = nc.gpsimd.dma_start(out=sb3(wbig[:], 16, 2048), in_=wo_t.rearrange("(cc p) o -> p cc o", p=128))
                cx.inc(d, "sg", 16)
                cx.wait(nc.tensor, "sg")
                with nc.psum_tensor("pv0", [128, 512], F32) as pv:
                    for lc in range(4):
                        for hq in range(4):
                            for ccc in range(16):
                                mm = nc.tensor.matmul(
                                    pv[:, 0:512],
                                    wsm[:, ccc * 512 + lc * 128: ccc * 512 + lc * 128 + 128],
                                    wbig[:, ccc * 2048 + hq * 512: ccc * 2048 + hq * 512 + 512],
                                    start=(ccc == 0), stop=(ccc == 15))
                            tv = cx.inc(mm, "st")
                            cx.wait(nc.scalar, "st", tv)
                            a = nc.scalar.activation(
                                out=veff[:, lc * 2048 + hq * 512: lc * 2048 + hq * 512 + 512],
                                in_=pv[:, 0:512], func=AF.Copy)
                            av = cx.inc(a, "sa")
                            cx.wait(nc.tensor, "sa", av)
                cx.wait(nc.gpsimd, "st")
                d = nc.gpsimd.dma_start(out=sb3(wsm[:], 16, 512), in_=wuk.rearrange("(cc p) l -> p cc l", p=128))
                cx.inc(d, "sg", 16)
                d = nc.gpsimd.dma_start(out=sb3(wbig[:], 16, 2048), in_=wq_n.rearrange("(cc p) o -> p cc o", p=128))
                cx.inc(d, "sg", 16)
                cx.wait(nc.tensor, "sg")
                with nc.psum_tensor("pk0", [128, 512], F32) as pk:
                    for h in range(NH):
                        for ccc in range(16):
                            mm = nc.tensor.matmul(
                                pk[:, 0:512],
                                wbig[:, ccc * 2048 + h * 128: ccc * 2048 + h * 128 + 128],
                                wsm[:, ccc * 512: ccc * 512 + 512],
                                start=(ccc == 0), stop=(ccc == 15))
                        tv = cx.inc(mm, "st")
                        cx.wait(nc.scalar, "st", tv)
                        a = nc.scalar.activation(out=keff[:, h * 512:(h + 1) * 512],
                                                 in_=pk[:, 0:512], func=AF.Copy, scale=ISQ)
                        av = cx.inc(a, "sa")
                        cx.wait(nc.tensor, "sa", av)
                with ExitStack() as s1:
                    xt = s1.enter_context(nc.sbuf_tensor("xt1", [128, 16 * 512], F32))
                    xb = s1.enter_context(nc.sbuf_tensor("xb1", [128, 16 * 512], BF16))
                    xsq = s1.enter_context(nc.sbuf_tensor("xsq1", [128, 16 * 512], BF16))
                    ckt = s1.enter_context(nc.sbuf_tensor("ckt1", [128, 512], BF16))
                    with (nc.psum_tensor("psm1", [128, 512], F32) as psm,
                          nc.psum_tensor("psq1", [128, 512], F32) as psq,
                          nc.psum_tensor("pbc1", [128, 512], F32) as pbc):
                        cx.barrier(BAR)
                        emit_ln_pass(xmy, lnmy_d, xt, xb, xsq, psm, psq, pbc)
                        cx.wait(nc.gpsimd, "st")
                        d = nc.gpsimd.dma_start(out=sb3(wsm[:], 16, 512),
                                                in_=wdkv_t.rearrange("(cc p) l -> p cc l", p=128))
                        gv = cx.inc(d, "sg", 16)
                        cx.wait(nc.tensor, "sg", gv)
                        for tt in range(4):
                            cx.wait(nc.sync, "st")
                            d = nc.sync.dma_start(out=sb3(xb[:], 16, 512),
                                                  in_=feat_tile(lnmy_d, 0, C, tt * 512, 512))
                            wv = cx.inc(d, "sw", 16)
                            cx.wait(nc.tensor, "sw", wv)
                            for tc in range(4):
                                for ccc in range(16):
                                    mm = nc.tensor.matmul(
                                        pbc[:, 0:512],
                                        xb[:, ccc * 512 + tc * 128: ccc * 512 + tc * 128 + 128],
                                        wsm[:, ccc * 512: ccc * 512 + 512],
                                        start=(ccc == 0), stop=(ccc == 15))
                                tv = cx.inc(mm, "st")
                                cx.wait(nc.scalar, "st", tv)
                                a = nc.scalar.activation(out=ckt[:], in_=pbc[:, 0:512], func=AF.Copy)
                                av = cx.inc(a, "sa")
                                cx.wait(nc.tensor, "sa", av)
                                cx.wait(nc.sync, "sa", av)
                                d = nc.sync.dma_start(
                                    out=ag0_in.ap()[tt * 512 + tc * 128: tt * 512 + tc * 128 + 128, :],
                                    in_=ckt[:])
                                sv_ = cx.inc(d, "ss", 16)
                                cx.wait(nc.scalar, "ss", sv_)
                        cx.wait(nc.gpsimd, "ss")
                        cc_ = nc.gpsimd.collective_compute(
                            "AllGather", ALU.bypass, replica_groups=RG_PAIRS,
                            ins=[ag0_in.ap().bitcast(F32).opt()],
                            outs=[ag0_out.ap().bitcast(F32).opt()])
                        cx.inc(cc_, "sc")
            with ExitStack() as s2:
                wbig = s2.enter_context(nc.sbuf_tensor("wbig2", [128, 16 * 2048], BF16))
                lnq = s2.enter_context(nc.sbuf_tensor("lnq2", [128, 16 * 256], BF16))
                qlat = s2.enter_context(nc.sbuf_tensor("qlat2", [128, 4 * NH * 256], BF16))
                qh = s2.enter_context(nc.sbuf_tensor("qh2", [128, 256], BF16))
                cx.wait(nc.gpsimd, "st")
                cx.wait(nc.gpsimd, "ss")
                d = nc.gpsimd.dma_start(out=sb3(wbig[:], 16, 2048), in_=wq_t.rearrange("(cc p) o -> p cc o", p=128))
                cx.inc(d, "sg", 16)
                cx.wait(nc.tensor, "sg")
                with (nc.psum_tensor("pq2", [128, 256], F32) as pq,
                      nc.psum_tensor("pql0", [128, 256], F32) as pql0,
                      nc.psum_tensor("pql1", [128, 256], F32) as pql1,
                      nc.psum_tensor("pql2", [128, 256], F32) as pql2,
                      nc.psum_tensor("pql3", [128, 256], F32) as pql3):
                    pqls = [pql0, pql1, pql2, pql3]
                    for qt in range(8):
                        cx.wait(nc.sync, "st")
                        d = nc.sync.dma_start(out=sb3(lnq[:], 16, 256), in_=feat_tile(lnmy_d, 0, C, qt * 256, 256))
                        wv = cx.inc(d, "sw", 16)
                        cx.wait(nc.tensor, "sw", wv)
                        prev_sa = None
                        for h in range(NH):
                            for ccc in range(16):
                                mm = nc.tensor.matmul(
                                    pq[:, 0:256],
                                    wbig[:, ccc * 2048 + h * 128: ccc * 2048 + h * 128 + 128],
                                    lnq[:, ccc * 256:(ccc + 1) * 256],
                                    start=(ccc == 0), stop=(ccc == 15))
                            tv = cx.inc(mm, "st")
                            cx.wait(nc.scalar, "st", tv)
                            a = nc.scalar.activation(out=qh[:], in_=pq[:, 0:256], func=AF.Copy)
                            av = cx.inc(a, "sa")
                            cx.wait(nc.tensor, "sa", av)
                            for lc in range(4):
                                mm = nc.tensor.matmul(
                                    pqls[lc][:, 0:256],
                                    keff[:, h * 512 + lc * 128: h * 512 + lc * 128 + 128],
                                    qh[:], start=True, stop=True)
                            tv = cx.inc(mm, "st")
                            cx.wait(nc.scalar, "st", tv)
                            for lc in range(4):
                                a = nc.scalar.activation(
                                    out=qlat[:, lc * (NH * 256) + h * 256: lc * (NH * 256) + h * 256 + 256],
                                    in_=pqls[lc][:, 0:256], func=AF.Copy)
                            prev_sa = cx.inc(a, "sa")
                        cx.wait(nc.sync, "sa")
                        d = nc.sync.dma_start(out=qlat_d[:, qt * (4 * NH * 256):(qt + 1) * (4 * NH * 256)],
                                              in_=qlat[:])
                        sv_ = cx.inc(d, "ss", 16)
                        cx.wait(nc.scalar, "ss", sv_)
            with ExitStack() as s3:
                cx.barrier(BAR)
                ckvl = s3.enter_context(nc.sbuf_tensor("ckvl3", [128, 4 * T], BF16))
                ckvt = s3.enter_context(nc.sbuf_tensor("ckvt3", [128, NBLK * NL], BF16))
                qlat = s3.enter_context(nc.sbuf_tensor("qlat3", [128, 4 * NH * 256], BF16))
                expb = s3.enter_context(nc.sbuf_tensor("expb3", [128, 3 * 512], BF16))
                ylat_sb = s3.enter_context(nc.sbuf_tensor("ylat3", [128, 4 * 512], BF16))
                yqt = s3.enter_context(nc.sbuf_tensor("yqt3", [128, NH * 256], BF16))
                acc = s3.enter_context(nc.sbuf_tensor("acc3", [128, 512], F32))
                sums = s3.enter_context(nc.sbuf_tensor("sums3", [1, 512], F32))
                cx.wait(nc.sync, "sc")
                d = nc.sync.dma_start(out=sb3(ckvt[:], NBLK, 512),
                                      in_=ag0_out.ap().rearrange("(s p) l -> p s l", p=128))
                wv = cx.inc(d, "sw", 16)
                for lc in range(4):
                    d = nc.sync.dma_start_transpose(ckvl[:, lc * T:(lc + 1) * T],
                                                    ag0_out.ap()[:, lc * 128:(lc + 1) * 128])
                    wv = cx.inc(d, "sw", 16)
                cx.wait(nc.tensor, "sw", wv)
                with (nc.psum_tensor("pl0", [128, 512], F32) as pl0,
                      nc.psum_tensor("pl1", [128, 512], F32) as pl1,
                      nc.psum_tensor("pl2", [128, 512], F32) as pl2,
                      nc.psum_tensor("pl3", [128, 512], F32) as pl3,
                      nc.psum_tensor("psc0", [128, 512], F32) as psc0,
                      nc.psum_tensor("psc1", [128, 512], F32) as psc1,
                      nc.psum_tensor("psc2", [128, 512], F32) as psc2,
                      nc.psum_tensor("paux", [128, 512], F32) as paux):
                    pls = [pl0, pl1, pl2, pl3]
                    pscs = [psc0, psc1, psc2]
                    for qt in range(8):
                        cx.wait(nc.sync, "st")
                        d = nc.sync.dma_start(out=qlat[:],
                                              in_=qlat_d[:, qt * (4 * NH * 256):(qt + 1) * (4 * NH * 256)])
                        wv = cx.inc(d, "sw", 16)
                        cx.wait(nc.tensor, "sw", wv)
                        kbs = [(0, u) for u in range(2 * qt + 2)] + [(1, u) for u in range(2 * qt + 2)]
                        nkb = len(kbs)
                        for hp in range(8):
                            h0 = 2 * hp
                            evs = {}
                            exs = {}
                            slots = {}

                            def emit_ylat0(kb):
                                cx.wait(nc.tensor, "sv", evs[kb])
                                for lc in range(4):
                                    mmy = nc.tensor.matmul(
                                        pls[lc][:, 0:512],
                                        ckvt[:, slots[kb] * 512 + lc * 128: slots[kb] * 512 + lc * 128 + 128],
                                        exs[kb], start=(kb == 0), stop=(kb == nkb - 1))
                                cx.inc(mmy, "st")

                            for kb, (half, u) in enumerate(kbs):
                                slot = 16 * half + u
                                slots[kb] = slot
                                psc = pscs[kb % 3]
                                for lc in range(4):
                                    mm = nc.tensor.matmul(
                                        psc[:, 0:512],
                                        ckvl[:, lc * T + slot * 128: lc * T + slot * 128 + 128],
                                        qlat[:, lc * (NH * 256) + h0 * 256: lc * (NH * 256) + h0 * 256 + 512],
                                        start=(lc == 0), stop=(lc == 3))
                                tv = cx.inc(mm, "st")
                                cx.wait(nc.scalar, "st", tv)
                                eslot = kb % 3
                                ex = expb[:, eslot * 512:(eslot + 1) * 512]
                                exs[kb] = ex
                                a = nc.scalar.activation(out=ex, in_=psc[:, 0:512], func=AF.Exp)
                                av = cx.inc(a, "sa")
                                cx.wait(nc.vector, "sa", av)
                                off = (2 * half + (u - 2 * qt)) if u >= 2 * qt else -1
                                if off >= 0:
                                    v = nc.vector.tensor_tensor(out=ex, in0=ex,
                                                                in1=m0_sb[:, off * 512:(off + 1) * 512],
                                                                op=ALU.mult)
                                    cx.inc(v, "sv")
                                if kb == 0:
                                    v = nc.vector.tensor_copy(out=acc[:], in_=ex)
                                else:
                                    v = nc.vector.tensor_tensor(out=acc[:], in0=acc[:], in1=ex, op=ALU.add)
                                evs[kb] = cx.inc(v, "sv")
                                if kb >= 2:
                                    emit_ylat0(kb - 2)
                            emit_ylat0(nkb - 2)
                            emit_ylat0(nkb - 1)
                            mm = nc.tensor.matmul(paux[0:1, 0:512], K.ones_col_f[:, 0:1], acc[:],
                                                  start=True, stop=True)
                            tv = cx.inc(mm, "st")
                            cx.wait(nc.scalar, "st", tv)
                            a = nc.scalar.activation(out=sums[0:1, 0:512], in_=paux[0:1, 0:512], func=AF.Ln)
                            cx.wait(nc.scalar, "sa", cx.inc(a, "sa"))
                            a = nc.scalar.activation(out=sums[0:1, 0:512], in_=sums[0:1, 0:512],
                                                     func=AF.Exp, scale=-1.0)
                            rv = cx.inc(a, "sa")
                            cx.wait(nc.tensor, "sa", rv)
                            mm = nc.tensor.matmul(paux[:, 0:512], K.ones_row[0:1, :],
                                                  sums[0:1, 0:512], start=True, stop=True)
                            tv = cx.inc(mm, "st")
                            cx.wait(nc.vector, "st", tv)
                            v = nc.vector.tensor_copy(out=K.rb[:], in_=paux[:, 0:512])
                            rv = cx.inc(v, "sv")
                            cx.wait(nc.scalar, "st", tv)
                            for lc in range(4):
                                a = nc.scalar.activation(out=ylat_sb[:, lc * 512:(lc + 1) * 512],
                                                         in_=pls[lc][:, 0:512], func=AF.Copy)
                            av = cx.inc(a, "sa")
                            cx.wait(nc.tensor, "sa", av)
                            cx.wait(nc.tensor, "sv", rv)
                            for hh in range(2):
                                h = h0 + hh
                                for lc in range(4):
                                    mm = nc.tensor.matmul(
                                        paux[:, 0:256],
                                        veff[:, lc * 2048 + h * 128: lc * 2048 + h * 128 + 128],
                                        ylat_sb[:, lc * 512 + hh * 256: lc * 512 + hh * 256 + 256],
                                        start=(lc == 0), stop=(lc == 3))
                                tv = cx.inc(mm, "st")
                                cx.wait(nc.vector, "st", tv)
                                v = nc.vector.tensor_tensor(
                                    out=yqt[:, h * 256:(h + 1) * 256],
                                    in0=paux[:, 0:256], in1=K.rb[:, hh * 256: hh * 256 + 256], op=ALU.mult)
                                yv = cx.inc(v, "sv")
                                cx.wait(nc.tensor, "sv", yv)
                        cx.wait(nc.sync, "sv")
                        d = nc.sync.dma_start(
                            out=y_d[:, qt * 256:(qt + 1) * 256].rearrange("(h p) t -> p h t", p=128),
                            in_=sb3(yqt[:], NH, 256))
                        sv_ = cx.inc(d, "ss", 16)
                        cx.wait(nc.vector, "ss", sv_)
        with ExitStack() as s4:
            cx.barrier(BAR)
            xt = s4.enter_context(nc.sbuf_tensor("xt4", [128, 16 * 512], F32))
            xb = s4.enter_context(nc.sbuf_tensor("xb4", [128, 16 * 512], BF16))
            xsq = s4.enter_context(nc.sbuf_tensor("xsq4", [128, 16 * 512], BF16))
            yb = s4.enter_context(nc.sbuf_tensor("yb4", [128, 16 * 512], BF16))
            with (nc.psum_tensor("psm4", [128, 512], F32) as psm,
                  nc.psum_tensor("psq4", [128, 512], F32) as psq,
                  nc.psum_tensor("pbc4", [128, 512], F32) as pbc):
                for tt4 in range(4):
                    cx.wait(nc.sync, "st")
                    cx.wait(nc.sync, "sa")
                    d = nc.sync.dma_start(out=sb3(yb[:], 16, 512), in_=feat_tile(y_d, 0, C, tt4 * 512, 512))
                    wv = cx.inc(d, "sw", 16)
                    cx.wait(nc.scalar, "sw", wv)
                    cx.wait(nc.tensor, "sw", wv)
                    emit_combine(lambda ccc, _t: yb[:, ccc * 512:(ccc + 1) * 512],
                                 None, xmy, x_d, lnmy_d, xt, xb, xsq, psm, psq, pbc,
                                 512, 1, base_t=tt4 * 512)

        with ExitStack() as sF:
            cx.barrier(BAR)
            us_sb = sF.enter_context(nc.sbuf_tensor("us_sbF", [128, 64], F32))
            ds2 = sF.enter_context(nc.sbuf_tensor("ds2F", [128, 16], F32))
            osb = sF.enter_context(nc.sbuf_tensor("osbF", [1, 1], F32))
            osc_b = sF.enter_context(nc.sbuf_tensor("osc_bF", [128, 1], F32))
            lnx3 = sF.enter_context(nc.sbuf_tensor("lnx3F", [128, 16 * 1024], BF16))
            hid = sF.enter_context(nc.sbuf_tensor("hidF", [128, 32 * 1024], BF16))
            ffnacc = sF.enter_context(nc.sbuf_tensor("ffnaccF", [128, 16 * 1024], BF16))
            scratch = sF.enter_context(nc.sbuf_tensor("scratchF", [128, 16384], BF16))
            upw = scratch[:, 0:8192]
            upsg = scratch[:, 8192:16384]
            dnw = scratch[:, 0:8192]
            dnsg = scratch[:, 8192:16384]
            xtf = scratch[:, 0:8192].bitcast(F32)
            xbf = scratch[:, 8192:12288]
            xsqf = scratch[:, 12288:16384]
            xt512 = hid[:, 0:16384].bitcast(F32)
            xb512 = hid[:, 16384:24576]
            xsq512 = hid[:, 24576:32768]
            d = nc.sync.dma_start(out=us_sb[:], in_=usc)
            cx.inc(d, "sw", 16)
            d = nc.sync.dma_start(out=ds2[:], in_=dsc)
            cx.inc(d, "sw", 16)
            d = nc.sync.dma_start(out=osb[:], in_=osc.rearrange("(a o) -> a o", a=1))
            cx.inc(d, "sw", 16)
            with (nc.psum_tensor("ph0F", [128, 512], F32) as ph0,
                  nc.psum_tensor("ph1F", [128, 512], F32) as ph1,
                  nc.psum_tensor("ph2F", [128, 512], F32) as ph2,
                  nc.psum_tensor("ph3F", [128, 512], F32) as ph3,
                  nc.psum_tensor("psmF0", [128, 512], F32) as psm,
                  nc.psum_tensor("psqF0", [128, 512], F32) as psq,
                  nc.psum_tensor("pbcF0", [128, 512], F32) as pbc):
                phs = [ph0, ph1]
                phs2 = [ph2, ph3]
                cx.wait(nc.tensor, "sw")
                cx.wait(nc.tensor, "sv")
                mm = nc.tensor.matmul(pbc[:, 0:1], K.ones_row[0:1, :], osb[0:1, 0:1], start=True, stop=True)
                tv = cx.inc(mm, "st")
                cx.wait(nc.vector, "st", tv)
                v = nc.vector.tensor_copy(out=osc_b[:], in_=pbc[:, 0:1])
                vv = cx.inc(v, "sv")
                cx.wait(nc.vector, "sv", vv)
                v = nc.vector.tensor_scalar(out=ds2[:], in0=ds2[:], scalar1=osc_b[:, 0:1],
                                            scalar2=None, op0=ALU.mult)
                cx.inc(v, "sv")
                for th in range(2):
                    cx.barrier(BAR)
                    d = nc.sync.dma_start(out=sb3(lnx3[:], 16, 1024),
                                          in_=feat_tile(lnmy_d, 0, C, th * 1024, 1024))
                    wv = cx.inc(d, "sw", 16)
                    cx.wait(nc.tensor, "sw", wv)
                    cx.wait(nc.scalar, "sw", wv)
                    for jh in range(2):
                        cx.barrier(BAR)
                        jgN = 16
                        sgv = {}

                        def up_load(jg):
                            j0 = jh * 4096 + jg * 256
                            sl = jg % 2
                            d = nc.gpsimd.dma_start(
                                out=sb3(upw[:, sl * 4096:(sl + 1) * 4096], 16, 256),
                                in_=up_t[:, j0:j0 + 256].rearrange("(cc p) j -> p cc j", p=128))
                            sgv[jg] = cx.inc(d, "sg", 16)

                        up_load(0)
                        mmv = {}
                        for jg in range(jgN):
                            sl = jg % 2
                            cx.wait(nc.scalar, "sg", sgv[jg])
                            if jg >= 2:
                                cx.wait(nc.scalar, "st", mmv[jg - 2])
                            for ccc in range(16):
                                a = nc.scalar.activation(
                                    out=upsg[:, sl * 4096 + ccc * 256: sl * 4096 + ccc * 256 + 256],
                                    in_=upw[:, sl * 4096 + ccc * 256: sl * 4096 + ccc * 256 + 256],
                                    func=AF.Sign)
                            av = cx.inc(a, "sa")
                            if jg + 1 < jgN:
                                cx.wait(nc.gpsimd, "sa", av)
                                up_load(jg + 1)
                            cx.wait(nc.tensor, "sa", av)
                            cx.wait(nc.tensor, "sv")
                            for jb in range(2):
                                jloc = jg * 2 + jb
                                pp = phs if jloc % 2 == 0 else phs2
                                for half in range(2):
                                    for ccc in range(16):
                                        mm = nc.tensor.matmul(
                                            pp[half][:, 0:512],
                                            upsg[:, sl * 4096 + ccc * 256 + jb * 128: sl * 4096 + ccc * 256 + jb * 128 + 128],
                                            lnx3[:, ccc * 1024 + half * 512: ccc * 1024 + half * 512 + 512],
                                            start=(ccc == 0), stop=(ccc == 15))
                                tv = cx.inc(mm, "st")
                                mmv[jg] = tv
                                cx.wait(nc.scalar, "st", tv)
                                for half in range(2):
                                    a = nc.scalar.activation(
                                        out=hid[:, jloc * 1024 + half * 512: jloc * 1024 + half * 512 + 512],
                                        in_=pp[half][:, 0:512], func=AF.Relu,
                                        scale=us_sb[:, jh * 32 + jloc: jh * 32 + jloc + 1])
                                av = cx.inc(a, "sa")
                        cx.barrier(BAR)
                        dgv = {}
                        dmmv = {}

                        def dn_load(cg):
                            sl = cg % 2
                            d = nc.gpsimd.dma_start(
                                out=dnw[:, sl * 4096:(sl + 1) * 4096].rearrange("p (jb g) -> p jb g", g=128),
                                in_=down_b[cg, jh * 32:(jh + 1) * 32].rearrange("jb p g -> p jb g"))
                            dgv[cg] = cx.inc(d, "sg", 16)

                        dn_load(0)
                        for cg in range(16):
                            sl = cg % 2
                            cx.wait(nc.scalar, "sg", dgv[cg])
                            if cg >= 2:
                                cx.wait(nc.scalar, "st", dmmv[cg - 2])
                            for q8 in range(8):
                                a = nc.scalar.activation(
                                    out=dnsg[:, sl * 4096 + q8 * 512: sl * 4096 + q8 * 512 + 512],
                                    in_=dnw[:, sl * 4096 + q8 * 512: sl * 4096 + q8 * 512 + 512], func=AF.Sign)
                            av = cx.inc(a, "sa")
                            if cg + 1 < 16:
                                cx.wait(nc.gpsimd, "sa", av)
                                dn_load(cg + 1)
                            cx.wait(nc.tensor, "sa", av)
                            ccc = cg
                            pp = phs if cg % 2 == 0 else phs2
                            for half in range(2):
                                for jb in range(32):
                                    mm = nc.tensor.matmul(
                                        pp[half][:, 0:512],
                                        dnsg[:, sl * 4096 + jb * 128: sl * 4096 + jb * 128 + 128],
                                        hid[:, jb * 1024 + half * 512: jb * 1024 + half * 512 + 512],
                                        start=(jb == 0), stop=(jb == 31))
                            tv = cx.inc(mm, "st")
                            dmmv[cg] = tv
                            cx.wait(nc.vector, "st", tv)
                            for half in range(2):
                                dst = ffnacc[:, ccc * 1024 + half * 512: ccc * 1024 + half * 512 + 512]
                                if jh == 0:
                                    v = nc.vector.tensor_copy(out=dst, in_=pp[half][:, 0:512])
                                else:
                                    nc.vector.tensor_tensor(out=K.f32a[:], in0=pp[half][:, 0:512],
                                                            in1=dst, op=ALU.add)
                                    v = nc.vector.tensor_scalar(out=dst, in0=K.f32a[:],
                                                                scalar1=ds2[:, ccc:ccc + 1],
                                                                scalar2=None, op0=ALU.mult)
                            vv = cx.inc(v, "sv")
                    cx.barrier(BAR)
                    emit_combine(lambda ccc, tt: ffnacc[:, ccc * 1024 + tt * 512: ccc * 1024 + tt * 512 + 512],
                                 None, x_d, x_d, lnmy_d, xt512, xb512, xsq512, psm, psq, pbc,
                                 512, 2, base_t=th * 1024)

        q_d = nc.dram_tensor("q_d", [C, MYT], BF16).ap()
        with ExitStack() as sG:
            cx.barrier(BAR)
            with ExitStack() as sg1:
                wkv = sg1.enter_context(nc.sbuf_tensor("wkvG", [128, 16 * 2048], BF16))
                lnt = sg1.enter_context(nc.sbuf_tensor("lntG", [128, 16 * 512], BF16))
                kvt = sg1.enter_context(nc.sbuf_tensor("kvtG", [128, 4096], BF16))
                with nc.psum_tensor("pkG", [128, 512], F32) as pk:
                    for sel in range(2):
                        cx.wait(nc.gpsimd, "st")
                        d = nc.gpsimd.dma_start(out=sb3(wkv[:], 16, 2048),
                                                in_=wqkv[:, (1 + sel) * C:(2 + sel) * C]
                                                .rearrange("(cc p) o -> p cc o", p=128))
                        gv = cx.inc(d, "sg", 16)
                        cx.wait(nc.tensor, "sg", gv)
                        for tt in range(4):
                            cx.wait(nc.sync, "st")
                            d = nc.sync.dma_start(out=sb3(lnt[:], 16, 512),
                                                  in_=feat_tile(lnmy_d, 0, C, tt * 512, 512))
                            wv = cx.inc(d, "sw", 16)
                            cx.wait(nc.tensor, "sw", wv)
                            for tc in range(4):
                                for jg in range(4):
                                    for ccc in range(16):
                                        mm = nc.tensor.matmul(
                                            pk[:, 0:512],
                                            lnt[:, ccc * 512 + tc * 128: ccc * 512 + tc * 128 + 128],
                                            wkv[:, ccc * 2048 + jg * 512: ccc * 2048 + jg * 512 + 512],
                                            start=(ccc == 0), stop=(ccc == 15))
                                    tv = cx.inc(mm, "st")
                                    cx.wait(nc.scalar, "st", tv)
                                    a = nc.scalar.activation(out=kvt[:, jg * 512:(jg + 1) * 512],
                                                             in_=pk[:, 0:512], func=AF.Copy)
                                    av = cx.inc(a, "sa")
                                    cx.wait(nc.tensor, "sa", av)
                                cx.wait(nc.sync, "sa")
                                r0 = tt * 512 + tc * 128
                                d = nc.sync.dma_start(out=ag1_in.ap()[r0:r0 + 128, sel * C:(sel + 1) * C],
                                                      in_=kvt[:, 0:2048])
                                sv_ = cx.inc(d, "ss", 16)
                                cx.wait(nc.scalar, "ss", sv_)
                cx.wait(nc.gpsimd, "ss")
                cc_ = nc.gpsimd.collective_compute(
                    "AllGather", ALU.bypass, replica_groups=RG_PAIRS,
                    ins=[ag1_in.ap().bitcast(F32).opt()],
                    outs=[ag1_out.ap().bitcast(F32).opt()])
                cx.inc(cc_, "sc")
                cx.wait(nc.gpsimd, "st")
                d = nc.gpsimd.dma_start(out=sb3(wkv[:], 16, 2048),
                                        in_=wqkv[:, 0:C].rearrange("(cc p) o -> p cc o", p=128))
                gv = cx.inc(d, "sg", 16)
                cx.wait(nc.tensor, "sg", gv)
                with nc.psum_tensor("pqG", [128, 512], F32) as pq:
                    for tt in range(4):
                        cx.wait(nc.sync, "st")
                        d = nc.sync.dma_start(out=sb3(lnt[:], 16, 512),
                                              in_=feat_tile(lnmy_d, 0, C, tt * 512, 512))
                        wv = cx.inc(d, "sw", 16)
                        cx.wait(nc.tensor, "sw", wv)
                        for h in range(NH):
                            for ccc in range(16):
                                mm = nc.tensor.matmul(
                                    pq[:, 0:512],
                                    wkv[:, ccc * 2048 + h * 128: ccc * 2048 + h * 128 + 128],
                                    lnt[:, ccc * 512:(ccc + 1) * 512],
                                    start=(ccc == 0), stop=(ccc == 15))
                            tv = cx.inc(mm, "st")
                            cx.wait(nc.scalar, "st", tv)
                            a = nc.scalar.activation(out=kvt[:, 0:512], in_=pq[:, 0:512],
                                                     func=AF.Copy, scale=ISQ)
                            av = cx.inc(a, "sa")
                            cx.wait(nc.sync, "sa", av)
                            d = nc.sync.dma_start(
                                out=q_d[h * 128:(h + 1) * 128, tt * 512:(tt + 1) * 512],
                                in_=kvt[:, 0:512])
                            sv_ = cx.inc(d, "ss", 16)
                            cx.wait(nc.scalar, "ss", sv_)
                            cx.wait(nc.tensor, "ss", sv_)
            with ExitStack() as sg2:
                cx.barrier(BAR)
                kall = sg2.enter_context(nc.sbuf_tensor("kallG", [128, 8 * T], BF16))
                qsb = sg2.enter_context(nc.sbuf_tensor("qsbG", [128, 8 * 256], BF16))
                vtb = sg2.enter_context(nc.sbuf_tensor("vtbG", [128, 2 * 2048], BF16))
                exb = sg2.enter_context(nc.sbuf_tensor("exbG", [128, 2 * 4 * 256], BF16))
                acc = sg2.enter_context(nc.sbuf_tensor("accG", [128, 4 * 256], F32))
                ysb = sg2.enter_context(nc.sbuf_tensor("ysbG", [128, 8 * 256], BF16))
                sums1 = sg2.enter_context(nc.sbuf_tensor("sums1G", [1, 256], F32))
                cx.wait(nc.sync, "sc")
                with (nc.psum_tensor("py0G", [128, 512], F32) as py0,
                      nc.psum_tensor("py1G", [128, 512], F32) as py1,
                      nc.psum_tensor("py2G", [128, 512], F32) as py2,
                      nc.psum_tensor("py3G", [128, 512], F32) as py3,
                      nc.psum_tensor("ps0G", [128, 512], F32) as ps0,
                      nc.psum_tensor("ps1G", [128, 512], F32) as ps1,
                      nc.psum_tensor("ps2G", [128, 512], F32) as ps2,
                      nc.psum_tensor("ps3G", [128, 512], F32) as ps3):
                    pys = [py0, py1, py2, py3]
                    pss = [ps0, ps1, ps2, ps3]
                    for hhalf in range(2):
                        hbase = hhalf * 8
                        cx.barrier(BAR)
                        for h_ in range(8):
                            d = nc.sync.dma_start_transpose(
                                kall[:, h_ * T:(h_ + 1) * T],
                                ag1_out.ap()[:, (hbase + h_) * 128:(hbase + h_ + 1) * 128])
                            kv_ = cx.inc(d, "sw", 16)
                        cx.wait(nc.tensor, "sw", kv_)
                        for qt in range(8):
                            cx.wait(nc.sync, "st")
                            d = nc.sync.dma_start(out=sb3(qsb[:], 8, 256),
                                                  in_=feat_tile(q_d, hbase * 128, 8 * 128, qt * 256, 256))
                            wv = cx.inc(d, "sw", 16)
                            cx.wait(nc.tensor, "sw", wv)
                            kbs = [(0, u) for u in range(2 * qt + 2)] + [(1, u) for u in range(2 * qt + 2)]
                            nkb = len(kbs)
                            for hg in range(2):
                                evs = {}
                                wvs = {}
                                slots_ = {}

                                def emit_ylat1(kb):
                                    cx.wait(nc.tensor, "sv", evs[kb])
                                    cx.wait(nc.tensor, "sw", wvs[kb])
                                    es = kb % 2
                                    vs = kb % 2
                                    for hh in range(4):
                                        h = hbase + hg * 4 + hh
                                        mmy = nc.tensor.matmul(
                                            pys[hh][:, 0:256],
                                            vtb[:, vs * 2048 + h * 128: vs * 2048 + h * 128 + 128],
                                            exb[:, es * 1024 + hh * 256: es * 1024 + hh * 256 + 256],
                                            start=(kb == 0), stop=(kb == nkb - 1))
                                    cx.inc(mmy, "st")

                                for kb, (half, u) in enumerate(kbs):
                                    slot = 16 * half + u
                                    slots_[kb] = slot
                                    vslot = kb % 2
                                    eslot = kb % 2
                                    cx.wait(nc.sync, "st")
                                    d = nc.sync.dma_start(out=vtb[:, vslot * 2048:(vslot + 1) * 2048],
                                                          in_=ag1_out.ap()[slot * 128:(slot + 1) * 128, C:2 * C])
                                    wvs[kb] = cx.inc(d, "sw", 16)
                                    if kb >= 1:
                                        emit_ylat1(kb - 1)
                                    tvs = []
                                    for hh in range(4):
                                        h_ = hg * 4 + hh
                                        mm = nc.tensor.matmul(
                                            pss[hh][:, 0:256],
                                            kall[:, h_ * T + slot * 128: h_ * T + slot * 128 + 128],
                                            qsb[:, h_ * 256:(h_ + 1) * 256],
                                            start=True, stop=True)
                                        tvs.append(cx.inc(mm, "st"))
                                    ev = None
                                    for hh in range(4):
                                        cx.wait(nc.scalar, "st", tvs[hh])
                                        ex = exb[:, eslot * 1024 + hh * 256: eslot * 1024 + hh * 256 + 256]
                                        a = nc.scalar.activation(out=ex, in_=pss[hh][:, 0:256], func=AF.Exp)
                                        av = cx.inc(a, "sa")
                                        cx.wait(nc.vector, "sa", av)
                                        off = (2 * half + (u - 2 * qt)) if u >= 2 * qt else -1
                                        if off >= 0:
                                            v = nc.vector.tensor_tensor(
                                                out=ex, in0=ex,
                                                in1=m1_sb[:, off * 256:(off + 1) * 256], op=ALU.mult)
                                            cx.inc(v, "sv")
                                        if kb == 0:
                                            v = nc.vector.tensor_copy(out=acc[:, hh * 256:(hh + 1) * 256], in_=ex)
                                        else:
                                            v = nc.vector.tensor_tensor(out=acc[:, hh * 256:(hh + 1) * 256],
                                                                        in0=acc[:, hh * 256:(hh + 1) * 256],
                                                                        in1=ex, op=ALU.add)
                                        ev = cx.inc(v, "sv")
                                    evs[kb] = ev
                                emit_ylat1(nkb - 1)
                                for hh in range(4):
                                    h_ = hg * 4 + hh
                                    mm = nc.tensor.matmul(ps0[0:1, 0:256], K.ones_col_f[:, 0:1],
                                                          acc[:, hh * 256:(hh + 1) * 256],
                                                          start=True, stop=True)
                                    tv = cx.inc(mm, "st")
                                    cx.wait(nc.scalar, "st", tv)
                                    a = nc.scalar.activation(out=sums1[0:1, 0:256], in_=ps0[0:1, 0:256], func=AF.Ln)
                                    cx.wait(nc.scalar, "sa", cx.inc(a, "sa"))
                                    a = nc.scalar.activation(out=sums1[0:1, 0:256], in_=sums1[0:1, 0:256],
                                                             func=AF.Exp, scale=-1.0)
                                    rv = cx.inc(a, "sa")
                                    cx.wait(nc.tensor, "sa", rv)
                                    mm = nc.tensor.matmul(ps0[:, 0:256], K.ones_row[0:1, :],
                                                          sums1[0:1, 0:256], start=True, stop=True)
                                    tv = cx.inc(mm, "st")
                                    cx.wait(nc.vector, "st", tv)
                                    v = nc.vector.tensor_copy(out=K.rb[:, 0:256], in_=ps0[:, 0:256])
                                    rv = cx.inc(v, "sv")
                                    cx.wait(nc.vector, "sv", rv)
                                    v = nc.vector.tensor_tensor(out=ysb[:, h_ * 256:(h_ + 1) * 256],
                                                                in0=pys[hh][:, 0:256],
                                                                in1=K.rb[:, 0:256], op=ALU.mult)
                                    yv = cx.inc(v, "sv")
                                    cx.wait(nc.tensor, "sv", yv)
                            cx.wait(nc.sync, "sv")
                            d = nc.sync.dma_start(
                                out=y_d[hbase * 128:(hbase + 8) * 128, qt * 256:(qt + 1) * 256]
                                    .rearrange("(h p) t -> p h t", p=128),
                                in_=sb3(ysb[:], 8, 256))
                            sv_ = cx.inc(d, "ss", 16)
                            cx.wait(nc.vector, "ss", sv_)
            with ExitStack() as sg3:
                cx.barrier(BAR)
                wo = sg3.enter_context(nc.sbuf_tensor("woG", [128, 16 * 2048], BF16))
                ytl = sg3.enter_context(nc.sbuf_tensor("ytlG", [128, 16 * 256], BF16))
                att = sg3.enter_context(nc.sbuf_tensor("attG", [128, 16 * 256], BF16))
                xsqg = sg3.enter_context(nc.sbuf_tensor("xsqG", [128, 16 * 256], BF16))
                xtg = sg3.enter_context(nc.sbuf_tensor("xtG", [128, 16 * 256], F32))
                xbg = sg3.enter_context(nc.sbuf_tensor("xbG", [128, 16 * 256], BF16))
                d = nc.gpsimd.dma_start(out=sb3(wo[:], 16, 2048),
                                        in_=wout.rearrange("(cc p) o -> p cc o", p=128))
                cx.inc(d, "sg", 16)
                cx.wait(nc.tensor, "sg")
                with (nc.psum_tensor("poG", [128, 512], F32) as po,
                      nc.psum_tensor("poGb", [128, 512], F32) as pob,
                      nc.psum_tensor("psmG", [128, 512], F32) as psm,
                      nc.psum_tensor("psqG", [128, 512], F32) as psq,
                      nc.psum_tensor("pbcG", [128, 512], F32) as pbc):
                    osavs = {}
                    for qt in range(8):
                        cx.wait(nc.sync, "st")
                        cx.wait(nc.sync, "sa")
                        cx.wait(nc.sync, "sv")
                        d = nc.sync.dma_start(out=sb3(ytl[:], 16, 256),
                                              in_=feat_tile(y_d, 0, C, qt * 256, 256))
                        wv = cx.inc(d, "sw", 16)
                        cx.wait(nc.tensor, "sw", wv)
                        for oc in range(16):
                            g = qt * 16 + oc
                            if g >= 2:
                                cx.wait(nc.tensor, "sa", osavs[g - 2])
                            for ccc in range(16):
                                mm = nc.tensor.matmul(
                                    [po, pob][g % 2][:, 0:256],
                                    wo[:, ccc * 2048 + oc * 128: ccc * 2048 + oc * 128 + 128],
                                    ytl[:, ccc * 256:(ccc + 1) * 256],
                                    start=(ccc == 0), stop=(ccc == 15))
                            tv = cx.inc(mm, "st")
                            cx.wait(nc.scalar, "st", tv)
                            a = nc.scalar.activation(out=att[:, oc * 256:(oc + 1) * 256],
                                                     in_=[po, pob][g % 2][:, 0:256], func=AF.Copy)
                            osavs[g] = cx.inc(a, "sa")
                        emit_combine(lambda ccc, _t: att[:, ccc * 256:(ccc + 1) * 256],
                                     None, x_d, x_d, lnmy_d, xtg, xbg, xsqg, psm, psq, pbc,
                                     256, 1, base_t=qt * 256)

        with ExitStack() as sH:
            cx.barrier(BAR)
            b1s = sH.enter_context(nc.sbuf_tensor("b1sH", [128, 32], F32))
            b2s = sH.enter_context(nc.sbuf_tensor("b2sH", [128, 32], F32))
            b3s = sH.enter_context(nc.sbuf_tensor("b3sH", [128, 16], F32))
            for ap_, t_ in ((b1, b1s), (b2, b2s), (b3, b3s)):
                d = nc.sync.dma_start(out=t_[:], in_=ap_)
                cx.inc(d, "sw", 16)
            h1 = sH.enter_context(nc.sbuf_tensor("h1H", [128, 32 * 1024], BF16))
            h2 = sH.enter_context(nc.sbuf_tensor("h2H", [128, 32 * 1024], BF16))
            scratch = sH.enter_context(nc.sbuf_tensor("scrH", [128, 16384], BF16))
            xtf = scratch[:, 0:8192].bitcast(F32)
            xbf = scratch[:, 8192:12288]
            xsqf = scratch[:, 12288:16384]
            with (nc.psum_tensor("pH0", [128, 512], F32) as pf0,
                  nc.psum_tensor("pH1", [128, 512], F32) as pf1,
                  nc.psum_tensor("pH2", [128, 512], F32) as pf2,
                  nc.psum_tensor("pH3", [128, 512], F32) as pf3,
                  nc.psum_tensor("psmH", [128, 512], F32) as psm,
                  nc.psum_tensor("psqH", [128, 512], F32) as psq,
                  nc.psum_tensor("pbcH", [128, 512], F32) as pbc):
                pfs = [pf0, pf1]
                pfs2 = [pf2, pf3]
                for th in range(2):
                    with ExitStack() as shA:
                        lnx7 = h2[:, 0:16 * 1024]
                        wbuf = shA.enter_context(nc.sbuf_tensor(f"wbufH{th}", [128, 16 * 512], BF16))
                        cx.barrier(BAR)
                        d = nc.sync.dma_start(out=sb3(lnx7, 16, 1024),
                                              in_=feat_tile(lnmy_d, 0, C, th * 1024, 1024))
                        wv = cx.inc(d, "sw", 16)
                        cx.wait(nc.tensor, "sw", wv)
                        wgv = {}
                        wmv = {}

                        def w1_load(jg):
                            sl = jg % 2
                            d = nc.gpsimd.dma_start(out=sb3(wbuf[:, sl * 4096:(sl + 1) * 4096], 16, 256),
                                                    in_=w1t[:, jg * 256:(jg + 1) * 256]
                                                    .rearrange("(cc p) j -> p cc j", p=128))
                            wgv[jg] = cx.inc(d, "sg", 16)

                        w1_load(0)
                        f1sav = {}
                        for jg in range(16):
                            sl = jg % 2
                            if jg + 1 < 16:
                                if jg >= 1:
                                    cx.wait(nc.gpsimd, "st", wmv[jg - 1])
                                w1_load(jg + 1)
                            cx.wait(nc.tensor, "sg", wgv[jg])
                            for jb in range(2):
                                jc = jg * 2 + jb
                                pp = pfs if jc % 2 == 0 else pfs2
                                if jc >= 2:
                                    cx.wait(nc.tensor, "sa", f1sav[jc - 2])
                                for half in range(2):
                                    for ccc in range(16):
                                        mm = nc.tensor.matmul(
                                            pp[half][:, 0:512],
                                            wbuf[:, sl * 4096 + ccc * 256 + jb * 128: sl * 4096 + ccc * 256 + jb * 128 + 128],
                                            lnx7[:, ccc * 1024 + half * 512: ccc * 1024 + half * 512 + 512],
                                            start=(ccc == 0), stop=(ccc == 15))
                                tv = cx.inc(mm, "st")
                                wmv[jg] = tv
                                cx.wait(nc.scalar, "st", tv)
                                for half in range(2):
                                    a = nc.scalar.activation(
                                        out=h1[:, jc * 1024 + half * 512: jc * 1024 + half * 512 + 512],
                                        in_=pp[half][:, 0:512], func=AF.Relu, bias=b1s[:, jc:jc + 1])
                                f1sav[jc] = cx.inc(a, "sa")
                    with ExitStack() as shB:
                        w2buf = shB.enter_context(nc.sbuf_tensor(f"w2bufH{th}", [128, 32 * 256], BF16))
                        cx.barrier(BAR)
                        wgv = {}
                        wmv = {}

                        def w2_load(jg):
                            sl = jg % 2
                            d = nc.gpsimd.dma_start(out=sb3(w2buf[:, sl * 4096:(sl + 1) * 4096], 32, 128),
                                                    in_=w2t[:, jg * 128:(jg + 1) * 128]
                                                    .rearrange("(cc p) j -> p cc j", p=128))
                            wgv[jg] = cx.inc(d, "sg", 16)

                        w2_load(0)
                        f2sav = {}
                        for jg in range(32):
                            sl = jg % 2
                            if jg + 1 < 32:
                                if jg >= 1:
                                    cx.wait(nc.gpsimd, "st", wmv[jg - 1])
                                w2_load(jg + 1)
                            cx.wait(nc.tensor, "sg", wgv[jg])
                            for jb in range(1):
                                jc = jg
                                pp = pfs if jc % 2 == 0 else pfs2
                                if jc >= 2:
                                    cx.wait(nc.tensor, "sa", f2sav[jc - 2])
                                for half in range(2):
                                    for ccc in range(32):
                                        mm = nc.tensor.matmul(
                                            pp[half][:, 0:512],
                                            w2buf[:, sl * 4096 + ccc * 128: sl * 4096 + ccc * 128 + 128],
                                            h1[:, ccc * 1024 + half * 512: ccc * 1024 + half * 512 + 512],
                                            start=(ccc == 0), stop=(ccc == 31))
                                tv = cx.inc(mm, "st")
                                wmv[jg] = tv
                                cx.wait(nc.scalar, "st", tv)
                                for half in range(2):
                                    a = nc.scalar.activation(
                                        out=h2[:, jc * 1024 + half * 512: jc * 1024 + half * 512 + 512],
                                        in_=pp[half][:, 0:512], func=AF.Relu, bias=b2s[:, jc:jc + 1])
                                f2sav[jc] = cx.inc(a, "sa")
                    with ExitStack() as shC:
                        w3buf = shC.enter_context(nc.sbuf_tensor(f"w3bufH{th}", [128, 32 * 256], BF16))
                        ffno = h1[:, 0:16 * 1024]
                        cx.barrier(BAR)
                        wgv = {}
                        wmv = {}

                        def w3_load(cg):
                            sl = cg % 2
                            d = nc.gpsimd.dma_start(out=sb3(w3buf[:, sl * 4096:(sl + 1) * 4096], 32, 128),
                                                    in_=w3t[:, cg * 128:(cg + 1) * 128]
                                                    .rearrange("(cc p) j -> p cc j", p=128))
                            wgv[cg] = cx.inc(d, "sg", 16)

                        w3_load(0)
                        f3sav = {}
                        for cg in range(16):
                            sl = cg % 2
                            if cg + 1 < 16:
                                if cg >= 1:
                                    cx.wait(nc.gpsimd, "st", wmv[cg - 1])
                                w3_load(cg + 1)
                            cx.wait(nc.tensor, "sg", wgv[cg])
                            for cb in range(1):
                                ccx = cg
                                pp = pfs if ccx % 2 == 0 else pfs2
                                if ccx >= 2:
                                    cx.wait(nc.tensor, "sa", f3sav[ccx - 2])
                                for half in range(2):
                                    for ccc in range(32):
                                        mm = nc.tensor.matmul(
                                            pp[half][:, 0:512],
                                            w3buf[:, sl * 4096 + ccc * 128: sl * 4096 + ccc * 128 + 128],
                                            h2[:, ccc * 1024 + half * 512: ccc * 1024 + half * 512 + 512],
                                            start=(ccc == 0), stop=(ccc == 31))
                                tv = cx.inc(mm, "st")
                                wmv[cg] = tv
                                cx.wait(nc.scalar, "st", tv)
                                for half in range(2):
                                    a = nc.scalar.activation(
                                        out=ffno[:, ccx * 1024 + half * 512: ccx * 1024 + half * 512 + 512],
                                        in_=pp[half][:, 0:512], func=AF.Identity, bias=b3s[:, ccx:ccx + 1])
                                f3sav[ccx] = cx.inc(a, "sa")
                            cx.barrier(BAR)
                        cx.barrier(BAR)
                        xt5 = h2[:, 0:16384].bitcast(F32)
                        xb5 = h2[:, 16384:24576]
                        xsq5 = h2[:, 24576:32768]
                        emit_combine(lambda ccc, tt: ffno[:, ccc * 1024 + tt * 512: ccc * 1024 + tt * 512 + 512],
                                     None, x_d, xout, None, xt5, xb5, xsq5, psm, psq, pbc,
                                     512, 2, base_t=th * 1024)
        cx.barrier(BAR)
        cx.wait(nc.sync, "ss")
    return nc



STORE_PERM = np.concatenate([np.arange(0, NBLK, 2), np.arange(1, NBLK, 2)])


def tok_perm():
    idx = np.arange(T).reshape(NBLK, 128)
    return idx[STORE_PERM].reshape(-1)


MASK_CLASS_ORDER = (0, 2, 1, 3)


def make_masks_l0(parity):
    out = np.zeros((4, 128, 512), np.float32)
    for o in range(4):
        m = np.zeros((128, 256), np.float32)
        for half, goff in ((0, 0), (1, 2)):
            gc = goff + parity
            for col in range(128):
                cc = half * 128 + col
                if o < gc:
                    m[:, cc] = 1.0
                elif o == gc:
                    m[: col + 1, cc] = 1.0
        out[o, :, 0:256] = m
        out[o, :, 256:512] = m
    return np.ascontiguousarray(out[list(MASK_CLASS_ORDER)].reshape(4 * 128, 512))


def make_masks_l1(parity):
    out = np.zeros((4, 128, 256), np.float32)
    for o in range(4):
        for half, goff in ((0, 0), (1, 2)):
            gc = goff + parity
            for col in range(128):
                cc = half * 128 + col
                if o < gc:
                    out[o, :, cc] = 1.0
                elif o == gc:
                    out[o, : col + 1, cc] = 1.0
    return np.ascontiguousarray(out[list(MASK_CLASS_ORDER)].reshape(4 * 128, 256))


def _f(x):
    return np.ascontiguousarray(np.asarray(x, dtype=np.float32))


_PROGRAMS = {}


def _programs():
    if not _PROGRAMS:
        _PROGRAMS["fused"] = build_fused()
    return _PROGRAMS


def kernel(**inputs):
    inp = {k: np.ascontiguousarray(np.asarray(v, dtype=np.float32)) for k, v in inputs.items()}
    perm = tok_perm()
    nc = _programs()["fused"]
    masks0 = [make_masks_l0(p) for p in (0, 1)]
    masks1 = [make_masks_l1(p) for p in (0, 1)]
    ident = np.eye(128, dtype=np.float32)
    x = inp["L0_attn_x"].reshape(B, T, C)
    in_maps = []
    for core in range(8):
        b, p = core // 2, core % 2
        xf = np.ascontiguousarray(x[b].T[:, perm])
        in_maps.append({
            "xmy": np.ascontiguousarray(xf[:, p * MYT:(p + 1) * MYT]),
            "wq_t": np.ascontiguousarray(inp["L0_attn_Wq_w"].T),
            "wq_n": inp["L0_attn_Wq_w"],
            "wdkv_t": np.ascontiguousarray(inp["L0_attn_Wdkv_w"].T),
            "wuk": inp["L0_attn_Wuk_w"], "wuv": inp["L0_attn_Wuv_w"],
            "wo_t": np.ascontiguousarray(inp["L0_attn_Wo_w"].T),
            "masks0": masks0[p], "ident": ident,
            "up_t": np.ascontiguousarray(inp["L0_ffn_up_weight"].T),
            "down_b": np.ascontiguousarray(
                inp["L0_ffn_down_weight"].reshape(16, 128, 64, 128).transpose(0, 2, 3, 1)),
            "usc": np.ascontiguousarray(inp["L0_ffn_up_scale"].reshape(64, 128).T),
            "dsc": np.ascontiguousarray(inp["L0_ffn_down_scale"].reshape(16, 128).T),
            "osc": inp["L0_ffn_output_scale"].reshape(1),
            "wqkv": inp["L1_attn_w_qkv"], "wout": inp["L1_attn_w_out"],
            "masks1": masks1[p],
            "w1t": np.ascontiguousarray(inp["L1_ffn_W1"].T),
            "w2t": np.ascontiguousarray(inp["L1_ffn_W2"].T),
            "w3t": np.ascontiguousarray(inp["L1_ffn_W3"].T),
            "b1": np.ascontiguousarray(inp["L1_ffn_b1"].reshape(32, 128).T),
            "b2": np.ascontiguousarray(inp["L1_ffn_b2"].reshape(32, 128).T),
            "b3": np.ascontiguousarray(inp["L1_ffn_b3"].reshape(16, 128).T),
        })
    res = run_bass_kernel_spmd(nc, in_maps, core_ids=list(range(8)))
    out = np.zeros((B, T, C), np.float32)
    for core in range(8):
        b, p = core // 2, core % 2
        gtok = perm[p * MYT:(p + 1) * MYT]
        out[b, gtok, :] = res.results[core]["xout"].T
    return out


def _kernel_unfused(**inputs):
    inp = {k: np.ascontiguousarray(np.asarray(v, dtype=np.float32)) for k, v in inputs.items()}
    perm = tok_perm()
    progs = {"a0": build_la0(), "f0": build_lf0(), "a1": build_la1(), "f1": build_lf1()}
    masks0 = [make_masks_l0(p) for p in (0, 1)]
    masks1 = [make_masks_l1(p) for p in (0, 1)]
    ident = np.eye(128, dtype=np.float32)

    def shards(xbtc):
        out = []
        for core in range(8):
            b, p = core // 2, core % 2
            xf = np.ascontiguousarray(xbtc[b].T[:, perm])
            out.append((xf, np.ascontiguousarray(xf[:, p * MYT:(p + 1) * MYT])))
        return out

    def gather(res):
        x = np.zeros((B, T, C), np.float32)
        for core in range(8):
            b, p = core // 2, core % 2
            gtok = perm[p * MYT:(p + 1) * MYT]
            x[b, gtok, :] = res.results[core]["xout"].T
        return x

    sh = shards(inp["L0_attn_x"])
    wq_t = np.ascontiguousarray(inp["L0_attn_Wq_w"].T)
    in_maps = [{"xfull": xf, "xmy": xm, "wq_t": wq_t,
                "wq_n": inp["L0_attn_Wq_w"],
                "wdkv_t": np.ascontiguousarray(inp["L0_attn_Wdkv_w"].T),
                "wuk": inp["L0_attn_Wuk_w"], "wuv": inp["L0_attn_Wuv_w"],
                "wo_t": np.ascontiguousarray(inp["L0_attn_Wo_w"].T),
                "masks": masks0[c % 2], "ident": ident}
               for c, (xf, xm) in enumerate(sh)]
    x1 = gather(run_bass_kernel_spmd(progs["a0"], in_maps, core_ids=list(range(8))))

    sh = shards(x1)
    in_maps = [{"xmy": xm,
                "up_t": np.ascontiguousarray(inp["L0_ffn_up_weight"].T),
                "down_t": np.ascontiguousarray(inp["L0_ffn_down_weight"].T),
                "usc": np.ascontiguousarray(inp["L0_ffn_up_scale"].reshape(64, 128).T),
                "dsc": np.ascontiguousarray(inp["L0_ffn_down_scale"].reshape(16, 128).T),
                "osc": inp["L0_ffn_output_scale"].reshape(1)}
               for c, (xf, xm) in enumerate(sh)]
    x2 = gather(run_bass_kernel_spmd(progs["f0"], in_maps, core_ids=list(range(8))))

    sh = shards(x2)
    in_maps = [{"xfull": xf, "xmy": xm,
                "wqkv": inp["L1_attn_w_qkv"], "wout": inp["L1_attn_w_out"],
                "masks": masks1[c % 2]}
               for c, (xf, xm) in enumerate(sh)]
    x3 = gather(run_bass_kernel_spmd(progs["a1"], in_maps, core_ids=list(range(8))))

    sh = shards(x3)
    in_maps = [{"xmy": xm,
                "w1t": np.ascontiguousarray(inp["L1_ffn_W1"].T),
                "w2t": np.ascontiguousarray(inp["L1_ffn_W2"].T),
                "w3t": np.ascontiguousarray(inp["L1_ffn_W3"].T),
                "b1": np.ascontiguousarray(inp["L1_ffn_b1"].reshape(32, 128).T),
                "b2": np.ascontiguousarray(inp["L1_ffn_b2"].reshape(32, 128).T),
                "b3": np.ascontiguousarray(inp["L1_ffn_b3"].reshape(16, 128).T)}
               for c, (xf, xm) in enumerate(sh)]
    x4 = gather(run_bass_kernel_spmd(progs["f1"], in_maps, core_ids=list(range(8))))
    return x4
